# version 16
# baseline (speedup 1.0000x reference)
import numpy as np
import ml_dtypes
import concourse.bass as bass
import concourse.mybir as mybir
from concourse.alu_op_type import AluOpType as OP
from concourse.bass_utils import run_bass_kernel_spmd
from contextlib import ExitStack

F32 = mybir.dt.float32
BF16 = mybir.dt.bfloat16
I32 = mybir.dt.int32
AF = mybir.ActivationFunctionType
AX = mybir.AxisListType

B, L, DM, EI, NS, RK = 8, 4096, 1024, 2048, 16, 64
NLAY = 2
TC = 512
NCH = L // TC
TB = 256
NBC = L // TB
NSL = 16
EPS = 1e-5
BETA = 0.6
N_CORES = 8
SLAB_H = 8


class Sem:
    def __init__(self, h):
        self.h = h
        self.n = 0


class Prog:
    ENGS = ("sync", "vector", "scalar", "gpsimd", "tensor")

    def __init__(self):
        self.ops = {e: [] for e in self.ENGS}

    def add(self, eng, fn, waits=(), inc=None, amt=1):
        ws = [(s.h, int(v)) for (s, v) in waits if v and v > 0]
        post = None
        if inc is not None:
            inc.n += amt
            post = inc.n
        self.ops[eng].append((fn, ws, inc.h if inc is not None else None, amt))
        return post

    def emit(self, block):
        for eng in self.ENGS:
            ops = self.ops[eng]
            if not ops:
                continue

            def make(ops_):
                def run(e):
                    for fn, ws, inch, amt in ops_:
                        for (h, v) in ws:
                            e.wait_ge(h, v)
                        inst = fn(e) if fn is not None else None
                        if inch is not None:
                            if inst is None:
                                inst = e.nop()
                            inst.then_inc(inch, amt)
                return run
            getattr(block, eng)(make(ops))


def build_nc():
    nc = bass.Bass("TRN2", target_bir_lowering=False,
                   detect_race_conditions=False)

    d_x = nc.dram_tensor("x", [L, DM], F32, kind="ExternalInput").ap()
    d_win = nc.dram_tensor("w_in", [NLAY, 32, 128, 1024], F32, kind="ExternalInput").ap()
    d_wout = nc.dram_tensor("w_out", [NLAY, 16, 128, 1024], F32, kind="ExternalInput").ap()
    d_wdt = nc.dram_tensor("w_dt", [NLAY, 64, 2048], F32, kind="ExternalInput").ap()
    d_wxp = nc.dram_tensor("w_xp", [NLAY, 16, 128, 96], F32, kind="ExternalInput").ap()
    d_alog = nc.dram_tensor("alog", [NLAY, 128, 256], F32, kind="ExternalInput").ap()
    d_cw = nc.dram_tensor("cw", [NLAY, 128, 64], F32, kind="ExternalInput").ap()
    d_cb = nc.dram_tensor("cb", [NLAY, 128, 16], F32, kind="ExternalInput").ap()
    d_dtb = nc.dram_tensor("dtb", [NLAY, 128, 16], F32, kind="ExternalInput").ap()
    d_dp = nc.dram_tensor("dp", [NLAY, 128, 16], F32, kind="ExternalInput").ap()
    d_nw = nc.dram_tensor("nw", [NLAY, 128, 8], F32, kind="ExternalInput").ap()
    d_w1 = nc.dram_tensor("w1", [32, 128, 1024], F32, kind="ExternalInput").ap()
    d_b1 = nc.dram_tensor("b1", [128, 32], F32, kind="ExternalInput").ap()
    d_w2 = nc.dram_tensor("w2", [8, 128, 4096], F32, kind="ExternalInput").ap()
    d_b2 = nc.dram_tensor("b2", [128, 8], F32, kind="ExternalInput").ap()
    d_cst = nc.dram_tensor("cst", [128, 4], F32, kind="ExternalInput").ap()
    d_id = nc.dram_tensor("ident", [128, 128], F32, kind="ExternalInput").ap()
    d_idb = nc.dram_tensor("identb", [128, 128], BF16, kind="ExternalInput").ap()

    d_out = nc.dram_tensor("out", [L, DM], F32, kind="ExternalOutput").ap()

    d_szsp = nc.dram_tensor("sz_sp", [NLAY, NCH, NSL, 128, TC], F32, kind="Internal").ap()
    d_xbsp = nc.dram_tensor("xb_sp", [NLAY, NCH, NSL, 128, TC], F32, kind="Internal").ap()
    d_acc = nc.dram_tensor("acc_sp", [L, DM], F32, kind="Internal").ap()
    d_gel = nc.dram_tensor("gel_sp", [NBC, 32, 128, TB], F32, kind="Internal").ap()
    d_bc = nc.dram_tensor("bc_sp", [32, 512], F32, kind="Internal").ap()
    d_r2d = nc.dram_tensor("r2_sp", [1, 512], F32, kind="Internal").ap()
    d_rw1 = nc.dram_tensor("rw1_sp", [2, 128], F32, kind="Internal").ap()
    d_rw2 = nc.dram_tensor("rw2_sp", [2, 128], F32, kind="Internal").ap()
    d_rw3 = nc.dram_tensor("rw3_sp", [2, 128], F32, kind="Internal").ap()
    d_scd = nc.dram_tensor("sc_sp", [1, 4], F32, kind="Internal").ap()

    P = Prog()

    with ExitStack() as octx:
        OE = octx.enter_context
        sT = Sem(OE(nc.semaphore("sT")))
        sV = Sem(OE(nc.semaphore("sV")))
        sS = Sem(OE(nc.semaphore("sS")))
        sG = Sem(OE(nc.semaphore("sG")))
        LX = Sem(OE(nc.semaphore("LX")))
        LXP = [Sem(OE(nc.semaphore(f"LXP{i}"))) for i in range(2)]
        LWI = [Sem(OE(nc.semaphore(f"LWI{i}"))) for i in range(2)]
        LWO = [Sem(OE(nc.semaphore(f"LWO{i}"))) for i in range(2)]
        LBB = Sem(OE(nc.semaphore("LBB")))
        LCB = Sem(OE(nc.semaphore("LCB")))
        LR2 = Sem(OE(nc.semaphore("LR2")))
        LMW = Sem(OE(nc.semaphore("LMW")))
        LGL = Sem(OE(nc.semaphore("LGL")))
        LAC = [Sem(OE(nc.semaphore(f"LAC{i}"))) for i in range(2)]
        LTH = Sem(OE(nc.semaphore("LTH")))
        LS1 = Sem(OE(nc.semaphore("LS1")))
        LC2 = Sem(OE(nc.semaphore("LC2")))
        LS2 = Sem(OE(nc.semaphore("LS2")))
        SZ = [Sem(OE(nc.semaphore(f"SZ{i}"))) for i in range(2)]
        SXB = Sem(OE(nc.semaphore("SXB")))
        SBC = Sem(OE(nc.semaphore("SBC")))
        SR2 = Sem(OE(nc.semaphore("SR2")))
        SAC = Sem(OE(nc.semaphore("SAC")))
        SGP = [Sem(OE(nc.semaphore(f"SGP{i}"))) for i in range(2)]
        STH = Sem(OE(nc.semaphore("STH")))
        SS1 = Sem(OE(nc.semaphore("SS1")))
        SC2 = Sem(OE(nc.semaphore("SC2")))
        SS2 = Sem(OE(nc.semaphore("SS2")))
        SOU = [Sem(OE(nc.semaphore(f"SOU{i}"))) for i in range(2)]
        ALL_DMA_SEMS = ([LX, LBB, LCB, LR2, LMW, LGL, LTH, LS1, LC2, LS2,
                         SXB, SBC, SR2, SAC, STH, SS1, SC2, SS2]
                        + LXP + LWI + LWO + LAC + SZ + SGP + SOU)

        t_id = OE(nc.sbuf_tensor("t_id", [128, 128], F32))
        t_idb = OE(nc.sbuf_tensor("t_idb", [128, 128], BF16))
        t_ones = OE(nc.sbuf_tensor("t_ones", [128, 1], F32))
        t_cst = OE(nc.sbuf_tensor("t_cst", [128, 4], F32))

        def LD(sem, dst, src, waits=(), eng="sync"):
            return P.add(eng, (lambda d_, s_: (lambda e: e.dma_start(out=d_, in_=s_)))(dst, src),
                         waits=waits, inc=sem, amt=16)

        def ST(sem, dst, src, waits=()):
            return P.add("scalar", (lambda d_, s_: (lambda e: e.dma_start(out=d_, in_=s_)))(dst, src),
                         waits=waits, inc=sem, amt=16)

        LD(LMW, t_cst[:, :], d_cst)
        LD(LMW, t_id[:, :], d_id)
        LD(LMW, t_idb[:, :], d_idb)
        P.add("vector", lambda e: e.memset(t_ones[:, :], 1.0), inc=sV)

        with ExitStack() as actx:
            AE = actx.enter_context
            p = [AE(nc.psum_tensor(f"p{i}", [128, 512], F32)) for i in range(8)]
            t_dtw = AE(nc.sbuf_tensor("t_dtw", [64, 2048], F32))
            t_xpw = AE(nc.sbuf_tensor("t_xpw", [128, 16 * 96], F32))
            t_negA = [AE(nc.sbuf_tensor(f"t_negA{l}", [128, 256], F32)) for l in range(NLAY)]
            t_cw = [AE(nc.sbuf_tensor(f"t_cw{l}", [128, 64], F32)) for l in range(NLAY)]
            t_cb = [AE(nc.sbuf_tensor(f"t_cb{l}", [128, 16], F32)) for l in range(NLAY)]
            t_dtb = [AE(nc.sbuf_tensor(f"t_dtb{l}", [128, 16], F32)) for l in range(NLAY)]
            t_dp = [AE(nc.sbuf_tensor(f"t_dp{l}", [128, 16], F32)) for l in range(NLAY)]
            t_nw = [AE(nc.sbuf_tensor(f"t_nw{l}", [128, 8], F32)) for l in range(NLAY)]
            t_vst = [AE(nc.sbuf_tensor(f"t_vst{l}", [128, 256], F32)) for l in range(NLAY)]
            t_hst = [AE(nc.sbuf_tensor(f"t_hst{l}", [128, 256], F32)) for l in range(NLAY)]
            t_tail = [AE(nc.sbuf_tensor(f"t_tail{l}", [128, 48], F32)) for l in range(NLAY)]
            t_xt = AE(nc.sbuf_tensor("t_xt", [128, 4096], F32))
            t_base = AE(nc.sbuf_tensor("t_base", [128, 8 * 512], F32))
            t_xb = AE(nc.sbuf_tensor("t_xb", [128, 16 * 512], F32))
            t_dl = AE(nc.sbuf_tensor("t_dl", [128, 16 * 512], F32))
            t_vh = AE(nc.sbuf_tensor("t_vh", [128, 8 * 512], F32))
            t_y = AE(nc.sbuf_tensor("t_y", [128, 8 * 512], F32))
            t_tb0 = AE(nc.sbuf_tensor("t_tb0", [128, 512], F32))
            t_bpw = AE(nc.sbuf_tensor("t_bpw", [128, 512], F32))
            t_rhs = AE(nc.sbuf_tensor("t_rhs", [128, 2560], F32))
            RHS_OFF = [0, 512, 896, 1152]
            t_BT = AE(nc.sbuf_tensor("t_BT", [128, 64], F32))
            t_pr = AE(nc.sbuf_tensor("t_pr", [128, 1], F32))
            t_Cb = AE(nc.sbuf_tensor("t_Cb", [128, 1024], F32))
            t_dbc = AE(nc.sbuf_tensor("t_dbc", [128, 512], F32))
            t_mv = AE(nc.sbuf_tensor("t_mv", [128, 8], F32))
            t_bnst = AE(nc.sbuf_tensor("t_bnst", [128, 48], F32))
            t_r = AE(nc.sbuf_tensor("t_r", [128, 4], F32))
            t_r2 = AE(nc.sbuf_tensor("t_r2", [1, 512], F32))
            t_r2B = AE(nc.sbuf_tensor("t_r2B", [128, 512], F32))
            t_wi = [AE(nc.sbuf_tensor(f"t_wi{i}", [128, 1024], F32)) for i in range(2)]
            t_wo = [AE(nc.sbuf_tensor(f"t_wo{i}", [128, 1024], F32)) for i in range(2)]
            t_xc = [AE(nc.sbuf_tensor(f"t_xc{i}", [128, 512], F32)) for i in range(2)]
            t_sz = [AE(nc.sbuf_tensor(f"t_sz{i}", [128, 512], F32)) for i in range(2)]
            t_e = t_xc
            t_dA = [AE(nc.sbuf_tensor(f"t_dA{i}", [128, 512], F32)) for i in range(2)]
            t_vw = t_xt
            t_xbr = [t_wi[0][:, 0:512], t_wi[1][:, 0:512]]
            t_szr = [t_wi[0][:, 512:1024], t_wi[1][:, 512:1024]]
            t_y2 = t_sz

            for l in range(NLAY):
                LD(LMW, t_negA[l][:, :], d_alog[l, :, :])
                LD(LMW, t_cw[l][:, :], d_cw[l, :, :])
                LD(LMW, t_cb[l][:, :], d_cb[l, :, :])
                LD(LMW, t_dtb[l][:, :], d_dtb[l, :, :])
                LD(LMW, t_dp[l][:, :], d_dp[l, :, :])
                LD(LMW, t_nw[l][:, :], d_nw[l, :, :])
            ld_init = LMW.n
            for l in range(NLAY):
                P.add("scalar", (lambda l_: (lambda e: e.activation(
                    out=t_negA[l_][:, :], in_=t_negA[l_][:, :], func=AF.Exp)))(l),
                    waits=[(LMW, ld_init)], inc=sS)
                sA = sS.n
                P.add("vector", (lambda l_: (lambda e: e.tensor_scalar(
                    out=t_negA[l_][:, :], in0=t_negA[l_][:, :], scalar1=-1.0,
                    scalar2=None, op0=OP.mult)))(l), waits=[(sS, sA)], inc=sV)
                P.add("vector", (lambda l_: (lambda e: e.memset(t_vst[l_][:, :], 0.0)))(l), inc=sV)
                P.add("vector", (lambda l_: (lambda e: e.memset(t_hst[l_][:, :], 0.0)))(l), inc=sV)
                P.add("vector", (lambda l_: (lambda e: e.memset(t_tail[l_][:, :], 0.0)))(l), inc=sV)
            LNB = float(np.log(np.float64(BETA)))
            P.add("vector", lambda e: e.memset(t_bpw[:, :], 1.0), inc=sV)
            P.add("vector", lambda e: e.drain(), inc=sV)
            P.add("vector", lambda e: e.tensor_tensor_scan(
                out=t_tb0[:, :], data0=t_bpw[:, :], data1=t_bpw[:, :],
                initial=t_cst[:, 3:4], op0=OP.mult, op1=OP.add),
                waits=[(LMW, ld_init)], inc=sV)
            io_v = sV.n
            P.add("scalar", lambda e: e.activation(
                out=t_bpw[:, :], in_=t_tb0[:, :], func=AF.Exp, scale=LNB),
                waits=[(sV, io_v), (LMW, ld_init)], inc=sS)
            P.add("tensor", lambda e: e.transpose(
                p[7][:, 0:1], t_tb0[0:1, 0:128], t_id[0:1, 0:1]),
                waits=[(sV, io_v), (LMW, ld_init)], inc=sT)
            pr_t = sT.n
            P.add("scalar", lambda e: e.activation(
                out=t_pr[:, :], in_=p[7][:, 0:1], func=AF.Copy),
                waits=[(sT, pr_t)], inc=sS)
            pr_s = sS.n
            P.add("scalar", lambda e: e.drain(), inc=sS)
            P.add("vector", lambda e: e.tensor_scalar(
                out=t_tb0[:, :], in0=t_tb0[:, :], scalar1=t_pr[:, 0:1],
                scalar2=None, op0=OP.subtract), waits=[(sS, pr_s)], inc=sV)
            P.add("vector", lambda e: e.drain(), inc=sV)
            P.add("vector", lambda e: e.tensor_scalar(
                out=t_rhs[:, 0:512], in0=t_tb0[:, :], scalar1=0.0,
                scalar2=None, op0=OP.is_ge), inc=sV)
            msk_v = sV.n
            P.add("scalar", lambda e: e.activation(
                out=t_tb0[:, :], in_=t_tb0[:, :], func=AF.Exp, scale=LNB),
                waits=[(sV, msk_v)], inc=sS)
            tb_exp = sS.n
            P.add("vector", lambda e: e.tensor_tensor(
                out=t_tb0[:, :], in0=t_tb0[:, :], in1=t_rhs[:, 0:512],
                op=OP.mult), waits=[(sS, tb_exp)], inc=sV)
            P.add("vector", lambda e: e.drain(), inc=sV)
            P.add("vector", lambda e: e.memset(t_rhs[:, :], 0.0), inc=sV)

            acc_fence = sV.n
            s0tp_fence = 0
            wi_fence = [0, 0]
            wo_fence = [0, 0]
            xh_fence = 0
            xbst_full_prev = [0] * NLAY
            stz_prev = [[0, 0] for _ in range(NLAY)]
            prev_land = {s: 0 for s in range(SLAB_H)}
            prev_xv = {s: 0 for s in range(SLAB_H)}
            prev_hsc = {s: 0 for s in range(SLAB_H)}
            prev_vmm_f = 0
            prev_ep = {0: 0, 1: 0}
            dtw_fence = 0
            xpw_fence = 0
            r2_ld = 0
            r2B_used = 0
            bc_ld = 0
            bb_ld = 0
            prev_opT = 0
            accst_fence = 0
            base_done = sV.n

            for c in range(NCH):
                for g in range(4):
                    LD(LX, t_xt[:, g * 1024:(g + 1) * 1024],
                       d_x[c * 512 + g * 128: c * 512 + (g + 1) * 128, :],
                       waits=[(sT, s0tp_fence), (sV, acc_fence),
                              (sT, prev_opT)] if g == 0 else [])
                xld_full = LX.n
                ln_v = []
                for g in range(4):
                    P.add("vector", (lambda a, g_: (lambda e: e.bn_stats(
                        t_bnst[:, 12 * g_:12 * g_ + 6], a)))(
                        t_xt[:, g * 1024: g * 1024 + 512], g),
                        waits=[(LX, xld_full)] if g == 0 else [], inc=sV)
                    P.add("vector", (lambda a, g_: (lambda e: e.bn_stats(
                        t_bnst[:, 12 * g_ + 6:12 * g_ + 12], a)))(
                        t_xt[:, g * 1024 + 512: (g + 1) * 1024], g), inc=sV)
                vstats = sV.n
                for g in range(4):
                    P.add("vector", (lambda g_: (lambda e: e.bn_aggr(
                        t_mv[:, 2 * g_:2 * g_ + 2], t_bnst[:, 12 * g_:12 * g_ + 12])))(g),
                        waits=[(sV, vstats)], inc=sV)
                vg = sV.n
                for g in range(4):
                    P.add("scalar", (lambda g_: (lambda e: e.activation(
                        out=t_r[:, g_:g_ + 1], in_=t_mv[:, 2 * g_ + 1:2 * g_ + 2],
                        func=AF.Ln, bias=t_cst[:, 0:1], scale=1.0)))(g),
                        waits=[(sV, vg)] if g == 0 else [], inc=sS)
                P.add("scalar", lambda e: e.drain(), inc=sS)
                for g in range(4):
                    P.add("scalar", (lambda g_: (lambda e: e.activation(
                        out=t_r[:, g_:g_ + 1], in_=t_r[:, g_:g_ + 1],
                        func=AF.Exp, scale=-0.5)))(g), inc=sS)
                P.add("scalar", lambda e: e.drain(), inc=sS)
                sg = sS.n
                for g in range(4):
                    P.add("vector", (lambda a, g_: (lambda e: e.tensor_scalar(
                        out=a, in0=a, scalar1=t_mv[:, 2 * g_:2 * g_ + 1],
                        scalar2=t_r[:, g_:g_ + 1], op0=OP.subtract, op1=OP.mult)))(
                        t_xt[:, g * 1024:(g + 1) * 1024], g),
                        waits=[(sS, sg)] if g == 0 else [], inc=sV)
                    ln_v.append(sV.n)
                ln_done = sV.n
                ev_f = {}
                for dsl in range(8):
                    bank = p[dsl % 2]
                    w0 = [(sV, ln_done), (sV, acc_fence)] if dsl == 0 else []
                    if dsl >= 2:
                        w0 = [(sV, ev_f[dsl - 2])]
                    last_t = None
                    for g in range(4):
                        last_t = P.add("tensor", (lambda bk, g_, d_: (lambda e: e.transpose(
                            bk[:, g_ * 128:(g_ + 1) * 128],
                            t_xt[:, g_ * 1024 + d_ * 128: g_ * 1024 + (d_ + 1) * 128],
                            t_id[:, :])))(bank, g, dsl),
                            waits=w0 if g == 0 else [], inc=sT)
                    P.add("vector", (lambda bk, d_: (lambda e: e.tensor_copy(
                        t_base[:, d_ * 512:(d_ + 1) * 512], bk[:, :])))(bank, dsl),
                        waits=[(sT, last_t), (SAC, accst_fence)], inc=sV)
                    ev_f[dsl] = sV.n
                s0tp_fence = sT.n
                base_done = sV.n

                for l in range(NLAY):
                    LD(LMW, t_dtw[:, :], d_wdt[l, :, :],
                       waits=[(sT, dtw_fence), (sT, xpw_fence)])
                    for k in range(16):
                        LD(LMW, t_xpw[:, k * 96:(k + 1) * 96], d_wxp[l, k, :, :])
                    mw_full = LMW.n
                    for dsl in range(8):
                        P.add("scalar", (lambda d_, i_: (lambda e: e.activation(
                            out=t_e[i_][:, :], in_=t_base[:, d_ * 512:(d_ + 1) * 512],
                            func=AF.Square)))(dsl, dsl % 2),
                            waits=[(sV, base_done)] if dsl < 2 else [(sT, sT.n)], inc=sS)
                        sq = sS.n
                        P.add("tensor", (lambda d_, i_: (lambda e: e.matmul(
                            p[2][0:1, :], t_ones[:, 0:1], t_e[i_][:, :],
                            start=(d_ == 0), stop=(d_ == 7))))(dsl, dsl % 2),
                            waits=[(sS, sq)], inc=sT)
                    rms_mm = sT.n
                    P.add("scalar", lambda e: e.activation(
                        out=t_r2[0:1, :], in_=p[2][0:1, :], func=AF.Ln,
                        bias=t_cst[0:1, 0:1], scale=1.0 / DM), waits=[(sT, rms_mm)], inc=sS)
                    P.add("scalar", lambda e: e.drain(), inc=sS)
                    P.add("scalar", lambda e: e.activation(
                        out=t_r2[0:1, :], in_=t_r2[0:1, :], func=AF.Exp, scale=-0.5), inc=sS)
                    rms_s = sS.n
                    r2st = ST(SR2, d_r2d, t_r2[0:1, :], waits=[(sS, rms_s), (LR2, r2_ld)])
                    r2_ld = LD(LR2, t_r2B[:, :], d_r2d[0:1, :].broadcast_to((128, 512)),
                               waits=[(SR2, r2st), (sV, r2B_used)])
                    for dsl in range(8):
                        P.add("vector", (lambda d_: (lambda e: e.tensor_tensor(
                            out=t_xt[:, d_ * 512:(d_ + 1) * 512],
                            in0=t_base[:, d_ * 512:(d_ + 1) * 512],
                            in1=t_r2B[:, :], op=OP.mult)))(dsl),
                            waits=[(LR2, r2_ld)] if dsl == 0 else [], inc=sV)
                        P.add("vector", (lambda d_, l_: (lambda e: e.tensor_scalar(
                            out=t_xt[:, d_ * 512:(d_ + 1) * 512],
                            in0=t_xt[:, d_ * 512:(d_ + 1) * 512],
                            scalar1=t_nw[l_][:, d_:d_ + 1], scalar2=None,
                            op0=OP.mult)))(dsl, l), inc=sV)
                    xnorm_done = sV.n
                    r2B_used = xnorm_done

                    consV = {}
                    consS = {}
                    wiT = {}
                    st_z = {}
                    st_xb = {}
                    for m in range(32):
                        pg = m % 2
                        lw = [(sT, wiT[m - 2])] if m >= 2 else \
                            [(sT, wi_fence[pg]), (sV, prev_ep[pg])]
                        wl = LD(LWI[pg], t_wi[pg][:, :], d_win[l, m, :, :], waits=lw)
                        bw = [(LWI[pg], wl)]
                        if m == 0:
                            bw.append((sV, xnorm_done))
                        if (m - 2) in consV:
                            bw.append((sV, consV[m - 2]))
                        if (m - 2) in consS:
                            bw.append((sS, consS[m - 2]))
                        for k in range(8):
                            P.add("tensor", (lambda pg_, k_: (lambda e: e.matmul(
                                p[pg_][:, :], t_wi[pg_][:, k_ * 128:(k_ + 1) * 128],
                                t_xt[:, k_ * 512:(k_ + 1) * 512],
                                start=(k_ == 0), stop=(k_ == 7))))(pg, k),
                                waits=bw if k == 0 else [], inc=sT)
                        wiT[m] = sT.n
                        wi_fence[pg] = sT.n
                        mm_done = sT.n
                        if m < 16:
                            slab = m
                            P.add("vector", (lambda pg_, s_, l_: (lambda e: e.tensor_scalar(
                                out=t_xc[pg_][:, :], in0=p[pg_][:, :],
                                scalar1=t_cw[l_][:, s_ * 4 + 3:s_ * 4 + 4], scalar2=None,
                                op0=OP.mult)))(pg, slab, l),
                                waits=[(sT, mm_done)], inc=sV)
                            for kk in (2, 1, 0):
                                sh = 3 - kk
                                P.add("vector", (lambda pg_, s_, l_, kk_, sh_: (lambda e:
                                    e.scalar_tensor_tensor(
                                        out=t_xc[pg_][:, sh_:512],
                                        in0=p[pg_][:, 0:512 - sh_],
                                        scalar=t_cw[l_][:, s_ * 4 + kk_:s_ * 4 + kk_ + 1],
                                        in1=t_xc[pg_][:, sh_:512],
                                        op0=OP.mult, op1=OP.add)))(pg, slab, l, kk, sh),
                                    inc=sV)
                                P.add("vector", (lambda pg_, s_, l_, kk_, sh_: (lambda e:
                                    e.scalar_tensor_tensor(
                                        out=t_xc[pg_][:, 0:sh_],
                                        in0=t_tail[l_][:, s_ * 3 + kk_:s_ * 3 + 3],
                                        scalar=t_cw[l_][:, s_ * 4 + kk_:s_ * 4 + kk_ + 1],
                                        in1=t_xc[pg_][:, 0:sh_],
                                        op0=OP.mult, op1=OP.add)))(pg, slab, l, kk, sh),
                                    inc=sV)
                            P.add("vector", (lambda pg_, s_, l_: (lambda e: e.tensor_copy(
                                t_tail[l_][:, s_ * 3:s_ * 3 + 3], p[pg_][:, 509:512])))(pg, slab, l),
                                inc=sV)
                            consV[m] = sV.n
                            conv_v = sV.n
                            P.add("scalar", (lambda pg_, s_, l_: (lambda e: e.activation(
                                out=t_xb[:, s_ * 512:(s_ + 1) * 512], in_=t_xc[pg_][:, :],
                                func=AF.Silu, bias=t_cb[l_][:, s_:s_ + 1], scale=1.0)))(pg, slab, l),
                                waits=[(sV, conv_v), (SXB, xbst_full_prev[l])], inc=sS)
                            consS[m] = sS.n
                            st_xb[slab] = ST(SXB, d_xbsp[l, c, slab, :, :],
                                             t_xb[:, slab * 512:(slab + 1) * 512],
                                             waits=[(sS, sS.n)])
                        else:
                            slab = m - 16
                            zw = [(sT, mm_done)]
                            zw.append((SZ[pg], st_z[slab - 2] if slab >= 2
                                       else stz_prev[l][slab % 2]))
                            P.add("scalar", (lambda pg_: (lambda e: e.activation(
                                out=t_sz[pg_][:, :], in_=p[pg_][:, :], func=AF.Silu)))(pg),
                                waits=zw, inc=sS)
                            consS[m] = sS.n
                            st_z[slab] = ST(SZ[pg], d_szsp[l, c, slab, :, :],
                                            t_sz[pg][:, :], waits=[(sS, sS.n)])
                            stz_prev[l][slab % 2] = st_z[slab]

                    xb_ok = consS[15]
                    xb_full = SXB.n
                    xbst_full_prev[l] = xb_full
                    for k in range(16):
                        P.add("tensor", (lambda k_, l_: (lambda e: e.matmul(
                            p[2][0:96, :], t_xpw[:, k_ * 96:(k_ + 1) * 96],
                            t_xb[:, k_ * 512:(k_ + 1) * 512],
                            start=(k_ == 0), stop=(k_ == 15))))(k, l),
                            waits=[(sS, xb_ok), (LMW, mw_full)] if k == 0 else [], inc=sT)
                    dbc_mm = sT.n
                    xpw_fence = dbc_mm
                    P.add("scalar", lambda e: e.activation(
                        out=t_dbc[0:96, :], in_=p[2][0:96, :], func=AF.Copy),
                        waits=[(sT, dbc_mm), (sG, sG.n)], inc=sS)
                    dbc_ev = sS.n
                    bc_st = ST(SBC, d_bc[16:32, :], t_dbc[80:96, :],
                               waits=[(sS, dbc_ev), (LCB, bc_ld)])
                    for uu in range(4):
                        P.add("tensor", (lambda uu_: (lambda e: e.transpose(
                            p[7][:, uu_ * 16:(uu_ + 1) * 16],
                            t_dbc[64:80, uu_ * 128:(uu_ + 1) * 128],
                            t_id[64:80, 64:80])))(uu),
                            waits=[(sS, dbc_ev)] if uu == 0 else [], inc=sT)
                    btp = sT.n
                    P.add("scalar", lambda e: e.activation(
                        out=t_BT[:, :], in_=p[7][:, 0:64], func=AF.Copy),
                        waits=[(sT, btp)], inc=sS)
                    btl = sS.n
                    dt_s = {}
                    for m in range(16):
                        pg = m % 2
                        w = [(sS, dbc_ev)]
                        if m >= 2:
                            w.append((sS, dt_s[m - 2]))
                        P.add("tensor", (lambda pg_, m_, l_: (lambda e: e.matmul(
                            p[pg_][:, :], t_dtw[:, m_ * 128:(m_ + 1) * 128],
                            t_dbc[0:64, :], start=True, stop=True)))(pg, m, l),
                            waits=w + ([(LMW, mw_full)] if m == 0 else []), inc=sT)
                        mm_t = sT.n
                        dtw_fence = mm_t
                        P.add("scalar", (lambda pg_, m_, l_: (lambda e: e.activation(
                            out=t_e[pg_][:, :], in_=p[pg_][:, :], func=AF.Exp,
                            bias=t_dtb[l_][:, m_:m_ + 1], scale=1.0)))(pg, m, l),
                            waits=[(sT, mm_t)], inc=sS)
                        P.add("scalar", lambda e: e.drain(), inc=sS)
                        P.add("scalar", (lambda pg_, m_: (lambda e: e.activation(
                            out=t_dl[:, m_ * 512:(m_ + 1) * 512], in_=t_e[pg_][:, :],
                            func=AF.Ln, bias=t_cst[:, 1:2], scale=1.0)))(pg, m), inc=sS)
                        dt_s[m] = sS.n
                        P.add("vector", (lambda m_: (lambda e: e.tensor_tensor(
                            out=t_xb[:, m_ * 512:(m_ + 1) * 512],
                            in0=t_dl[:, m_ * 512:(m_ + 1) * 512],
                            in1=t_xb[:, m_ * 512:(m_ + 1) * 512], op=OP.mult)))(m),
                            waits=[(sS, dt_s[m]), (SXB, xb_full)], inc=sV)
                    w_done = sV.n

                    wtl = {}
                    rdl = {}
                    for r in range(4):
                        b0 = (r % 2) * 4
                        tps = {}
                        for i in range(4):
                            s = 4 * r + i
                            tw = []
                            if r == 0 and i == 0:
                                tw.append((sV, w_done))
                            if r == 1 and i == 3:
                                tw.append((sS, btl))
                            if r >= 2:
                                tw.append((sS, rdl[r - 2]))
                            for uu in range(4):
                                P.add("tensor", (lambda s_, uu_, b_: (lambda e: e.transpose(
                                    p[b_][:, uu_ * 128:(uu_ + 1) * 128],
                                    t_xb[:, s_ * 512 + uu_ * 128: s_ * 512 + (uu_ + 1) * 128],
                                    t_id[:, :])))(s, uu, b0 + i),
                                    waits=tw if uu == 0 else [], inc=sT)
                            tps[i] = sT.n
                        for i in range(4):
                            s = 4 * r + i
                            P.add("scalar", (lambda s_, b_: (lambda e: e.activation(
                                out=t_xb[:, s_ * 512:(s_ + 1) * 512], in_=p[b_][:, :],
                                func=AF.Copy)))(s, b0 + i),
                                waits=[(sT, tps[i])], inc=sS)
                            wtl[s] = sS.n
                        rdl[r] = sS.n

                    vmm_last = [prev_vmm_f]
                    vml = [prev_vmm_f, prev_vmm_f]
                    epd_v = {}
                    for hf in range(2):
                        s0 = hf * SLAB_H
                        hsc = {}
                        dA_c = {}
                        xv_c = {}
                        hc_c = {}
                        xh_n = {}
                        cb_n = {}
                        vmm = {}
                        land_c = {}
                        rhs_c = {}

                        def load_cb(n, hf=hf, cb_n=cb_n, hc_c=hc_c):
                            par = n % 2
                            gw = [(SBC, bc_st)]
                            if n >= 2:
                                gw.append((sV, hc_c[n - 2]))
                            elif hf == 1:
                                gw.append((sV, prev_half_hc))
                            cb_n[n] = LD(LCB, t_Cb[:, par * 512:(par + 1) * 512],
                                         d_bc[16 + n:17 + n, :].broadcast_to((128, 512)),
                                         waits=gw)

                        def emit_rhs_uu(n, uu, rhs_c=rhs_c):
                            w0 = [(sT, vml[n % 2])] if uu == 0 else []
                            pb = (n % 2) * 1280
                            P.add("scalar", (lambda n_, uu_, pb_: (lambda e: e.activation(
                                out=t_rhs[:, pb_ + RHS_OFF[uu_]:
                                          pb_ + RHS_OFF[uu_] + 512 - 128 * uu_],
                                in_=t_tb0[:, 0:512 - 128 * uu_], func=AF.Copy,
                                scale=t_BT[:, uu_ * 16 + n_:uu_ * 16 + n_ + 1])))(n, uu, pb),
                                waits=w0, inc=sS)
                            rhs_c[n] = sS.n

                        def emit_rhs(n):
                            for uu in range(4):
                                emit_rhs_uu(n, uu)

                        def emit_vmm(n, hf=hf, s0=s0, vmm=vmm, land_c=land_c,
                                     rhs_c=rhs_c):
                            pb = (n % 2) * 1280
                            for s in range(SLAB_H):
                                slab = s0 + s
                                for uu in range(4):
                                    mw = []
                                    if uu == 0:
                                        mw.append((sS, rhs_c[n]))
                                        mw.append((sV, land_c[(n - 1, s)] if n > 0
                                                   else prev_land[s]))
                                    P.add("tensor", (lambda s_, slab_, uu_, pb_: (lambda e: e.matmul(
                                        p[s_][:, 128 * uu_:512],
                                        t_xb[:, slab_ * 512 + uu_ * 128: slab_ * 512 + (uu_ + 1) * 128],
                                        t_rhs[:, pb_ + RHS_OFF[uu_]:
                                              pb_ + RHS_OFF[uu_] + 512 - 128 * uu_],
                                        start=(uu_ == 0), stop=(uu_ == 3))))(s, slab, uu, pb),
                                        waits=mw, inc=sT)
                                vmm[(n, s)] = sT.n
                            vml[n % 2] = sT.n
                            vmm_last[0] = sT.n

                        def emit_land(n, hf=hf, s0=s0, l=l, vmm=vmm,
                                      land_c=land_c, xv_c=xv_c):
                            for s in range(SLAB_H):
                                slab = s0 + s
                                lw = [(sT, vmm[(n, s)])]
                                if n > 0:
                                    lw.append((sS, xv_c[(n - 1, s)]))
                                else:
                                    lw.append((sS, prev_xv[s]))
                                P.add("vector", (lambda s_, slab_, l_, n_: (lambda e:
                                    e.scalar_tensor_tensor(
                                        out=t_vw[:, s_ * 512:(s_ + 1) * 512],
                                        in0=t_bpw[:, :],
                                        scalar=t_vst[l_][:, slab_ * 16 + n_:slab_ * 16 + n_ + 1],
                                        in1=p[s_][:, :], op0=OP.mult,
                                        op1=OP.add)))(s, slab, l, n),
                                    waits=lw, inc=sV)
                                land_c[(n, s)] = sV.n
                                P.add("scalar", (lambda s_, slab_, l_, n_: (lambda e: e.activation(
                                    out=t_vst[l_][:, slab_ * 16 + n_:slab_ * 16 + n_ + 1],
                                    in_=t_vw[:, s_ * 512 + 511:s_ * 512 + 512],
                                    func=AF.Copy)))(s, slab, l, n),
                                    waits=[(sV, land_c[(n, s)])], inc=sS)
                                xv_c[(n, s)] = sS.n

                        load_cb(0)
                        load_cb(1)
                        emit_rhs(0)
                        emit_vmm(0)
                        emit_land(0)
                        for n in range(16):
                            par = n % 2
                            if n + 1 < 16:
                                emit_rhs(n + 1)
                                emit_vmm(n + 1)
                            for s in range(SLAB_H):
                                slab = s0 + s
                                sw = []
                                if s >= 2:
                                    sw.append((sV, hsc[(n, s - 2)]))
                                elif n > 0:
                                    sw.append((sV, hsc[(n - 1, SLAB_H - 2 + s)]))
                                else:
                                    sw.append((sV, prev_hsc[SLAB_H - 2 + s]))
                                P.add("scalar", (lambda s_, slab_, l_, n_: (lambda e: e.activation(
                                    out=t_dA[s_ % 2][:, :],
                                    in_=t_dl[:, slab_ * 512:(slab_ + 1) * 512],
                                    func=AF.Exp,
                                    scale=t_negA[l_][:, slab_ * 16 + n_:slab_ * 16 + n_ + 1])))(s, slab, l, n),
                                    waits=sw, inc=sS)
                                dA_c[(n, s)] = sS.n
                                P.add("vector", (lambda s_, slab_, l_, n_: (lambda e: e.tensor_tensor_scan(
                                    out=t_vh[:, s_ * 512:(s_ + 1) * 512],
                                    data0=t_dA[s_ % 2][:, :],
                                    data1=t_vw[:, s_ * 512:(s_ + 1) * 512],
                                    initial=t_hst[l_][:, slab_ * 16 + n_:slab_ * 16 + n_ + 1],
                                    op0=OP.mult, op1=OP.add)))(s, slab, l, n),
                                    waits=[(sS, dA_c[(n, s)])], inc=sV)
                                hsc[(n, s)] = sV.n
                            P.add("scalar", (lambda l_, n_, s0_: (lambda e: e.activation(
                                out=t_hst[l_][:, s0_ * 16 + n_: s0_ * 16 + n_ + 113: 16],
                                in_=t_vh[:, 511::512], func=AF.Copy)))(l, n, s0),
                                waits=[(sV, hsc[(n, SLAB_H - 1)])], inc=sS)
                            xh_n[n] = sS.n
                            vw = [(LCB, cb_n[n]), (sS, xh_n[n])]
                            if n == 0:
                                P.add("vector", (lambda par_: (lambda e: e.tensor_tensor(
                                    out=t_y[:, :].rearrange("p (r f) -> p r f", r=SLAB_H),
                                    in0=t_vh[:, :].rearrange("p (r f) -> p r f", r=SLAB_H),
                                    in1=t_Cb[:, par_ * 512:(par_ + 1) * 512].unsqueeze(1)
                                    .broadcast_to((128, SLAB_H, 512)),
                                    op=OP.mult)))(par),
                                    waits=vw, inc=sV)
                                hc_c[n] = sV.n
                                yac_w = sV.n
                            else:
                                P.add("vector", (lambda par_: (lambda e: e.tensor_tensor(
                                    out=t_vh[:, :].rearrange("p (r f) -> p r f", r=SLAB_H),
                                    in0=t_vh[:, :].rearrange("p (r f) -> p r f", r=SLAB_H),
                                    in1=t_Cb[:, par_ * 512:(par_ + 1) * 512].unsqueeze(1)
                                    .broadcast_to((128, SLAB_H, 512)),
                                    op=OP.mult)))(par),
                                    waits=vw, inc=sV)
                                hc_c[n] = sV.n
                                P.add("vector", lambda e: e.tensor_tensor(
                                    out=t_y[:, :], in0=t_y[:, :],
                                    in1=t_vh[:, :], op=OP.add), inc=sV)
                                yac_w = sV.n
                            if n + 1 < 16:
                                emit_land(n + 1)
                            if n + 2 < 16:
                                load_cb(n + 2)
                        prev_half_hc = hc_c[15]
                        prev_land = {s: land_c[(15, s)] for s in range(SLAB_H)}
                        prev_xv = {s: xv_c[(15, s)] for s in range(SLAB_H)}
                        prev_hsc = {s: hsc[(15, s)] for s in range(SLAB_H)}
                        xh_fence = sS.n

                        ep_v = {}
                        for s in range(SLAB_H):
                            slab = s0 + s
                            pg = s % 2
                            w = [(sV, ep_v[s - 2] if s >= 2 else prev_ep[s % 2]),
                                 (sT, wi_fence[pg]), (SXB, xb_full),
                                 (SZ[pg], st_z[14 + pg])]
                            rx = LD(LWI[pg], t_xbr[pg], d_xbsp[l, c, slab, :, :],
                                    waits=w)
                            rz = LD(LWI[pg], t_szr[pg], d_szsp[l, c, slab, :, :])
                            y2w = [(LWI[pg], rz)]
                            if s == 0:
                                y2w.append((sV, yac_w))
                                y2w.append((sS, xh_n[15]))
                            P.add("vector", (lambda pg_, s_, l_, slab_: (lambda e:
                                e.scalar_tensor_tensor(
                                    out=t_y2[pg_][:, :], in0=t_xbr[pg_],
                                    scalar=t_dp[l_][:, slab_:slab_ + 1],
                                    in1=t_y[:, s_ * 512:(s_ + 1) * 512],
                                    op0=OP.mult, op1=OP.add)))(pg, s, l, slab),
                                waits=y2w, inc=sV)
                            P.add("vector", (lambda pg_, slab_: (lambda e: e.tensor_tensor(
                                out=t_dl[:, slab_ * 512:(slab_ + 1) * 512],
                                in0=t_y2[pg_][:, :], in1=t_szr[pg_],
                                op=OP.mult)))(pg, slab), inc=sV)
                            ep_v[s] = sV.n
                            epd_v[slab] = sV.n
                        prev_ep = {0: ep_v[6], 1: ep_v[7]}
                    prev_vmm_f = vmm_last[0]

                    ow_T_l = {}
                    for slab in range(16):
                        pg = slab % 2
                        oww = [(sT, ow_T_l[slab - 2])] if slab >= 2 else \
                            [(sT, wo_fence[pg])]
                        owl = LD(LWO[pg], t_wo[pg][:, :], d_wout[l, slab, :, :],
                                 waits=oww)
                        mw = [(LWO[pg], owl), (sV, epd_v[slab])]
                        if slab == 0:
                            mw.append((sV, prev_land[SLAB_H - 1]))
                        for m in range(8):
                            P.add("tensor", (lambda pg_, m_, slab_: (lambda e: e.matmul(
                                p[m_][:, :], t_wo[pg_][:, m_ * 128:(m_ + 1) * 128],
                                t_dl[:, slab_ * 512:(slab_ + 1) * 512],
                                start=(slab_ == 0), stop=(slab_ == 15))))(pg, m, slab),
                                waits=mw if m == 0 else [], inc=sT)
                        ow_T_l[slab] = sT.n
                        wo_fence[pg] = sT.n
                    op_T = sT.n
                    for m in range(8):
                        P.add("vector", (lambda m_: (lambda e: e.tensor_tensor(
                            out=t_base[:, m_ * 512:(m_ + 1) * 512],
                            in0=t_base[:, m_ * 512:(m_ + 1) * 512],
                            in1=p[m_][:, :], op=OP.add)))(m),
                            waits=[(sT, op_T)] if m == 0 else [], inc=sV)
                    base_done = sV.n

                last_t = None
                for tg in range(4):
                    for dsl in range(8):
                        bk = p[2 * tg + dsl // 4]
                        last_t = P.add("tensor", (lambda bk_, d_, tg_: (lambda e: e.transpose(
                            bk_[:, (d_ % 4) * 128:((d_ % 4) + 1) * 128],
                            t_base[:, d_ * 512 + tg_ * 128: d_ * 512 + (tg_ + 1) * 128],
                            t_id[:, :])))(bk, dsl, tg),
                            waits=[(sV, base_done)] if (tg == 0 and dsl == 0) else [],
                            inc=sT)
                acc_v = {}
                acc_st = {}
                for tg in range(4):
                    pg = tg % 2
                    xw = [(sT, op_T if tg < 2 else last_t)]
                    if tg >= 2:
                        xw.append((sV, acc_v[tg - 2]))
                    rx2 = LD(LXP[pg], t_xt[:, pg * 1024:(pg + 1) * 1024],
                             d_x[c * 512 + tg * 128: c * 512 + (tg + 1) * 128, :],
                             waits=xw)
                    aw = [(sT, last_t), (LXP[pg], rx2), (SAC, accst_fence)]
                    P.add("vector", (lambda pg_, tg_: (lambda e: e.tensor_tensor(
                        out=t_base[:, tg_ * 1024: tg_ * 1024 + 512],
                        in0=p[2 * tg_][:, :],
                        in1=t_xt[:, pg_ * 1024: pg_ * 1024 + 512], op=OP.add)))(pg, tg),
                        waits=aw, inc=sV)
                    P.add("vector", (lambda pg_, tg_: (lambda e: e.tensor_tensor(
                        out=t_base[:, tg_ * 1024 + 512: (tg_ + 1) * 1024],
                        in0=p[2 * tg_ + 1][:, :],
                        in1=t_xt[:, pg_ * 1024 + 512: (pg_ + 1) * 1024], op=OP.add)))(pg, tg),
                        inc=sV)
                    acc_v[tg] = sV.n
                    acc_st[tg] = ST(SAC, d_acc[c * 512 + tg * 128: c * 512 + (tg + 1) * 128, :],
                                    t_base[:, tg * 1024:(tg + 1) * 1024],
                                    waits=[(sV, acc_v[tg])])
                acc_fence = acc_v[3]
                accst_fence = acc_st[3]
                prev_opT = op_T

            A_V, A_S, A_T, A_G = sV.n, sS.n, sT.n, sG.n

        barrier = ([(sV, A_V), (sS, A_S), (sT, A_T), (sG, A_G)]
                   + [(s, s.n) for s in ALL_DMA_SEMS if s.n > 0])

        with ExitStack() as bctx:
            BE = bctx.enter_context
            q0 = [BE(nc.psum_tensor(f"q0{i}", [128, 512], F32)) for i in range(2)]
            qs = BE(nc.psum_tensor("qs", [128, 512], F32))
            pQ = [BE(nc.psum_tensor(f"pQ{i}", [128, 1024], BF16)) for i in range(2)]
            t_w1q = [BE(nc.sbuf_tensor(f"t_w1q{i}", [128, 1024], BF16)) for i in range(32)]
            t_wp = [BE(nc.sbuf_tensor(f"t_wp{i}", [128, 1024], F32)) for i in range(2)]
            t_sgn = BE(nc.sbuf_tensor("t_sgn", [128, 1024], F32))
            t_absv = BE(nc.sbuf_tensor("t_absv", [128, 1024], F32))
            t_absb = BE(nc.sbuf_tensor("t_absb", [128, 33], F32))
            t_sc = BE(nc.sbuf_tensor("t_sc", [1, 8], F32))
            t_thB = BE(nc.sbuf_tensor("t_thB", [128, 2], F32))
            t_xt2 = BE(nc.sbuf_tensor("t_xt2", [128, 2048], F32))
            t_bn2 = BE(nc.sbuf_tensor("t_bn2", [128, 24], F32))
            t_mv2 = BE(nc.sbuf_tensor("t_mv2", [128, 4], F32))
            t_rc2 = BE(nc.sbuf_tensor("t_rc2", [128, 8], F32))
            t_qf = BE(nc.sbuf_tensor("t_qf", [128, 1024], F32))
            t_qi = BE(nc.sbuf_tensor("t_qi", [128, 1024], I32))
            t_qb = BE(nc.sbuf_tensor("t_qb", [128, 2048], BF16))
            t_xqf = BE(nc.sbuf_tensor("t_xqf", [128, 8 * 256], BF16))
            t_s1r = BE(nc.sbuf_tensor("t_s1r", [2, 128], F32))
            t_s1B = BE(nc.sbuf_tensor("t_s1B", [128, 256], F32))
            t_gp = [BE(nc.sbuf_tensor(f"t_gp{i}", [128, 256], F32)) for i in range(2)]
            t_b1 = BE(nc.sbuf_tensor("t_b1", [128, 32], F32))

            LD(LMW, t_b1[:, :], d_b1, waits=barrier)

            w1red = {}
            for i in range(32):
                w = [(sV, w1red[i - 2])] if i >= 2 else []
                wl = LD(LWI[i % 2], t_wp[i % 2][:, :], d_w1[i, :, :], waits=w)
                P.add("vector", (lambda i_: (lambda e: e.tensor_reduce(
                    out=t_absb[:, i_:i_ + 1], in_=t_wp[i_ % 2][:, :], axis=AX.X,
                    op=OP.add, apply_absolute_value=True)))(i),
                    waits=[(LWI[i % 2], wl)] + (barrier if i == 0 else []), inc=sV)
                w1red[i] = sV.n
            P.add("vector", lambda e: e.drain(), inc=sV)
            P.add("vector", lambda e: e.tensor_reduce(
                out=t_absb[:, 32:33], in_=t_absb[:, 0:32], axis=AX.X, op=OP.add), inc=sV)
            vv = sV.n
            P.add("tensor", lambda e: e.matmul(
                qs[0:1, 0:1], t_ones[:, 0:1], t_absb[:, 32:33], start=True, stop=True),
                waits=[(sV, vv)] + barrier, inc=sT)
            wsT = sT.n
            P.add("vector", lambda e: e.tensor_scalar(
                out=t_sc[0:1, 0:1], in0=qs[0:1, 0:1], scalar1=float(2.0 ** -22),
                scalar2=EPS, op0=OP.mult, op1=OP.max), waits=[(sT, wsT)], inc=sV)
            P.add("vector", lambda e: e.tensor_scalar(
                out=t_sc[0:1, 1:2], in0=qs[0:1, 0:1], scalar1=float(2.0 ** -23),
                scalar2=EPS * 0.5, op0=OP.mult, op1=OP.max), inc=sV)
            P.add("vector", lambda e: e.tensor_scalar(
                out=t_sc[0:1, 2:3], in0=qs[0:1, 0:1], scalar1=float(2.0 ** -22 / 127.0),
                scalar2=EPS / 127.0, op0=OP.mult, op1=OP.max), inc=sV)
            P.add("vector", lambda e: e.drain(), inc=sV)
            scv = sV.n
            th_st = ST(STH, d_scd[0:1, 0:3], t_sc[0:1, 0:3], waits=[(sV, scv)])
            th_ld = LD(LTH, t_thB[:, 0:2], d_scd[0:1, 1:3].broadcast_to((128, 2)),
                       waits=[(STH, th_st)])
            thg = th_ld
            w1q_v = {}
            sgn_f = 0
            for i in range(32):
                w = [(sV, w1q_v[i - 2])] if i >= 2 else [(sV, w1red[31])]
                rl = LD(LWI[i % 2], t_wp[i % 2][:, :], d_w1[i, :, :], waits=w)
                P.add("scalar", (lambda i_: (lambda e: e.activation(
                    out=t_sgn[:, :], in_=t_wp[i_ % 2][:, :], func=AF.Sign)))(i),
                    waits=[(LWI[i % 2], rl), (sV, sgn_f)] + (barrier if i == 0 else []), inc=sS)
                P.add("scalar", (lambda i_: (lambda e: e.activation(
                    out=t_absv[:, :], in_=t_wp[i_ % 2][:, :], func=AF.Abs)))(i), inc=sS)
                ssn = sS.n
                P.add("vector", lambda e: e.tensor_scalar(
                    out=t_absv[:, :], in0=t_absv[:, :], scalar1=t_thB[:, 0:1],
                    scalar2=None, op0=OP.is_gt), waits=[(sS, ssn), (LTH, thg)], inc=sV)
                P.add("vector", (lambda i_: (lambda e: e.tensor_tensor(
                    out=t_w1q[i_][:, :], in0=t_absv[:, :], in1=t_sgn[:, :],
                    op=OP.mult)))(i), inc=sV)
                w1q_v[i] = sV.n
                sgn_f = sV.n
            w1q_done = sV.n

            prev_q_done = 0
            prev_sgn = 0
            s1_ldf = 0
            s1B_used = 0
            prev_g = {30: 0, 31: 0}
            prev_tq = 0
            prev_gst = {30: 0, 31: 0}
            for c in range(NBC):
                for g in range(2):
                    LD(LX, t_xt2[:, g * 1024:(g + 1) * 1024],
                       d_x[c * 256 + g * 128: c * 256 + (g + 1) * 128, :],
                       waits=([(sV, prev_q_done)] if g == 0 else []))
                x2_full = LX.n
                qv = {}
                for g in range(2):
                    w = [(LX, x2_full)] if g == 0 else []
                    if c == 0 and g == 0:
                        w.append((sV, w1q_done))
                    P.add("vector", (lambda a, g_: (lambda e: e.bn_stats(
                        t_bn2[:, 12 * g_:12 * g_ + 6], a)))(
                        t_xt2[:, g * 1024:g * 1024 + 512], g), waits=w, inc=sV)
                    P.add("vector", (lambda a, g_: (lambda e: e.bn_stats(
                        t_bn2[:, 12 * g_ + 6:12 * g_ + 12], a)))(
                        t_xt2[:, g * 1024 + 512:(g + 1) * 1024], g), inc=sV)
                for g in range(2):
                    P.add("vector", (lambda g_: (lambda e: e.bn_aggr(
                        t_mv2[:, 2 * g_:2 * g_ + 2], t_bn2[:, 12 * g_:12 * g_ + 12])))(g),
                        inc=sV)
                vg = sV.n
                for g in range(2):
                    P.add("scalar", (lambda g_: (lambda e: e.activation(
                        out=t_rc2[:, g_:g_ + 1], in_=t_mv2[:, 2 * g_ + 1:2 * g_ + 2],
                        func=AF.Ln, bias=t_cst[:, 0:1], scale=1.0)))(g),
                        waits=[(sV, vg)] if g == 0 else [], inc=sS)
                P.add("scalar", lambda e: e.drain(), inc=sS)
                for g in range(2):
                    P.add("scalar", (lambda g_: (lambda e: e.activation(
                        out=t_rc2[:, g_:g_ + 1], in_=t_rc2[:, g_:g_ + 1],
                        func=AF.Exp, scale=-0.5)))(g), inc=sS)
                sg = sS.n
                for g in range(2):
                    P.add("vector", (lambda a, g_: (lambda e: e.tensor_scalar(
                        out=a, in0=a, scalar1=t_mv2[:, 2 * g_:2 * g_ + 1],
                        scalar2=t_rc2[:, g_:g_ + 1], op0=OP.subtract, op1=OP.mult)))(
                        t_xt2[:, g * 1024:(g + 1) * 1024], g),
                        waits=[(sS, sg)] if g == 0 else [], inc=sV)
                for g in range(2):
                    P.add("vector", (lambda a, g_: (lambda e: e.tensor_reduce(
                        out=t_rc2[:, 2 + g_:3 + g_], in_=a, axis=AX.X, op=OP.max,
                        apply_absolute_value=True)))(
                        t_xt2[:, g * 1024:(g + 1) * 1024], g), inc=sV)
                P.add("vector", lambda e: e.drain(), inc=sV)
                for g in range(2):
                    P.add("vector", (lambda g_: (lambda e: e.tensor_scalar(
                        out=t_rc2[:, 2 + g_:3 + g_], in0=t_rc2[:, 2 + g_:3 + g_],
                        scalar1=EPS, scalar2=None, op0=OP.max)))(g), inc=sV)
                P.add("vector", lambda e: e.drain(), inc=sV)
                for g in range(2):
                    P.add("vector", (lambda g_: (lambda e: e.reciprocal(
                        out=t_rc2[:, 4 + g_:5 + g_], in_=t_rc2[:, 2 + g_:3 + g_])))(g), inc=sV)
                P.add("vector", lambda e: e.drain(), inc=sV)
                for g in range(2):
                    P.add("vector", (lambda g_: (lambda e: e.tensor_scalar(
                        out=t_rc2[:, 4 + g_:5 + g_], in0=t_rc2[:, 4 + g_:5 + g_],
                        scalar1=127.0, scalar2=None, op0=OP.mult)))(g), inc=sV)
                for g in range(2):
                    P.add("vector", (lambda g_: (lambda e: e.tensor_scalar(
                        out=t_rc2[:, 6 + g_:7 + g_], in0=t_rc2[:, 2 + g_:3 + g_],
                        scalar1=t_thB[:, 1:2], scalar2=None, op0=OP.mult)))(g), inc=sV)
                P.add("vector", lambda e: e.drain(), inc=sV)
                vq0 = sV.n
                for g in range(2):
                    gsl = t_xt2[:, g * 1024:(g + 1) * 1024]
                    P.add("vector", (lambda a, g_: (lambda e: e.tensor_scalar(
                        out=t_qf[:, :], in0=a, scalar1=t_rc2[:, 4 + g_:5 + g_],
                        scalar2=None, op0=OP.mult)))(gsl, g),
                        waits=[(sV, vq0)] if g == 0 else [], inc=sV)
                    P.add("vector", lambda e: e.tensor_copy(t_qi[:, :], t_qf[:, :]), inc=sV)
                    P.add("vector", lambda e: e.tensor_tensor(
                        out=t_sgn[:, :], in0=t_qf[:, :], in1=t_qi[:, :],
                        op=OP.subtract), inc=sV)
                    P.add("vector", lambda e: e.scalar_tensor_tensor(
                        out=t_qf[:, :], in0=t_sgn[:, :], scalar=0.5, in1=t_qi[:, :],
                        op0=OP.is_gt, op1=OP.add), inc=sV)
                    P.add("vector", lambda e: e.tensor_scalar(
                        out=t_sgn[:, :], in0=t_sgn[:, :], scalar1=-0.5, scalar2=None,
                        op0=OP.is_lt), inc=sV)
                    P.add("vector", lambda e: e.tensor_tensor(
                        out=t_qf[:, :], in0=t_qf[:, :], in1=t_sgn[:, :],
                        op=OP.subtract), inc=sV)
                    P.add("vector", (lambda g_: (lambda e: e.tensor_scalar(
                        out=t_qb[:, g_ * 1024:(g_ + 1) * 1024], in0=t_qf[:, :],
                        scalar1=127.0, scalar2=-128.0, op0=OP.min, op1=OP.max)))(g), inc=sV)
                    qv[g] = sV.n
                q_done = sV.n
                prev_q_done = q_done
                prev_sgn = sS.n
                P.add("tensor", lambda e: e.transpose(
                    qs[0:2, 0:128], t_rc2[:, 6:8], t_id[:, :]),
                    waits=[(sV, q_done), (sV, prev_s1e if c > 0 else 0)], inc=sT)
                s1t = sT.n
                P.add("vector", lambda e: e.tensor_copy(t_s1r[:, :], qs[0:2, 0:128]),
                      waits=[(sT, s1t)], inc=sV)
                s1e = sV.n
                prev_s1e = s1e
                s1_st = ST(SS1, d_rw1, t_s1r[:, :], waits=[(sV, s1e), (LS1, s1_ldf)])
                LD(LS1, t_s1B[:, 0:128], d_rw1[0:1, :].broadcast_to((128, 128)),
                   waits=[(SS1, s1_st), (sV, s1B_used)])
                s1_ldf = LD(LS1, t_s1B[:, 128:256], d_rw1[1:2, :].broadcast_to((128, 128)))
                s1g = s1_ldf
                ev_v = {}
                for hp in range(2):
                    bank = pQ[hp]
                    last_t = None
                    for dd in range(4):
                        dsl = hp * 4 + dd
                        for g in range(2):
                            w = []
                            if dd == 0 and g == 0:
                                w = [(sV, q_done)]
                                if c > 0:
                                    w.append((sV, prev_ev[hp]))
                            last_t = P.add("tensor", (lambda bk, d_, g_: (lambda e: e.transpose(
                                bk[:, (2 * (d_ % 4) + g_) * 128:(2 * (d_ % 4) + g_ + 1) * 128],
                                t_qb[:, g_ * 1024 + d_ * 128: g_ * 1024 + (d_ + 1) * 128],
                                t_idb[:, :])))(bank, dsl, g), waits=w, inc=sT)
                    for dd in range(4):
                        dsl = hp * 4 + dd
                        for g in range(2):
                            P.add("vector", (lambda bk, d_, g_: (lambda e: e.tensor_copy(
                                t_xqf[:, d_ * 256 + g_ * 128: d_ * 256 + (g_ + 1) * 128],
                                bk[:, (2 * (d_ % 4) + g_) * 128:(2 * (d_ % 4) + g_ + 1) * 128])))(bank, dsl, g),
                                waits=[(sT, last_t)] if (dd == 0 and g == 0) else [], inc=sV)
                    ev_v[hp] = sV.n
                prev_ev = ev_v
                xq_ev = sV.n
                g_done = {}
                gst = {}
                for m in range(32):
                    pg = m % 2
                    w = [(sV, xq_ev), (LS1, s1g)]
                    if m >= 2:
                        w.append((sS, g_done[m - 2]))
                    else:
                        w.append((sS, prev_g[30 + m]))
                    for k in range(8):
                        P.add("tensor", (lambda pg_, k_, m_: (lambda e: e.matmul(
                            q0[pg_][:, 0:256], t_w1q[m_][:, k_ * 128:(k_ + 1) * 128],
                            t_xqf[:, k_ * 256:(k_ + 1) * 256],
                            start=(k_ == 0), stop=(k_ == 7))))(pg, k, m),
                            waits=w if k == 0 else [], inc=sT)
                    mmt = sT.n
                    vw = [(sT, mmt)]
                    vw.append((SGP[pg], gst[m - 2] if m >= 2 else prev_gst[30 + m]))
                    P.add("vector", (lambda pg_: (lambda e: e.tensor_tensor(
                        out=t_gp[pg_][:, :], in0=q0[pg_][:, 0:256], in1=t_s1B[:, :],
                        op=OP.mult)))(pg), waits=vw, inc=sV)
                    s1B_used = sV.n
                    vt = sV.n
                    P.add("scalar", (lambda pg_, m_: (lambda e: e.activation(
                        out=t_gp[pg_][:, :], in_=t_gp[pg_][:, :], func=AF.Gelu,
                        bias=t_b1[:, m_:m_ + 1], scale=1.0)))(pg, m),
                        waits=[(sV, vt)], inc=sS)
                    g_done[m] = sS.n
                    gst[m] = ST(SGP[pg], d_gel[c, m, :, :], t_gp[pg][:, :],
                                waits=[(sS, g_done[m])])
                prev_mm1 = sT.n
                prev_g = {30: g_done[30], 31: g_done[31]}
                prev_gst = {30: gst[30], 31: gst[31]}

            B1_V, B1_S, B1_T, B1_G = sV.n, sS.n, sT.n, sG.n

        barrier2 = ([(sV, B1_V), (sS, B1_S), (sT, B1_T), (sG, B1_G)]
                    + [(s, s.n) for s in ALL_DMA_SEMS if s.n > 0])

        with ExitStack() as b2ctx:
            BE2 = b2ctx.enter_context
            pm2 = [BE2(nc.psum_tensor(f"pm2{i}", [128, 512], F32)) for i in range(2)]
            pt2 = [BE2(nc.psum_tensor(f"pt2{i}", [128, 512], F32)) for i in range(2)]
            po2 = [BE2(nc.psum_tensor(f"po2{i}", [128, 1024], F32)) for i in range(2)]
            t_w2q = [BE2(nc.sbuf_tensor(f"t_w2q{i}", [128, 4096], BF16)) for i in range(8)]
            t_wp2 = [BE2(nc.sbuf_tensor(f"t_wp2{i}", [128, 4096], F32)) for i in range(2)]
            t_sgn2 = BE2(nc.sbuf_tensor("t_sgn2", [128, 4096], F32))
            t_abs2 = BE2(nc.sbuf_tensor("t_abs2", [128, 4096], F32))
            t_ab2 = BE2(nc.sbuf_tensor("t_ab2", [128, 9], F32))
            t_sc2 = BE2(nc.sbuf_tensor("t_sc2", [1, 4], F32))
            t_th2 = BE2(nc.sbuf_tensor("t_th2", [128, 2], F32))
            t_gel = BE2(nc.sbuf_tensor("t_gel", [128, 32 * 256], F32))
            t_mx = BE2(nc.sbuf_tensor("t_mx", [128, 20], F32))
            t_c2r = BE2(nc.sbuf_tensor("t_c2r", [2, 128], F32))
            t_c2B = BE2(nc.sbuf_tensor("t_c2B", [128, 256], F32))
            t_s2B = BE2(nc.sbuf_tensor("t_s2B", [128, 256], F32))
            t_sgn3 = BE2(nc.sbuf_tensor("t_sgn3", [128, 256], F32))
            t_qf2 = BE2(nc.sbuf_tensor("t_qf2", [128, 256], F32))
            t_qi2 = BE2(nc.sbuf_tensor("t_qi2", [128, 256], I32))
            t_xq2 = BE2(nc.sbuf_tensor("t_xq2", [128, 32 * 256], BF16))
            t_ff = BE2(nc.sbuf_tensor("t_ff", [128, 8 * 256], F32))
            t_b2 = BE2(nc.sbuf_tensor("t_b2", [128, 8], F32))
            t_acr = [BE2(nc.sbuf_tensor(f"t_acr{i}", [128, 1024], F32)) for i in range(2)]
            t_op = [BE2(nc.sbuf_tensor(f"t_op{i}", [128, 1024], F32)) for i in range(2)]

            LD(LMW, t_b2[:, :], d_b2, waits=barrier2)
            w2red = {}
            for i in range(8):
                w = [(sV, w2red[i - 2])] if i >= 2 else []
                wl = LD(LWI[i % 2], t_wp2[i % 2][:, :], d_w2[i, :, :], waits=w)
                P.add("vector", (lambda i_: (lambda e: e.tensor_reduce(
                    out=t_ab2[:, i_:i_ + 1], in_=t_wp2[i_ % 2][:, :], axis=AX.X,
                    op=OP.add, apply_absolute_value=True)))(i),
                    waits=[(LWI[i % 2], wl)] + (barrier2 if i == 0 else []), inc=sV)
                w2red[i] = sV.n
            P.add("vector", lambda e: e.drain(), inc=sV)
            P.add("vector", lambda e: e.tensor_reduce(
                out=t_ab2[:, 8:9], in_=t_ab2[:, 0:8], axis=AX.X, op=OP.add), inc=sV)
            vv = sV.n
            P.add("tensor", lambda e: e.matmul(
                pm2[0][0:1, 0:1], t_ones[:, 0:1], t_ab2[:, 8:9], start=True, stop=True),
                waits=[(sV, vv)] + barrier2, inc=sT)
            wsT = sT.n
            P.add("vector", lambda e: e.tensor_scalar(
                out=t_sc2[0:1, 0:1], in0=pm2[0][0:1, 0:1], scalar1=float(2.0 ** -22),
                scalar2=EPS, op0=OP.mult, op1=OP.max), waits=[(sT, wsT)], inc=sV)
            P.add("vector", lambda e: e.tensor_scalar(
                out=t_sc2[0:1, 1:2], in0=pm2[0][0:1, 0:1], scalar1=float(2.0 ** -23),
                scalar2=EPS * 0.5, op0=OP.mult, op1=OP.max), inc=sV)
            P.add("vector", lambda e: e.tensor_scalar(
                out=t_sc2[0:1, 2:3], in0=pm2[0][0:1, 0:1], scalar1=float(2.0 ** -22 / 127.0),
                scalar2=EPS / 127.0, op0=OP.mult, op1=OP.max), inc=sV)
            P.add("vector", lambda e: e.drain(), inc=sV)
            scv = sV.n
            th2_st = ST(STH, d_scd[0:1, 0:3], t_sc2[0:1, 0:3],
                        waits=[(sV, scv), (LTH, th_ld)])
            th2_ld = LD(LTH, t_th2[:, 0:2], d_scd[0:1, 1:3].broadcast_to((128, 2)),
                        waits=[(STH, th2_st)])
            thg = th2_ld
            w2q_v = {}
            sgn_f = 0
            for i in range(8):
                w = [(sV, w2q_v[i - 2])] if i >= 2 else [(sV, w2red[7])]
                rl = LD(LWI[i % 2], t_wp2[i % 2][:, :], d_w2[i, :, :], waits=w)
                P.add("scalar", (lambda i_: (lambda e: e.activation(
                    out=t_sgn2[:, :], in_=t_wp2[i_ % 2][:, :], func=AF.Sign)))(i),
                    waits=[(LWI[i % 2], rl), (sV, sgn_f)] + (barrier2 if i == 0 else []), inc=sS)
                P.add("scalar", (lambda i_: (lambda e: e.activation(
                    out=t_abs2[:, :], in_=t_wp2[i_ % 2][:, :], func=AF.Abs)))(i), inc=sS)
                ssn = sS.n
                P.add("vector", lambda e: e.tensor_scalar(
                    out=t_abs2[:, :], in0=t_abs2[:, :], scalar1=t_th2[:, 0:1],
                    scalar2=None, op0=OP.is_gt), waits=[(sS, ssn), (LTH, thg)], inc=sV)
                P.add("vector", (lambda i_: (lambda e: e.tensor_tensor(
                    out=t_w2q[i_][:, :], in0=t_abs2[:, :], in1=t_sgn2[:, :],
                    op=OP.mult)))(i), inc=sV)
                w2q_v[i] = sV.n
                sgn_f = sV.n
            w2q_done = sV.n

            prev_q2 = 0
            prev_sgn3 = 0
            c2_ldf = 0
            s2_ldf = 0
            s2B_used = 0
            prev_out_v = {0: 0, 1: 0}
            prev_out_st = {0: 0, 1: 0}
            prev_mm2 = 0
            prev_ff = {6: 0, 7: 0}
            for c in range(NBC):
                for hs in range(32):
                    w = []
                    if hs == 0:
                        w = [(sV, w2q_done if c == 0 else prev_q2)]
                    LD(LGL, t_gel[:, hs * 256:(hs + 1) * 256], d_gel[c, hs, :, :],
                       waits=w)
                gl_full = LGL.n
                mxv = {}
                for tg in range(2):
                    for grp in range(8):
                        bk = pt2[grp % 2]
                        w = [(LGL, gl_full)]
                        if (tg, grp) >= (0, 2):
                            pr = (tg, grp - 2) if grp >= 2 else (tg - 1, 6 + grp)
                            if pr in mxv:
                                w.append((sV, mxv[pr]))
                        if c > 0 and tg == 0 and grp < 2:
                            w.append((sV, prev_mxload))
                        last_t = None
                        for j in range(4):
                            hs = grp * 4 + j
                            last_t = P.add("tensor", (lambda bk_, hs_, j_, tg_: (lambda e:
                                e.transpose(bk_[:, j_ * 128:(j_ + 1) * 128],
                                            t_gel[:, hs_ * 256 + tg_ * 128: hs_ * 256 + (tg_ + 1) * 128],
                                            t_id[:, :])))(bk, hs, j, tg),
                                waits=w if j == 0 else [], inc=sT)
                        P.add("vector", (lambda bk_, tg_, grp_: (lambda e: e.tensor_reduce(
                            out=t_mx[:, tg_ * 8 + grp_: tg_ * 8 + grp_ + 1], in_=bk_[:, :],
                            axis=AX.X, op=OP.max, apply_absolute_value=True)))(bk, tg, grp),
                            waits=[(sT, last_t)], inc=sV)
                        mxv[(tg, grp)] = sV.n
                prev_mxload = sV.n
                P.add("vector", lambda e: e.drain(), inc=sV)
                for tg in range(2):
                    P.add("vector", (lambda tg_: (lambda e: e.tensor_reduce(
                        out=t_mx[:, 16 + tg_:17 + tg_], in_=t_mx[:, tg_ * 8:(tg_ + 1) * 8],
                        axis=AX.X, op=OP.max)))(tg), inc=sV)
                P.add("vector", lambda e: e.drain(), inc=sV)
                for tg in range(2):
                    P.add("vector", (lambda tg_: (lambda e: e.tensor_scalar(
                        out=t_mx[:, 16 + tg_:17 + tg_], in0=t_mx[:, 16 + tg_:17 + tg_],
                        scalar1=EPS, scalar2=None, op0=OP.max)))(tg), inc=sV)
                P.add("vector", lambda e: e.drain(), inc=sV)
                for tg in range(2):
                    P.add("vector", (lambda tg_: (lambda e: e.reciprocal(
                        out=t_mx[:, 18 + tg_:19 + tg_], in_=t_mx[:, 16 + tg_:17 + tg_])))(tg), inc=sV)
                P.add("vector", lambda e: e.drain(), inc=sV)
                P.add("vector", lambda e: e.tensor_scalar(
                    out=t_mx[:, 18:20], in0=t_mx[:, 18:20], scalar1=127.0, scalar2=None,
                    op0=OP.mult), inc=sV)
                P.add("vector", lambda e: e.tensor_scalar(
                    out=t_mx[:, 16:18], in0=t_mx[:, 16:18], scalar1=t_th2[:, 1:2],
                    scalar2=None, op0=OP.mult), inc=sV)
                P.add("vector", lambda e: e.drain(), inc=sV)
                c2v = sV.n
                P.add("tensor", lambda e: e.transpose(
                    pt2[0][0:2, 0:128], t_mx[:, 18:20], t_id[:, :]),
                    waits=[(sV, c2v)], inc=sT)
                P.add("tensor", lambda e: e.transpose(
                    pt2[0][0:2, 128:256], t_mx[:, 16:18], t_id[:, :]), inc=sT)
                c2t = sT.n
                P.add("vector", lambda e: e.tensor_copy(t_c2r[:, :], pt2[0][0:2, 0:128]),
                      waits=[(sT, c2t)], inc=sV)
                c2e = sV.n
                c2_st = ST(SC2, d_rw2, t_c2r[:, :], waits=[(sV, c2e), (LC2, c2_ldf)])
                LD(LC2, t_c2B[:, 0:128], d_rw2[0:1, :].broadcast_to((128, 128)),
                   waits=[(SC2, c2_st), (sV, prev_q2), (sT, prev_mm2)])
                c2_ldf = LD(LC2, t_c2B[:, 128:256], d_rw2[1:2, :].broadcast_to((128, 128)))
                P.add("vector", lambda e: e.tensor_copy(t_c2r[:, :], pt2[0][0:2, 128:256]),
                      waits=[(SC2, c2_st)], inc=sV)
                s2e = sV.n
                s2_st = ST(SS2, d_rw3, t_c2r[:, :], waits=[(sV, s2e), (LS2, s2_ldf)])
                LD(LS2, t_s2B[:, 0:128], d_rw3[0:1, :].broadcast_to((128, 128)),
                   waits=[(SS2, s2_st), (sV, s2B_used)])
                s2_ldf = LD(LS2, t_s2B[:, 128:256], d_rw3[1:2, :].broadcast_to((128, 128)))
                c2g = s2_ldf
                for hs in range(32):
                    w = [(LC2, c2_ldf), (LS2, s2_ldf)]
                    if hs == 0:
                        w.append((sV, prev_q2))
                        w.append((sT, prev_mm2))
                    P.add("vector", (lambda hs_: (lambda e: e.tensor_tensor(
                        out=t_qf2[:, :], in0=t_gel[:, hs_ * 256:(hs_ + 1) * 256],
                        in1=t_c2B[:, :], op=OP.mult)))(hs), waits=w, inc=sV)
                    P.add("vector", lambda e: e.tensor_copy(t_qi2[:, :], t_qf2[:, :]), inc=sV)
                    P.add("vector", lambda e: e.tensor_tensor(
                        out=t_sgn3[:, :], in0=t_qf2[:, :], in1=t_qi2[:, :],
                        op=OP.subtract), inc=sV)
                    P.add("vector", lambda e: e.scalar_tensor_tensor(
                        out=t_qf2[:, :], in0=t_sgn3[:, :], scalar=0.5, in1=t_qi2[:, :],
                        op0=OP.is_gt, op1=OP.add), inc=sV)
                    P.add("vector", lambda e: e.tensor_scalar(
                        out=t_sgn3[:, :], in0=t_sgn3[:, :], scalar1=-0.5, scalar2=None,
                        op0=OP.is_lt), inc=sV)
                    P.add("vector", lambda e: e.tensor_tensor(
                        out=t_qf2[:, :], in0=t_qf2[:, :], in1=t_sgn3[:, :],
                        op=OP.subtract), inc=sV)
                    P.add("vector", (lambda hs_: (lambda e: e.tensor_scalar(
                        out=t_xq2[:, hs_ * 256:(hs_ + 1) * 256], in0=t_qf2[:, :],
                        scalar1=127.0, scalar2=-128.0, op0=OP.min, op1=OP.max)))(hs), inc=sV)
                q2_done = sV.n
                prev_q2 = q2_done
                prev_sgn3 = sS.n
                ff_v = {}
                for m in range(8):
                    pg = m % 2
                    w = [(sV, q2_done)]
                    if m >= 2:
                        w.append((sV, ff_v[m - 2]))
                    else:
                        w.append((sV, prev_ff[6 + m]))
                    for k in range(32):
                        P.add("tensor", (lambda pg_, k_, m_: (lambda e: e.matmul(
                            pm2[pg_][:, 0:256], t_w2q[m_][:, k_ * 128:(k_ + 1) * 128],
                            t_xq2[:, k_ * 256:(k_ + 1) * 256],
                            start=(k_ == 0), stop=(k_ == 31))))(pg, k, m),
                            waits=w if k == 0 else [], inc=sT)
                    mmt = sT.n
                    P.add("vector", (lambda pg_, m_: (lambda e: e.tensor_tensor(
                        out=t_ff[:, m_ * 256:(m_ + 1) * 256], in0=pm2[pg_][:, 0:256],
                        in1=t_s2B[:, :], op=OP.mult)))(pg, m), waits=[(sT, mmt)], inc=sV)
                    s2B_used = sV.n
                    P.add("vector", (lambda m_: (lambda e: e.tensor_scalar(
                        out=t_ff[:, m_ * 256:(m_ + 1) * 256],
                        in0=t_ff[:, m_ * 256:(m_ + 1) * 256],
                        scalar1=t_b2[:, m_:m_ + 1], scalar2=None, op0=OP.add)))(m), inc=sV)
                    ff_v[m] = sV.n
                prev_mm2 = sT.n
                prev_ff = {6: ff_v[6], 7: ff_v[7]}
                ffn_done = sV.n
                for tg in range(2):
                    pg = tg % 2
                    ral = LD(LAC[pg], t_acr[pg][:, :],
                             d_acc[c * 256 + tg * 128: c * 256 + (tg + 1) * 128, :],
                             waits=[(sV, prev_out_v[tg]), (SOU[pg], prev_out_st[tg])])
                    bk = po2[pg]
                    last_t = None
                    for dsl in range(8):
                        w = []
                        if dsl == 0:
                            w = [(sV, ffn_done), (sV, prev_out_v[tg])]
                        last_t = P.add("tensor", (lambda bk_, d_, tg_: (lambda e: e.transpose(
                            bk_[:, d_ * 128:(d_ + 1) * 128],
                            t_ff[:, d_ * 256 + tg_ * 128: d_ * 256 + (tg_ + 1) * 128],
                            t_id[:, :])))(bk, dsl, tg), waits=w, inc=sT)
                    ow = [(sT, last_t), (LAC[pg], ral), (SOU[pg], prev_out_st[tg])]
                    P.add("vector", (lambda pg_, bk_: (lambda e: e.tensor_tensor(
                        out=t_op[pg_][:, :], in0=bk_[:, :], in1=t_acr[pg_][:, :],
                        op=OP.add)))(pg, bk), waits=ow, inc=sV)
                    prev_out_v[tg] = sV.n
                    prev_out_st[tg] = ST(
                        SOU[pg],
                        d_out[c * 256 + tg * 128: c * 256 + (tg + 1) * 128, :],
                        t_op[pg][:, :], waits=[(sV, prev_out_v[tg])])

        P.add("scalar", lambda e: e.nop(),
              waits=[(s, s.n) for s in ALL_DMA_SEMS if s.n > 0])

        block = OE(nc.Block())
        P.emit(block)

    return nc


def _host_layout(inputs):
    f = np.ascontiguousarray
    ip = inputs
    win = np.stack([ip["L0_attn_in_proj_w"][l].reshape(32, 128, 8, 128)
                    .transpose(0, 3, 2, 1).reshape(32, 128, 1024) for l in range(NLAY)])
    wout = np.stack([ip["L0_attn_out_w"][l].reshape(8, 128, 16, 128)
                     .transpose(2, 3, 0, 1).reshape(16, 128, 1024) for l in range(NLAY)])
    wdt = np.stack([ip["L0_attn_dt_w"][l].reshape(16, 128, 64)
                    .transpose(2, 0, 1).reshape(64, 2048) for l in range(NLAY)])
    wxp = np.stack([ip["L0_attn_x_proj_w"][l].T.reshape(16, 128, 96) for l in range(NLAY)])
    alog = np.stack([ip["L0_attn_A_log"][l].reshape(16, 128, 16)
                     .transpose(1, 0, 2).reshape(128, 256) for l in range(NLAY)])
    cw = np.stack([ip["L0_attn_conv_w"][l].reshape(16, 128, 4)
                   .transpose(1, 0, 2).reshape(128, 64) for l in range(NLAY)])
    cb = np.stack([ip["L0_attn_conv_b"][l].reshape(16, 128).T for l in range(NLAY)])
    dtb = np.stack([ip["L0_attn_dt_b"][l].reshape(16, 128).T for l in range(NLAY)])
    dp = np.stack([ip["L0_attn_Dp"][l].reshape(16, 128).T for l in range(NLAY)])
    nw = np.stack([ip["L0_attn_norm_w"][l].reshape(8, 128).T for l in range(NLAY)])
    w1 = ip["L0_ffn_W1"].reshape(32, 128, 8, 128).transpose(0, 3, 2, 1).reshape(32, 128, 1024)
    b1 = ip["L0_ffn_b1"].reshape(32, 128).T
    w2 = ip["L0_ffn_W2"].reshape(8, 128, 32, 128).transpose(0, 3, 2, 1).reshape(8, 128, 4096)
    b2 = ip["L0_ffn_b2"].reshape(8, 128).T
    com = {
        "w_in": f(win), "w_out": f(wout), "w_dt": f(wdt), "w_xp": f(wxp),
        "alog": f(alog), "cw": f(cw), "cb": f(cb), "dtb": f(dtb), "dp": f(dp),
        "nw": f(nw), "w1": f(w1), "b1": f(b1), "w2": f(w2), "b2": f(b2),
        "cst": np.tile(np.array([[1e-5, 1.0, 0.5, 0.0]], dtype=np.float32), (128, 1)),
        "ident": np.eye(128, dtype=np.float32),
        "identb": np.eye(128, dtype=ml_dtypes.bfloat16),
    }
    maps = []
    for b in range(N_CORES):
        m = dict(com)
        m["x"] = f(np.asarray(ip["L0_attn_x"][b]))
        maps.append(m)
    return maps


def kernel(**inputs):
    in_maps = _host_layout(inputs)
    nc = build_nc()
    res = run_bass_kernel_spmd(nc, in_maps, core_ids=list(range(N_CORES)))
    return np.stack([np.asarray(r["out"], dtype=np.float32) for r in res.results])


# version 17
# speedup vs baseline: 1.0417x; 1.0417x over previous
import numpy as np
import ml_dtypes
import concourse.bass as bass
import concourse.mybir as mybir
from concourse.alu_op_type import AluOpType as OP
from concourse.bass_utils import run_bass_kernel_spmd
from contextlib import ExitStack

F32 = mybir.dt.float32
BF16 = mybir.dt.bfloat16
I32 = mybir.dt.int32
AF = mybir.ActivationFunctionType
AX = mybir.AxisListType

B, L, DM, EI, NS, RK = 8, 4096, 1024, 2048, 16, 64
NLAY = 2
TC = 512
NCH = L // TC
TB = 256
NBC = L // TB
NSL = 16
EPS = 1e-5
BETA = 0.6
N_CORES = 8
SLAB_H = 8


class Sem:
    def __init__(self, h):
        self.h = h
        self.n = 0


class Prog:
    ENGS = ("sync", "vector", "scalar", "gpsimd", "tensor")

    def __init__(self):
        self.ops = {e: [] for e in self.ENGS}

    def add(self, eng, fn, waits=(), inc=None, amt=1):
        ws = [(s.h, int(v)) for (s, v) in waits if v and v > 0]
        post = None
        if inc is not None:
            inc.n += amt
            post = inc.n
        self.ops[eng].append((fn, ws, inc.h if inc is not None else None, amt))
        return post

    def emit(self, block):
        for eng in self.ENGS:
            ops = self.ops[eng]
            if not ops:
                continue

            def make(ops_):
                def run(e):
                    for fn, ws, inch, amt in ops_:
                        for (h, v) in ws:
                            e.wait_ge(h, v)
                        inst = fn(e) if fn is not None else None
                        if inch is not None:
                            if inst is None:
                                inst = e.nop()
                            inst.then_inc(inch, amt)
                return run
            getattr(block, eng)(make(ops))


def build_nc():
    nc = bass.Bass("TRN2", target_bir_lowering=False,
                   detect_race_conditions=False)

    d_x = nc.dram_tensor("x", [L, DM], F32, kind="ExternalInput").ap()
    d_win = nc.dram_tensor("w_in", [NLAY, 32, 128, 1024], F32, kind="ExternalInput").ap()
    d_wout = nc.dram_tensor("w_out", [NLAY, 16, 128, 1024], F32, kind="ExternalInput").ap()
    d_wdt = nc.dram_tensor("w_dt", [NLAY, 64, 2048], F32, kind="ExternalInput").ap()
    d_wxp = nc.dram_tensor("w_xp", [NLAY, 16, 128, 96], F32, kind="ExternalInput").ap()
    d_alog = nc.dram_tensor("alog", [NLAY, 128, 256], F32, kind="ExternalInput").ap()
    d_cw = nc.dram_tensor("cw", [NLAY, 128, 64], F32, kind="ExternalInput").ap()
    d_cb = nc.dram_tensor("cb", [NLAY, 128, 16], F32, kind="ExternalInput").ap()
    d_dtb = nc.dram_tensor("dtb", [NLAY, 128, 16], F32, kind="ExternalInput").ap()
    d_dp = nc.dram_tensor("dp", [NLAY, 128, 16], F32, kind="ExternalInput").ap()
    d_nw = nc.dram_tensor("nw", [NLAY, 128, 8], F32, kind="ExternalInput").ap()
    d_w1 = nc.dram_tensor("w1", [32, 128, 1024], F32, kind="ExternalInput").ap()
    d_b1 = nc.dram_tensor("b1", [128, 32], F32, kind="ExternalInput").ap()
    d_w2 = nc.dram_tensor("w2", [8, 128, 4096], F32, kind="ExternalInput").ap()
    d_b2 = nc.dram_tensor("b2", [128, 8], F32, kind="ExternalInput").ap()
    d_cst = nc.dram_tensor("cst", [128, 4], F32, kind="ExternalInput").ap()
    d_id = nc.dram_tensor("ident", [128, 128], F32, kind="ExternalInput").ap()
    d_idb = nc.dram_tensor("identb", [128, 128], BF16, kind="ExternalInput").ap()

    d_out = nc.dram_tensor("out", [L, DM], F32, kind="ExternalOutput").ap()

    d_szsp = nc.dram_tensor("sz_sp", [NLAY, NCH, NSL, 128, TC], F32, kind="Internal").ap()
    d_xbsp = nc.dram_tensor("xb_sp", [NLAY, NCH, NSL, 128, TC], F32, kind="Internal").ap()
    d_acc = nc.dram_tensor("acc_sp", [L, DM], F32, kind="Internal").ap()
    d_gel = nc.dram_tensor("gel_sp", [NBC, 32, 128, TB], F32, kind="Internal").ap()
    d_bc = nc.dram_tensor("bc_sp", [32, 512], F32, kind="Internal").ap()
    d_r2d = nc.dram_tensor("r2_sp", [1, 512], F32, kind="Internal").ap()
    d_rw1 = nc.dram_tensor("rw1_sp", [2, 128], F32, kind="Internal").ap()
    d_rw2 = nc.dram_tensor("rw2_sp", [2, 128], F32, kind="Internal").ap()
    d_rw3 = nc.dram_tensor("rw3_sp", [2, 128], F32, kind="Internal").ap()
    d_scd = nc.dram_tensor("sc_sp", [1, 4], F32, kind="Internal").ap()

    P = Prog()

    with ExitStack() as octx:
        OE = octx.enter_context
        sT = Sem(OE(nc.semaphore("sT")))
        sV = Sem(OE(nc.semaphore("sV")))
        sS = Sem(OE(nc.semaphore("sS")))
        sG = Sem(OE(nc.semaphore("sG")))
        LX = Sem(OE(nc.semaphore("LX")))
        LXP = [Sem(OE(nc.semaphore(f"LXP{i}"))) for i in range(2)]
        LWI = [Sem(OE(nc.semaphore(f"LWI{i}"))) for i in range(2)]
        LWO = [Sem(OE(nc.semaphore(f"LWO{i}"))) for i in range(2)]
        LBB = Sem(OE(nc.semaphore("LBB")))
        LCB = Sem(OE(nc.semaphore("LCB")))
        LR2 = Sem(OE(nc.semaphore("LR2")))
        LMW = Sem(OE(nc.semaphore("LMW")))
        LGL = Sem(OE(nc.semaphore("LGL")))
        LAC = [Sem(OE(nc.semaphore(f"LAC{i}"))) for i in range(2)]
        LTH = Sem(OE(nc.semaphore("LTH")))
        LS1 = Sem(OE(nc.semaphore("LS1")))
        LC2 = Sem(OE(nc.semaphore("LC2")))
        LS2 = Sem(OE(nc.semaphore("LS2")))
        SZ = [Sem(OE(nc.semaphore(f"SZ{i}"))) for i in range(2)]
        SXB = Sem(OE(nc.semaphore("SXB")))
        SBC = Sem(OE(nc.semaphore("SBC")))
        SR2 = Sem(OE(nc.semaphore("SR2")))
        SAC = Sem(OE(nc.semaphore("SAC")))
        SGP = [Sem(OE(nc.semaphore(f"SGP{i}"))) for i in range(2)]
        STH = Sem(OE(nc.semaphore("STH")))
        SS1 = Sem(OE(nc.semaphore("SS1")))
        SC2 = Sem(OE(nc.semaphore("SC2")))
        SS2 = Sem(OE(nc.semaphore("SS2")))
        SOU = [Sem(OE(nc.semaphore(f"SOU{i}"))) for i in range(2)]
        ALL_DMA_SEMS = ([LX, LBB, LCB, LR2, LMW, LGL, LTH, LS1, LC2, LS2,
                         SXB, SBC, SR2, SAC, STH, SS1, SC2, SS2]
                        + LXP + LWI + LWO + LAC + SZ + SGP + SOU)

        t_id = OE(nc.sbuf_tensor("t_id", [128, 128], F32))
        t_idb = OE(nc.sbuf_tensor("t_idb", [128, 128], BF16))
        t_ones = OE(nc.sbuf_tensor("t_ones", [128, 1], F32))
        t_cst = OE(nc.sbuf_tensor("t_cst", [128, 4], F32))

        def LD(sem, dst, src, waits=(), eng="sync"):
            return P.add(eng, (lambda d_, s_: (lambda e: e.dma_start(out=d_, in_=s_)))(dst, src),
                         waits=waits, inc=sem, amt=16)

        def ST(sem, dst, src, waits=()):
            return P.add("scalar", (lambda d_, s_: (lambda e: e.dma_start(out=d_, in_=s_)))(dst, src),
                         waits=waits, inc=sem, amt=16)

        LD(LMW, t_cst[:, :], d_cst)
        LD(LMW, t_id[:, :], d_id)
        LD(LMW, t_idb[:, :], d_idb)
        P.add("vector", lambda e: e.memset(t_ones[:, :], 1.0), inc=sV)

        with ExitStack() as actx:
            AE = actx.enter_context
            p = [AE(nc.psum_tensor(f"p{i}", [128, 512], F32)) for i in range(8)]
            t_dtw = AE(nc.sbuf_tensor("t_dtw", [64, 2048], F32))
            t_xpw = AE(nc.sbuf_tensor("t_xpw", [128, 16 * 96], F32))
            t_negA = [AE(nc.sbuf_tensor(f"t_negA{l}", [128, 256], F32)) for l in range(NLAY)]
            t_cw = [AE(nc.sbuf_tensor(f"t_cw{l}", [128, 64], F32)) for l in range(NLAY)]
            t_cb = [AE(nc.sbuf_tensor(f"t_cb{l}", [128, 16], F32)) for l in range(NLAY)]
            t_dtb = [AE(nc.sbuf_tensor(f"t_dtb{l}", [128, 16], F32)) for l in range(NLAY)]
            t_dp = [AE(nc.sbuf_tensor(f"t_dp{l}", [128, 16], F32)) for l in range(NLAY)]
            t_nw = [AE(nc.sbuf_tensor(f"t_nw{l}", [128, 8], F32)) for l in range(NLAY)]
            t_vst = [AE(nc.sbuf_tensor(f"t_vst{l}", [128, 256], F32)) for l in range(NLAY)]
            t_hst = [AE(nc.sbuf_tensor(f"t_hst{l}", [128, 256], F32)) for l in range(NLAY)]
            t_tail = [AE(nc.sbuf_tensor(f"t_tail{l}", [128, 48], F32)) for l in range(NLAY)]
            t_xt = AE(nc.sbuf_tensor("t_xt", [128, 4096], F32))
            t_base = AE(nc.sbuf_tensor("t_base", [128, 8 * 512], F32))
            t_xb = AE(nc.sbuf_tensor("t_xb", [128, 16 * 512], F32))
            t_dl = AE(nc.sbuf_tensor("t_dl", [128, 16 * 512], F32))
            t_vh = AE(nc.sbuf_tensor("t_vh", [128, 8 * 512], F32))
            t_y = AE(nc.sbuf_tensor("t_y", [128, 8 * 512], F32))
            t_tb0 = AE(nc.sbuf_tensor("t_tb0", [128, 512], F32))
            t_bpw = AE(nc.sbuf_tensor("t_bpw", [128, 512], F32))
            t_rhs = AE(nc.sbuf_tensor("t_rhs", [128, 2560], F32))
            RHS_OFF = [0, 512, 896, 1152]
            t_BT = AE(nc.sbuf_tensor("t_BT", [128, 64], F32))
            t_pr = AE(nc.sbuf_tensor("t_pr", [128, 1], F32))
            t_Cb = AE(nc.sbuf_tensor("t_Cb", [128, 1024], F32))
            t_dbc = AE(nc.sbuf_tensor("t_dbc", [128, 512], F32))
            t_mv = AE(nc.sbuf_tensor("t_mv", [128, 8], F32))
            t_bnst = AE(nc.sbuf_tensor("t_bnst", [128, 48], F32))
            t_r = AE(nc.sbuf_tensor("t_r", [128, 4], F32))
            t_r2 = AE(nc.sbuf_tensor("t_r2", [1, 512], F32))
            t_r2B = AE(nc.sbuf_tensor("t_r2B", [128, 512], F32))
            t_wi = [AE(nc.sbuf_tensor(f"t_wi{i}", [128, 1024], F32)) for i in range(2)]
            t_wo = [AE(nc.sbuf_tensor(f"t_wo{i}", [128, 1024], F32)) for i in range(2)]
            t_xc = [AE(nc.sbuf_tensor(f"t_xc{i}", [128, 512], F32)) for i in range(2)]
            t_sz = [AE(nc.sbuf_tensor(f"t_sz{i}", [128, 512], F32)) for i in range(2)]
            t_e = t_xc
            t_dA = [AE(nc.sbuf_tensor(f"t_dA{i}", [128, 512], F32)) for i in range(2)]
            t_vw = t_xt
            t_xbr = [t_wi[0][:, 0:512], t_wi[1][:, 0:512]]
            t_szr = [t_wi[0][:, 512:1024], t_wi[1][:, 512:1024]]
            t_y2 = t_sz

            for l in range(NLAY):
                LD(LMW, t_negA[l][:, :], d_alog[l, :, :])
                LD(LMW, t_cw[l][:, :], d_cw[l, :, :])
                LD(LMW, t_cb[l][:, :], d_cb[l, :, :])
                LD(LMW, t_dtb[l][:, :], d_dtb[l, :, :])
                LD(LMW, t_dp[l][:, :], d_dp[l, :, :])
                LD(LMW, t_nw[l][:, :], d_nw[l, :, :])
            ld_init = LMW.n
            for l in range(NLAY):
                P.add("scalar", (lambda l_: (lambda e: e.activation(
                    out=t_negA[l_][:, :], in_=t_negA[l_][:, :], func=AF.Exp)))(l),
                    waits=[(LMW, ld_init)], inc=sS)
                sA = sS.n
                P.add("vector", (lambda l_: (lambda e: e.tensor_scalar(
                    out=t_negA[l_][:, :], in0=t_negA[l_][:, :], scalar1=-1.0,
                    scalar2=None, op0=OP.mult)))(l), waits=[(sS, sA)], inc=sV)
                P.add("vector", (lambda l_: (lambda e: e.memset(t_vst[l_][:, :], 0.0)))(l), inc=sV)
                P.add("vector", (lambda l_: (lambda e: e.memset(t_hst[l_][:, :], 0.0)))(l), inc=sV)
                P.add("vector", (lambda l_: (lambda e: e.memset(t_tail[l_][:, :], 0.0)))(l), inc=sV)
            LNB = float(np.log(np.float64(BETA)))
            P.add("vector", lambda e: e.memset(t_bpw[:, :], 1.0), inc=sV)
            P.add("vector", lambda e: e.drain(), inc=sV)
            P.add("vector", lambda e: e.tensor_tensor_scan(
                out=t_tb0[:, :], data0=t_bpw[:, :], data1=t_bpw[:, :],
                initial=t_cst[:, 3:4], op0=OP.mult, op1=OP.add),
                waits=[(LMW, ld_init)], inc=sV)
            io_v = sV.n
            P.add("scalar", lambda e: e.activation(
                out=t_bpw[:, :], in_=t_tb0[:, :], func=AF.Exp, scale=LNB),
                waits=[(sV, io_v), (LMW, ld_init)], inc=sS)
            P.add("tensor", lambda e: e.transpose(
                p[7][:, 0:1], t_tb0[0:1, 0:128], t_id[0:1, 0:1]),
                waits=[(sV, io_v), (LMW, ld_init)], inc=sT)
            pr_t = sT.n
            P.add("scalar", lambda e: e.activation(
                out=t_pr[:, :], in_=p[7][:, 0:1], func=AF.Copy),
                waits=[(sT, pr_t)], inc=sS)
            pr_s = sS.n
            P.add("scalar", lambda e: e.drain(), inc=sS)
            P.add("vector", lambda e: e.tensor_scalar(
                out=t_tb0[:, :], in0=t_tb0[:, :], scalar1=t_pr[:, 0:1],
                scalar2=None, op0=OP.subtract), waits=[(sS, pr_s)], inc=sV)
            P.add("vector", lambda e: e.drain(), inc=sV)
            P.add("vector", lambda e: e.tensor_scalar(
                out=t_rhs[:, 0:512], in0=t_tb0[:, :], scalar1=0.0,
                scalar2=None, op0=OP.is_ge), inc=sV)
            msk_v = sV.n
            P.add("scalar", lambda e: e.activation(
                out=t_tb0[:, :], in_=t_tb0[:, :], func=AF.Exp, scale=LNB),
                waits=[(sV, msk_v)], inc=sS)
            tb_exp = sS.n
            P.add("vector", lambda e: e.tensor_tensor(
                out=t_tb0[:, :], in0=t_tb0[:, :], in1=t_rhs[:, 0:512],
                op=OP.mult), waits=[(sS, tb_exp)], inc=sV)
            P.add("vector", lambda e: e.drain(), inc=sV)
            P.add("vector", lambda e: e.memset(t_rhs[:, :], 0.0), inc=sV)

            acc_fence = sV.n
            s0tp_fence = 0
            wi_fence = [0, 0]
            wo_fence = [0, 0]
            xh_fence = 0
            xbst_full_prev = [0] * NLAY
            stz_prev = [[0, 0] for _ in range(NLAY)]
            prev_land = {s: 0 for s in range(SLAB_H)}
            prev_xv = {s: 0 for s in range(SLAB_H)}
            prev_hsc = {s: 0 for s in range(SLAB_H)}
            prev_vmm_f = 0
            prev_ep = {0: 0, 1: 0}
            dtw_fence = 0
            xpw_fence = 0
            r2_ld = 0
            r2B_used = 0
            bc_ld = 0
            bb_ld = 0
            prev_opT = 0
            accst_fence = 0
            base_done = sV.n

            for c in range(NCH):
                for g in range(4):
                    LD(LX, t_xt[:, g * 1024:(g + 1) * 1024],
                       d_x[c * 512 + g * 128: c * 512 + (g + 1) * 128, :],
                       waits=[(sT, s0tp_fence), (sV, acc_fence),
                              (sT, prev_opT)] if g == 0 else [])
                xld_full = LX.n
                ln_v = []
                for g in range(4):
                    P.add("vector", (lambda a, g_: (lambda e: e.bn_stats(
                        t_bnst[:, 12 * g_:12 * g_ + 6], a)))(
                        t_xt[:, g * 1024: g * 1024 + 512], g),
                        waits=[(LX, xld_full)] if g == 0 else [], inc=sV)
                    P.add("vector", (lambda a, g_: (lambda e: e.bn_stats(
                        t_bnst[:, 12 * g_ + 6:12 * g_ + 12], a)))(
                        t_xt[:, g * 1024 + 512: (g + 1) * 1024], g), inc=sV)
                vstats = sV.n
                for g in range(4):
                    P.add("vector", (lambda g_: (lambda e: e.bn_aggr(
                        t_mv[:, 2 * g_:2 * g_ + 2], t_bnst[:, 12 * g_:12 * g_ + 12])))(g),
                        waits=[(sV, vstats)], inc=sV)
                vg = sV.n
                for g in range(4):
                    P.add("scalar", (lambda g_: (lambda e: e.activation(
                        out=t_r[:, g_:g_ + 1], in_=t_mv[:, 2 * g_ + 1:2 * g_ + 2],
                        func=AF.Ln, bias=t_cst[:, 0:1], scale=1.0)))(g),
                        waits=[(sV, vg)] if g == 0 else [], inc=sS)
                P.add("scalar", lambda e: e.drain(), inc=sS)
                for g in range(4):
                    P.add("scalar", (lambda g_: (lambda e: e.activation(
                        out=t_r[:, g_:g_ + 1], in_=t_r[:, g_:g_ + 1],
                        func=AF.Exp, scale=-0.5)))(g), inc=sS)
                P.add("scalar", lambda e: e.drain(), inc=sS)
                sg = sS.n
                for g in range(4):
                    P.add("vector", (lambda a, g_: (lambda e: e.tensor_scalar(
                        out=a, in0=a, scalar1=t_mv[:, 2 * g_:2 * g_ + 1],
                        scalar2=t_r[:, g_:g_ + 1], op0=OP.subtract, op1=OP.mult)))(
                        t_xt[:, g * 1024:(g + 1) * 1024], g),
                        waits=[(sS, sg)] if g == 0 else [], inc=sV)
                    ln_v.append(sV.n)
                ln_done = sV.n
                ev_f = {}
                for dsl in range(8):
                    bank = p[dsl % 2]
                    w0 = [(sV, ln_done), (sV, acc_fence)] if dsl == 0 else []
                    if dsl >= 2:
                        w0 = [(sV, ev_f[dsl - 2])]
                    last_t = None
                    for g in range(4):
                        last_t = P.add("tensor", (lambda bk, g_, d_: (lambda e: e.transpose(
                            bk[:, g_ * 128:(g_ + 1) * 128],
                            t_xt[:, g_ * 1024 + d_ * 128: g_ * 1024 + (d_ + 1) * 128],
                            t_id[:, :])))(bank, g, dsl),
                            waits=w0 if g == 0 else [], inc=sT)
                    P.add("vector", (lambda bk, d_: (lambda e: e.tensor_copy(
                        t_base[:, d_ * 512:(d_ + 1) * 512], bk[:, :])))(bank, dsl),
                        waits=[(sT, last_t), (SAC, accst_fence)], inc=sV)
                    ev_f[dsl] = sV.n
                s0tp_fence = sT.n
                base_done = sV.n

                for l in range(NLAY):
                    LD(LMW, t_dtw[:, :], d_wdt[l, :, :],
                       waits=[(sT, dtw_fence), (sT, xpw_fence)])
                    for k in range(16):
                        LD(LMW, t_xpw[:, k * 96:(k + 1) * 96], d_wxp[l, k, :, :])
                    mw_full = LMW.n
                    for dsl in range(8):
                        P.add("scalar", (lambda d_, i_: (lambda e: e.activation(
                            out=t_e[i_][:, :], in_=t_base[:, d_ * 512:(d_ + 1) * 512],
                            func=AF.Square)))(dsl, dsl % 2),
                            waits=[(sV, base_done)] if dsl < 2 else [(sT, sT.n)], inc=sS)
                        sq = sS.n
                        P.add("tensor", (lambda d_, i_: (lambda e: e.matmul(
                            p[2][0:1, :], t_ones[:, 0:1], t_e[i_][:, :],
                            start=(d_ == 0), stop=(d_ == 7))))(dsl, dsl % 2),
                            waits=[(sS, sq)], inc=sT)
                    rms_mm = sT.n
                    P.add("scalar", lambda e: e.activation(
                        out=t_r2[0:1, :], in_=p[2][0:1, :], func=AF.Ln,
                        bias=t_cst[0:1, 0:1], scale=1.0 / DM), waits=[(sT, rms_mm)], inc=sS)
                    P.add("scalar", lambda e: e.drain(), inc=sS)
                    P.add("scalar", lambda e: e.activation(
                        out=t_r2[0:1, :], in_=t_r2[0:1, :], func=AF.Exp, scale=-0.5), inc=sS)
                    rms_s = sS.n
                    r2st = ST(SR2, d_r2d, t_r2[0:1, :], waits=[(sS, rms_s), (LR2, r2_ld)])
                    r2_ld = LD(LR2, t_r2B[:, :], d_r2d[0:1, :].broadcast_to((128, 512)),
                               waits=[(SR2, r2st), (sV, r2B_used)])
                    for dsl in range(8):
                        P.add("vector", (lambda d_: (lambda e: e.tensor_tensor(
                            out=t_xt[:, d_ * 512:(d_ + 1) * 512],
                            in0=t_base[:, d_ * 512:(d_ + 1) * 512],
                            in1=t_r2B[:, :], op=OP.mult)))(dsl),
                            waits=[(LR2, r2_ld)] if dsl == 0 else [], inc=sV)
                        P.add("vector", (lambda d_, l_: (lambda e: e.tensor_scalar(
                            out=t_xt[:, d_ * 512:(d_ + 1) * 512],
                            in0=t_xt[:, d_ * 512:(d_ + 1) * 512],
                            scalar1=t_nw[l_][:, d_:d_ + 1], scalar2=None,
                            op0=OP.mult)))(dsl, l), inc=sV)
                    xnorm_done = sV.n
                    r2B_used = xnorm_done

                    consV = {}
                    consS = {}
                    wiT = {}
                    st_z = {}
                    st_xb = {}
                    for m in range(32):
                        pg = m % 2
                        lw = [(sT, wiT[m - 2])] if m >= 2 else \
                            [(sT, wi_fence[pg]), (sV, prev_ep[pg])]
                        wl = LD(LWI[pg], t_wi[pg][:, :], d_win[l, m, :, :], waits=lw)
                        bw = [(LWI[pg], wl)]
                        if m == 0:
                            bw.append((sV, xnorm_done))
                        if (m - 2) in consV:
                            bw.append((sV, consV[m - 2]))
                        if (m - 2) in consS:
                            bw.append((sS, consS[m - 2]))
                        for k in range(8):
                            P.add("tensor", (lambda pg_, k_: (lambda e: e.matmul(
                                p[pg_][:, :], t_wi[pg_][:, k_ * 128:(k_ + 1) * 128],
                                t_xt[:, k_ * 512:(k_ + 1) * 512],
                                start=(k_ == 0), stop=(k_ == 7))))(pg, k),
                                waits=bw if k == 0 else [], inc=sT)
                        wiT[m] = sT.n
                        wi_fence[pg] = sT.n
                        mm_done = sT.n
                        if m < 16:
                            slab = m
                            P.add("vector", (lambda pg_, s_, l_: (lambda e: e.tensor_scalar(
                                out=t_xc[pg_][:, :], in0=p[pg_][:, :],
                                scalar1=t_cw[l_][:, s_ * 4 + 3:s_ * 4 + 4], scalar2=None,
                                op0=OP.mult)))(pg, slab, l),
                                waits=[(sT, mm_done)], inc=sV)
                            for kk in (2, 1, 0):
                                sh = 3 - kk
                                P.add("vector", (lambda pg_, s_, l_, kk_, sh_: (lambda e:
                                    e.scalar_tensor_tensor(
                                        out=t_xc[pg_][:, sh_:512],
                                        in0=p[pg_][:, 0:512 - sh_],
                                        scalar=t_cw[l_][:, s_ * 4 + kk_:s_ * 4 + kk_ + 1],
                                        in1=t_xc[pg_][:, sh_:512],
                                        op0=OP.mult, op1=OP.add)))(pg, slab, l, kk, sh),
                                    inc=sV)
                                P.add("vector", (lambda pg_, s_, l_, kk_, sh_: (lambda e:
                                    e.scalar_tensor_tensor(
                                        out=t_xc[pg_][:, 0:sh_],
                                        in0=t_tail[l_][:, s_ * 3 + kk_:s_ * 3 + 3],
                                        scalar=t_cw[l_][:, s_ * 4 + kk_:s_ * 4 + kk_ + 1],
                                        in1=t_xc[pg_][:, 0:sh_],
                                        op0=OP.mult, op1=OP.add)))(pg, slab, l, kk, sh),
                                    inc=sV)
                            P.add("vector", (lambda pg_, s_, l_: (lambda e: e.tensor_copy(
                                t_tail[l_][:, s_ * 3:s_ * 3 + 3], p[pg_][:, 509:512])))(pg, slab, l),
                                inc=sV)
                            consV[m] = sV.n
                            conv_v = sV.n
                            P.add("scalar", (lambda pg_, s_, l_: (lambda e: e.activation(
                                out=t_xb[:, s_ * 512:(s_ + 1) * 512], in_=t_xc[pg_][:, :],
                                func=AF.Silu, bias=t_cb[l_][:, s_:s_ + 1], scale=1.0)))(pg, slab, l),
                                waits=[(sV, conv_v), (SXB, xbst_full_prev[l])], inc=sS)
                            consS[m] = sS.n
                            st_xb[slab] = ST(SXB, d_xbsp[l, c, slab, :, :],
                                             t_xb[:, slab * 512:(slab + 1) * 512],
                                             waits=[(sS, sS.n)])
                        else:
                            slab = m - 16
                            zw = [(sT, mm_done)]
                            zw.append((SZ[pg], st_z[slab - 2] if slab >= 2
                                       else stz_prev[l][slab % 2]))
                            P.add("scalar", (lambda pg_: (lambda e: e.activation(
                                out=t_sz[pg_][:, :], in_=p[pg_][:, :], func=AF.Silu)))(pg),
                                waits=zw, inc=sS)
                            consS[m] = sS.n
                            st_z[slab] = ST(SZ[pg], d_szsp[l, c, slab, :, :],
                                            t_sz[pg][:, :], waits=[(sS, sS.n)])
                            stz_prev[l][slab % 2] = st_z[slab]

                    xb_ok = consS[15]
                    xb_full = SXB.n
                    xbst_full_prev[l] = xb_full
                    for k in range(16):
                        P.add("tensor", (lambda k_, l_: (lambda e: e.matmul(
                            p[2][0:96, :], t_xpw[:, k_ * 96:(k_ + 1) * 96],
                            t_xb[:, k_ * 512:(k_ + 1) * 512],
                            start=(k_ == 0), stop=(k_ == 15))))(k, l),
                            waits=[(sS, xb_ok), (LMW, mw_full)] if k == 0 else [], inc=sT)
                    dbc_mm = sT.n
                    xpw_fence = dbc_mm
                    P.add("scalar", lambda e: e.activation(
                        out=t_dbc[0:96, :], in_=p[2][0:96, :], func=AF.Copy),
                        waits=[(sT, dbc_mm), (sG, sG.n)], inc=sS)
                    dbc_ev = sS.n
                    bc_st = ST(SBC, d_bc[16:32, :], t_dbc[80:96, :],
                               waits=[(sS, dbc_ev), (LCB, bc_ld)])
                    for uu in range(4):
                        P.add("tensor", (lambda uu_: (lambda e: e.transpose(
                            p[7][:, uu_ * 16:(uu_ + 1) * 16],
                            t_dbc[64:80, uu_ * 128:(uu_ + 1) * 128],
                            t_id[64:80, 64:80])))(uu),
                            waits=[(sS, dbc_ev)] if uu == 0 else [], inc=sT)
                    btp = sT.n
                    P.add("scalar", lambda e: e.activation(
                        out=t_BT[:, :], in_=p[7][:, 0:64], func=AF.Copy),
                        waits=[(sT, btp)], inc=sS)
                    btl = sS.n
                    dt_s = {}
                    for m in range(16):
                        pg = m % 2
                        w = [(sS, dbc_ev)]
                        if m >= 2:
                            w.append((sS, dt_s[m - 2]))
                        P.add("tensor", (lambda pg_, m_, l_: (lambda e: e.matmul(
                            p[pg_][:, :], t_dtw[:, m_ * 128:(m_ + 1) * 128],
                            t_dbc[0:64, :], start=True, stop=True)))(pg, m, l),
                            waits=w + ([(LMW, mw_full)] if m == 0 else []), inc=sT)
                        mm_t = sT.n
                        dtw_fence = mm_t
                        P.add("scalar", (lambda pg_, m_, l_: (lambda e: e.activation(
                            out=t_e[pg_][:, :], in_=p[pg_][:, :], func=AF.Exp,
                            bias=t_dtb[l_][:, m_:m_ + 1], scale=1.0)))(pg, m, l),
                            waits=[(sT, mm_t)], inc=sS)
                        P.add("scalar", lambda e: e.drain(), inc=sS)
                        P.add("scalar", (lambda pg_, m_: (lambda e: e.activation(
                            out=t_dl[:, m_ * 512:(m_ + 1) * 512], in_=t_e[pg_][:, :],
                            func=AF.Ln, bias=t_cst[:, 1:2], scale=1.0)))(pg, m), inc=sS)
                        dt_s[m] = sS.n
                        P.add("vector", (lambda m_: (lambda e: e.tensor_tensor(
                            out=t_xb[:, m_ * 512:(m_ + 1) * 512],
                            in0=t_dl[:, m_ * 512:(m_ + 1) * 512],
                            in1=t_xb[:, m_ * 512:(m_ + 1) * 512], op=OP.mult)))(m),
                            waits=[(sS, dt_s[m]), (SXB, xb_full)], inc=sV)
                    w_done = sV.n

                    wtl = {}
                    rdl = {}
                    for r in range(4):
                        b0 = (r % 2) * 4
                        tps = {}
                        for i in range(4):
                            s = 4 * r + i
                            tw = []
                            if r == 0 and i == 0:
                                tw.append((sV, w_done))
                            if r == 1 and i == 3:
                                tw.append((sS, btl))
                            if r >= 2:
                                tw.append((sS, rdl[r - 2]))
                            for uu in range(4):
                                P.add("tensor", (lambda s_, uu_, b_: (lambda e: e.transpose(
                                    p[b_][:, uu_ * 128:(uu_ + 1) * 128],
                                    t_xb[:, s_ * 512 + uu_ * 128: s_ * 512 + (uu_ + 1) * 128],
                                    t_id[:, :])))(s, uu, b0 + i),
                                    waits=tw if uu == 0 else [], inc=sT)
                            tps[i] = sT.n
                        for i in range(4):
                            s = 4 * r + i
                            P.add("scalar", (lambda s_, b_: (lambda e: e.activation(
                                out=t_xb[:, s_ * 512:(s_ + 1) * 512], in_=p[b_][:, :],
                                func=AF.Copy)))(s, b0 + i),
                                waits=[(sT, tps[i])], inc=sS)
                            wtl[s] = sS.n
                        rdl[r] = sS.n

                    vmm_last = [prev_vmm_f]
                    vml = [prev_vmm_f, prev_vmm_f]
                    epd_v = {}
                    for hf in range(2):
                        s0 = hf * SLAB_H
                        hsc = {}
                        dA_c = {}
                        xv_c = {}
                        hc_c = {}
                        xh_n = {}
                        cb_n = {}
                        vmm = {}
                        land_c = {}
                        rhs_c = {}

                        def load_cb(n, hf=hf, cb_n=cb_n, hc_c=hc_c):
                            par = n % 2
                            gw = [(SBC, bc_st)]
                            if n >= 2:
                                gw.append((sV, hc_c[n - 2]))
                            elif hf == 1:
                                gw.append((sV, prev_half_hc))
                            cb_n[n] = LD(LCB, t_Cb[:, par * 512:(par + 1) * 512],
                                         d_bc[16 + n:17 + n, :].broadcast_to((128, 512)),
                                         waits=gw)

                        def emit_rhs_uu(n, uu, rhs_c=rhs_c):
                            w0 = [(sT, vml[n % 2])] if uu == 0 else []
                            pb = (n % 2) * 1280
                            P.add("scalar", (lambda n_, uu_, pb_: (lambda e: e.activation(
                                out=t_rhs[:, pb_ + RHS_OFF[uu_]:
                                          pb_ + RHS_OFF[uu_] + 512 - 128 * uu_],
                                in_=t_tb0[:, 0:512 - 128 * uu_], func=AF.Copy,
                                scale=t_BT[:, uu_ * 16 + n_:uu_ * 16 + n_ + 1])))(n, uu, pb),
                                waits=w0, inc=sS)
                            rhs_c[n] = sS.n

                        def emit_rhs(n):
                            for uu in range(4):
                                emit_rhs_uu(n, uu)

                        def emit_vmm(n, hf=hf, s0=s0, vmm=vmm, land_c=land_c,
                                     rhs_c=rhs_c):
                            pb = (n % 2) * 1280
                            for s in range(SLAB_H):
                                slab = s0 + s
                                for uu in range(4):
                                    mw = []
                                    if uu == 0:
                                        mw.append((sS, rhs_c[n]))
                                        mw.append((sV, land_c[(n - 1, s)] if n > 0
                                                   else prev_land[s]))
                                    P.add("tensor", (lambda s_, slab_, uu_, pb_: (lambda e: e.matmul(
                                        p[s_][:, 128 * uu_:512],
                                        t_xb[:, slab_ * 512 + uu_ * 128: slab_ * 512 + (uu_ + 1) * 128],
                                        t_rhs[:, pb_ + RHS_OFF[uu_]:
                                              pb_ + RHS_OFF[uu_] + 512 - 128 * uu_],
                                        start=(uu_ == 0), stop=(uu_ == 3))))(s, slab, uu, pb),
                                        waits=mw, inc=sT)
                                vmm[(n, s)] = sT.n
                            vml[n % 2] = sT.n
                            vmm_last[0] = sT.n

                        def emit_land(n, hf=hf, s0=s0, l=l, vmm=vmm,
                                      land_c=land_c, xv_c=xv_c):
                            for s in range(SLAB_H):
                                slab = s0 + s
                                lw = [(sT, vmm[(n, s)])]
                                if n > 0:
                                    lw.append((sS, xv_c[(n - 1, s)]))
                                else:
                                    lw.append((sS, prev_xv[s]))
                                P.add("vector", (lambda s_, slab_, l_, n_: (lambda e:
                                    e.scalar_tensor_tensor(
                                        out=t_vw[:, s_ * 512:(s_ + 1) * 512],
                                        in0=t_bpw[:, :],
                                        scalar=t_vst[l_][:, slab_ * 16 + n_:slab_ * 16 + n_ + 1],
                                        in1=p[s_][:, :], op0=OP.mult,
                                        op1=OP.add)))(s, slab, l, n),
                                    waits=lw, inc=sV)
                                land_c[(n, s)] = sV.n
                                P.add("scalar", (lambda s_, slab_, l_, n_: (lambda e: e.activation(
                                    out=t_vst[l_][:, slab_ * 16 + n_:slab_ * 16 + n_ + 1],
                                    in_=t_vw[:, s_ * 512 + 511:s_ * 512 + 512],
                                    func=AF.Copy)))(s, slab, l, n),
                                    waits=[(sV, land_c[(n, s)])], inc=sS)
                                xv_c[(n, s)] = sS.n

                        load_cb(0)
                        load_cb(1)
                        emit_rhs(0)
                        emit_vmm(0)
                        emit_land(0)
                        for n in range(16):
                            par = n % 2
                            if n + 1 < 16:
                                emit_rhs(n + 1)
                                emit_vmm(n + 1)
                            def emit_dA(n_, s_, hf=hf, s0=s0, l=l,
                                        hsc=hsc, dA_c=dA_c):
                                slab_ = s0 + s_
                                sw = []
                                if s_ >= 2:
                                    sw.append((sV, hsc[(n_, s_ - 2)]))
                                elif n_ > 0:
                                    sw.append((sV, hsc[(n_ - 1, SLAB_H - 2 + s_)]))
                                else:
                                    sw.append((sV, prev_hsc[SLAB_H - 2 + s_]))
                                P.add("scalar", (lambda sx, sl, lx, nx: (lambda e: e.activation(
                                    out=t_dA[sx % 2][:, :],
                                    in_=t_dl[:, sl * 512:(sl + 1) * 512],
                                    func=AF.Exp,
                                    scale=t_negA[lx][:, sl * 16 + nx:sl * 16 + nx + 1])))(s_, slab_, l, n_),
                                    waits=sw, inc=sS)
                                dA_c[(n_, s_)] = sS.n

                            for s in range(SLAB_H):
                                slab = s0 + s
                                if n == 0 or s >= 2:
                                    emit_dA(n, s)
                                P.add("vector", (lambda s_, slab_, l_, n_: (lambda e: e.tensor_tensor_scan(
                                    out=t_vh[:, s_ * 512:(s_ + 1) * 512],
                                    data0=t_dA[s_ % 2][:, :],
                                    data1=t_vw[:, s_ * 512:(s_ + 1) * 512],
                                    initial=t_hst[l_][:, slab_ * 16 + n_:slab_ * 16 + n_ + 1],
                                    op0=OP.mult, op1=OP.add)))(s, slab, l, n),
                                    waits=[(sS, dA_c[(n, s)])], inc=sV)
                                hsc[(n, s)] = sV.n
                            if n + 1 < 16:
                                emit_dA(n + 1, 0)
                            P.add("scalar", (lambda l_, n_, s0_: (lambda e: e.activation(
                                out=t_hst[l_][:, s0_ * 16 + n_: s0_ * 16 + n_ + 113: 16],
                                in_=t_vh[:, 511::512], func=AF.Copy)))(l, n, s0),
                                waits=[(sV, hsc[(n, SLAB_H - 1)])], inc=sS)
                            xh_n[n] = sS.n
                            if n + 1 < 16:
                                emit_dA(n + 1, 1)
                            vw = [(LCB, cb_n[n]), (sS, xh_n[n])]
                            if n == 0:
                                P.add("vector", (lambda par_: (lambda e: e.tensor_tensor(
                                    out=t_y[:, :].rearrange("p (r f) -> p r f", r=SLAB_H),
                                    in0=t_vh[:, :].rearrange("p (r f) -> p r f", r=SLAB_H),
                                    in1=t_Cb[:, par_ * 512:(par_ + 1) * 512].unsqueeze(1)
                                    .broadcast_to((128, SLAB_H, 512)),
                                    op=OP.mult)))(par),
                                    waits=vw, inc=sV)
                                hc_c[n] = sV.n
                                yac_w = sV.n
                            else:
                                P.add("vector", (lambda par_: (lambda e: e.tensor_tensor(
                                    out=t_vh[:, :].rearrange("p (r f) -> p r f", r=SLAB_H),
                                    in0=t_vh[:, :].rearrange("p (r f) -> p r f", r=SLAB_H),
                                    in1=t_Cb[:, par_ * 512:(par_ + 1) * 512].unsqueeze(1)
                                    .broadcast_to((128, SLAB_H, 512)),
                                    op=OP.mult)))(par),
                                    waits=vw, inc=sV)
                                hc_c[n] = sV.n
                                P.add("vector", lambda e: e.tensor_tensor(
                                    out=t_y[:, :], in0=t_y[:, :],
                                    in1=t_vh[:, :], op=OP.add), inc=sV)
                                yac_w = sV.n
                            if n + 1 < 16:
                                emit_land(n + 1)
                            if n + 2 < 16:
                                load_cb(n + 2)
                        prev_half_hc = hc_c[15]
                        prev_land = {s: land_c[(15, s)] for s in range(SLAB_H)}
                        prev_xv = {s: xv_c[(15, s)] for s in range(SLAB_H)}
                        prev_hsc = {s: hsc[(15, s)] for s in range(SLAB_H)}
                        xh_fence = sS.n

                        ep_v = {}
                        for s in range(SLAB_H):
                            slab = s0 + s
                            pg = s % 2
                            w = [(sV, ep_v[s - 2] if s >= 2 else prev_ep[s % 2]),
                                 (sT, wi_fence[pg]), (SXB, xb_full),
                                 (SZ[pg], st_z[14 + pg])]
                            rx = LD(LWI[pg], t_xbr[pg], d_xbsp[l, c, slab, :, :],
                                    waits=w)
                            rz = LD(LWI[pg], t_szr[pg], d_szsp[l, c, slab, :, :])
                            y2w = [(LWI[pg], rz)]
                            if s == 0:
                                y2w.append((sV, yac_w))
                                y2w.append((sS, xh_n[15]))
                            P.add("vector", (lambda pg_, s_, l_, slab_: (lambda e:
                                e.scalar_tensor_tensor(
                                    out=t_y2[pg_][:, :], in0=t_xbr[pg_],
                                    scalar=t_dp[l_][:, slab_:slab_ + 1],
                                    in1=t_y[:, s_ * 512:(s_ + 1) * 512],
                                    op0=OP.mult, op1=OP.add)))(pg, s, l, slab),
                                waits=y2w, inc=sV)
                            P.add("vector", (lambda pg_, slab_: (lambda e: e.tensor_tensor(
                                out=t_dl[:, slab_ * 512:(slab_ + 1) * 512],
                                in0=t_y2[pg_][:, :], in1=t_szr[pg_],
                                op=OP.mult)))(pg, slab), inc=sV)
                            ep_v[s] = sV.n
                            epd_v[slab] = sV.n
                        prev_ep = {0: ep_v[6], 1: ep_v[7]}
                    prev_vmm_f = vmm_last[0]

                    ow_T_l = {}
                    for slab in range(16):
                        pg = slab % 2
                        oww = [(sT, ow_T_l[slab - 2])] if slab >= 2 else \
                            [(sT, wo_fence[pg])]
                        owl = LD(LWO[pg], t_wo[pg][:, :], d_wout[l, slab, :, :],
                                 waits=oww)
                        mw = [(LWO[pg], owl), (sV, epd_v[slab])]
                        if slab == 0:
                            mw.append((sV, prev_land[SLAB_H - 1]))
                        for m in range(8):
                            P.add("tensor", (lambda pg_, m_, slab_: (lambda e: e.matmul(
                                p[m_][:, :], t_wo[pg_][:, m_ * 128:(m_ + 1) * 128],
                                t_dl[:, slab_ * 512:(slab_ + 1) * 512],
                                start=(slab_ == 0), stop=(slab_ == 15))))(pg, m, slab),
                                waits=mw if m == 0 else [], inc=sT)
                        ow_T_l[slab] = sT.n
                        wo_fence[pg] = sT.n
                    op_T = sT.n
                    for m in range(8):
                        P.add("vector", (lambda m_: (lambda e: e.tensor_tensor(
                            out=t_base[:, m_ * 512:(m_ + 1) * 512],
                            in0=t_base[:, m_ * 512:(m_ + 1) * 512],
                            in1=p[m_][:, :], op=OP.add)))(m),
                            waits=[(sT, op_T)] if m == 0 else [], inc=sV)
                    base_done = sV.n

                last_t = None
                for tg in range(4):
                    for dsl in range(8):
                        bk = p[2 * tg + dsl // 4]
                        last_t = P.add("tensor", (lambda bk_, d_, tg_: (lambda e: e.transpose(
                            bk_[:, (d_ % 4) * 128:((d_ % 4) + 1) * 128],
                            t_base[:, d_ * 512 + tg_ * 128: d_ * 512 + (tg_ + 1) * 128],
                            t_id[:, :])))(bk, dsl, tg),
                            waits=[(sV, base_done)] if (tg == 0 and dsl == 0) else [],
                            inc=sT)
                acc_v = {}
                acc_st = {}
                for tg in range(4):
                    pg = tg % 2
                    xw = [(sT, op_T if tg < 2 else last_t)]
                    if tg >= 2:
                        xw.append((sV, acc_v[tg - 2]))
                    rx2 = LD(LXP[pg], t_xt[:, pg * 1024:(pg + 1) * 1024],
                             d_x[c * 512 + tg * 128: c * 512 + (tg + 1) * 128, :],
                             waits=xw)
                    aw = [(sT, last_t), (LXP[pg], rx2), (SAC, accst_fence)]
                    P.add("vector", (lambda pg_, tg_: (lambda e: e.tensor_tensor(
                        out=t_base[:, tg_ * 1024: tg_ * 1024 + 512],
                        in0=p[2 * tg_][:, :],
                        in1=t_xt[:, pg_ * 1024: pg_ * 1024 + 512], op=OP.add)))(pg, tg),
                        waits=aw, inc=sV)
                    P.add("vector", (lambda pg_, tg_: (lambda e: e.tensor_tensor(
                        out=t_base[:, tg_ * 1024 + 512: (tg_ + 1) * 1024],
                        in0=p[2 * tg_ + 1][:, :],
                        in1=t_xt[:, pg_ * 1024 + 512: (pg_ + 1) * 1024], op=OP.add)))(pg, tg),
                        inc=sV)
                    acc_v[tg] = sV.n
                    acc_st[tg] = ST(SAC, d_acc[c * 512 + tg * 128: c * 512 + (tg + 1) * 128, :],
                                    t_base[:, tg * 1024:(tg + 1) * 1024],
                                    waits=[(sV, acc_v[tg])])
                acc_fence = acc_v[3]
                accst_fence = acc_st[3]
                prev_opT = op_T

            A_V, A_S, A_T, A_G = sV.n, sS.n, sT.n, sG.n

        barrier = ([(sV, A_V), (sS, A_S), (sT, A_T), (sG, A_G)]
                   + [(s, s.n) for s in ALL_DMA_SEMS if s.n > 0])

        with ExitStack() as bctx:
            BE = bctx.enter_context
            q0 = [BE(nc.psum_tensor(f"q0{i}", [128, 512], F32)) for i in range(2)]
            qs = BE(nc.psum_tensor("qs", [128, 512], F32))
            pQ = [BE(nc.psum_tensor(f"pQ{i}", [128, 1024], BF16)) for i in range(2)]
            t_w1q = [BE(nc.sbuf_tensor(f"t_w1q{i}", [128, 1024], BF16)) for i in range(32)]
            t_wp = [BE(nc.sbuf_tensor(f"t_wp{i}", [128, 1024], F32)) for i in range(2)]
            t_sgn = BE(nc.sbuf_tensor("t_sgn", [128, 1024], F32))
            t_absv = BE(nc.sbuf_tensor("t_absv", [128, 1024], F32))
            t_absb = BE(nc.sbuf_tensor("t_absb", [128, 33], F32))
            t_sc = BE(nc.sbuf_tensor("t_sc", [1, 8], F32))
            t_thB = BE(nc.sbuf_tensor("t_thB", [128, 2], F32))
            t_xt2 = BE(nc.sbuf_tensor("t_xt2", [128, 2048], F32))
            t_bn2 = BE(nc.sbuf_tensor("t_bn2", [128, 24], F32))
            t_mv2 = BE(nc.sbuf_tensor("t_mv2", [128, 4], F32))
            t_rc2 = BE(nc.sbuf_tensor("t_rc2", [128, 8], F32))
            t_qf = BE(nc.sbuf_tensor("t_qf", [128, 1024], F32))
            t_qi = BE(nc.sbuf_tensor("t_qi", [128, 1024], I32))
            t_qb = BE(nc.sbuf_tensor("t_qb", [128, 2048], BF16))
            t_xqf = BE(nc.sbuf_tensor("t_xqf", [128, 8 * 256], BF16))
            t_s1r = BE(nc.sbuf_tensor("t_s1r", [2, 128], F32))
            t_s1B = BE(nc.sbuf_tensor("t_s1B", [128, 256], F32))
            t_gp = [BE(nc.sbuf_tensor(f"t_gp{i}", [128, 256], F32)) for i in range(2)]
            t_b1 = BE(nc.sbuf_tensor("t_b1", [128, 32], F32))

            LD(LMW, t_b1[:, :], d_b1, waits=barrier)

            w1red = {}
            for i in range(32):
                w = [(sV, w1red[i - 2])] if i >= 2 else []
                wl = LD(LWI[i % 2], t_wp[i % 2][:, :], d_w1[i, :, :], waits=w)
                P.add("vector", (lambda i_: (lambda e: e.tensor_reduce(
                    out=t_absb[:, i_:i_ + 1], in_=t_wp[i_ % 2][:, :], axis=AX.X,
                    op=OP.add, apply_absolute_value=True)))(i),
                    waits=[(LWI[i % 2], wl)] + (barrier if i == 0 else []), inc=sV)
                w1red[i] = sV.n
            P.add("vector", lambda e: e.drain(), inc=sV)
            P.add("vector", lambda e: e.tensor_reduce(
                out=t_absb[:, 32:33], in_=t_absb[:, 0:32], axis=AX.X, op=OP.add), inc=sV)
            vv = sV.n
            P.add("tensor", lambda e: e.matmul(
                qs[0:1, 0:1], t_ones[:, 0:1], t_absb[:, 32:33], start=True, stop=True),
                waits=[(sV, vv)] + barrier, inc=sT)
            wsT = sT.n
            P.add("vector", lambda e: e.tensor_scalar(
                out=t_sc[0:1, 0:1], in0=qs[0:1, 0:1], scalar1=float(2.0 ** -22),
                scalar2=EPS, op0=OP.mult, op1=OP.max), waits=[(sT, wsT)], inc=sV)
            P.add("vector", lambda e: e.tensor_scalar(
                out=t_sc[0:1, 1:2], in0=qs[0:1, 0:1], scalar1=float(2.0 ** -23),
                scalar2=EPS * 0.5, op0=OP.mult, op1=OP.max), inc=sV)
            P.add("vector", lambda e: e.tensor_scalar(
                out=t_sc[0:1, 2:3], in0=qs[0:1, 0:1], scalar1=float(2.0 ** -22 / 127.0),
                scalar2=EPS / 127.0, op0=OP.mult, op1=OP.max), inc=sV)
            P.add("vector", lambda e: e.drain(), inc=sV)
            scv = sV.n
            th_st = ST(STH, d_scd[0:1, 0:3], t_sc[0:1, 0:3], waits=[(sV, scv)])
            th_ld = LD(LTH, t_thB[:, 0:2], d_scd[0:1, 1:3].broadcast_to((128, 2)),
                       waits=[(STH, th_st)])
            thg = th_ld
            w1q_v = {}
            sgn_f = 0
            for i in range(32):
                w = [(sV, w1q_v[i - 2])] if i >= 2 else [(sV, w1red[31])]
                rl = LD(LWI[i % 2], t_wp[i % 2][:, :], d_w1[i, :, :], waits=w)
                P.add("scalar", (lambda i_: (lambda e: e.activation(
                    out=t_sgn[:, :], in_=t_wp[i_ % 2][:, :], func=AF.Sign)))(i),
                    waits=[(LWI[i % 2], rl), (sV, sgn_f)] + (barrier if i == 0 else []), inc=sS)
                P.add("scalar", (lambda i_: (lambda e: e.activation(
                    out=t_absv[:, :], in_=t_wp[i_ % 2][:, :], func=AF.Abs)))(i), inc=sS)
                ssn = sS.n
                P.add("vector", lambda e: e.tensor_scalar(
                    out=t_absv[:, :], in0=t_absv[:, :], scalar1=t_thB[:, 0:1],
                    scalar2=None, op0=OP.is_gt), waits=[(sS, ssn), (LTH, thg)], inc=sV)
                P.add("vector", (lambda i_: (lambda e: e.tensor_tensor(
                    out=t_w1q[i_][:, :], in0=t_absv[:, :], in1=t_sgn[:, :],
                    op=OP.mult)))(i), inc=sV)
                w1q_v[i] = sV.n
                sgn_f = sV.n
            w1q_done = sV.n

            prev_q_done = 0
            prev_sgn = 0
            s1_ldf = 0
            s1B_used = 0
            prev_g = {30: 0, 31: 0}
            prev_tq = 0
            prev_gst = {30: 0, 31: 0}
            for c in range(NBC):
                for g in range(2):
                    LD(LX, t_xt2[:, g * 1024:(g + 1) * 1024],
                       d_x[c * 256 + g * 128: c * 256 + (g + 1) * 128, :],
                       waits=([(sV, prev_q_done)] if g == 0 else []))
                x2_full = LX.n
                qv = {}
                for g in range(2):
                    w = [(LX, x2_full)] if g == 0 else []
                    if c == 0 and g == 0:
                        w.append((sV, w1q_done))
                    P.add("vector", (lambda a, g_: (lambda e: e.bn_stats(
                        t_bn2[:, 12 * g_:12 * g_ + 6], a)))(
                        t_xt2[:, g * 1024:g * 1024 + 512], g), waits=w, inc=sV)
                    P.add("vector", (lambda a, g_: (lambda e: e.bn_stats(
                        t_bn2[:, 12 * g_ + 6:12 * g_ + 12], a)))(
                        t_xt2[:, g * 1024 + 512:(g + 1) * 1024], g), inc=sV)
                for g in range(2):
                    P.add("vector", (lambda g_: (lambda e: e.bn_aggr(
                        t_mv2[:, 2 * g_:2 * g_ + 2], t_bn2[:, 12 * g_:12 * g_ + 12])))(g),
                        inc=sV)
                vg = sV.n
                for g in range(2):
                    P.add("scalar", (lambda g_: (lambda e: e.activation(
                        out=t_rc2[:, g_:g_ + 1], in_=t_mv2[:, 2 * g_ + 1:2 * g_ + 2],
                        func=AF.Ln, bias=t_cst[:, 0:1], scale=1.0)))(g),
                        waits=[(sV, vg)] if g == 0 else [], inc=sS)
                P.add("scalar", lambda e: e.drain(), inc=sS)
                for g in range(2):
                    P.add("scalar", (lambda g_: (lambda e: e.activation(
                        out=t_rc2[:, g_:g_ + 1], in_=t_rc2[:, g_:g_ + 1],
                        func=AF.Exp, scale=-0.5)))(g), inc=sS)
                sg = sS.n
                for g in range(2):
                    P.add("vector", (lambda a, g_: (lambda e: e.tensor_scalar(
                        out=a, in0=a, scalar1=t_mv2[:, 2 * g_:2 * g_ + 1],
                        scalar2=t_rc2[:, g_:g_ + 1], op0=OP.subtract, op1=OP.mult)))(
                        t_xt2[:, g * 1024:(g + 1) * 1024], g),
                        waits=[(sS, sg)] if g == 0 else [], inc=sV)
                for g in range(2):
                    P.add("vector", (lambda a, g_: (lambda e: e.tensor_reduce(
                        out=t_rc2[:, 2 + g_:3 + g_], in_=a, axis=AX.X, op=OP.max,
                        apply_absolute_value=True)))(
                        t_xt2[:, g * 1024:(g + 1) * 1024], g), inc=sV)
                P.add("vector", lambda e: e.drain(), inc=sV)
                for g in range(2):
                    P.add("vector", (lambda g_: (lambda e: e.tensor_scalar(
                        out=t_rc2[:, 2 + g_:3 + g_], in0=t_rc2[:, 2 + g_:3 + g_],
                        scalar1=EPS, scalar2=None, op0=OP.max)))(g), inc=sV)
                P.add("vector", lambda e: e.drain(), inc=sV)
                for g in range(2):
                    P.add("vector", (lambda g_: (lambda e: e.reciprocal(
                        out=t_rc2[:, 4 + g_:5 + g_], in_=t_rc2[:, 2 + g_:3 + g_])))(g), inc=sV)
                P.add("vector", lambda e: e.drain(), inc=sV)
                for g in range(2):
                    P.add("vector", (lambda g_: (lambda e: e.tensor_scalar(
                        out=t_rc2[:, 4 + g_:5 + g_], in0=t_rc2[:, 4 + g_:5 + g_],
                        scalar1=127.0, scalar2=None, op0=OP.mult)))(g), inc=sV)
                for g in range(2):
                    P.add("vector", (lambda g_: (lambda e: e.tensor_scalar(
                        out=t_rc2[:, 6 + g_:7 + g_], in0=t_rc2[:, 2 + g_:3 + g_],
                        scalar1=t_thB[:, 1:2], scalar2=None, op0=OP.mult)))(g), inc=sV)
                P.add("vector", lambda e: e.drain(), inc=sV)
                vq0 = sV.n
                for g in range(2):
                    gsl = t_xt2[:, g * 1024:(g + 1) * 1024]
                    P.add("vector", (lambda a, g_: (lambda e: e.tensor_scalar(
                        out=t_qf[:, :], in0=a, scalar1=t_rc2[:, 4 + g_:5 + g_],
                        scalar2=None, op0=OP.mult)))(gsl, g),
                        waits=[(sV, vq0)] if g == 0 else [], inc=sV)
                    P.add("vector", lambda e: e.tensor_copy(t_qi[:, :], t_qf[:, :]), inc=sV)
                    P.add("vector", lambda e: e.tensor_tensor(
                        out=t_sgn[:, :], in0=t_qf[:, :], in1=t_qi[:, :],
                        op=OP.subtract), inc=sV)
                    P.add("vector", lambda e: e.scalar_tensor_tensor(
                        out=t_qf[:, :], in0=t_sgn[:, :], scalar=0.5, in1=t_qi[:, :],
                        op0=OP.is_gt, op1=OP.add), inc=sV)
                    P.add("vector", lambda e: e.tensor_scalar(
                        out=t_sgn[:, :], in0=t_sgn[:, :], scalar1=-0.5, scalar2=None,
                        op0=OP.is_lt), inc=sV)
                    P.add("vector", lambda e: e.tensor_tensor(
                        out=t_qf[:, :], in0=t_qf[:, :], in1=t_sgn[:, :],
                        op=OP.subtract), inc=sV)
                    P.add("vector", (lambda g_: (lambda e: e.tensor_scalar(
                        out=t_qb[:, g_ * 1024:(g_ + 1) * 1024], in0=t_qf[:, :],
                        scalar1=127.0, scalar2=-128.0, op0=OP.min, op1=OP.max)))(g), inc=sV)
                    qv[g] = sV.n
                q_done = sV.n
                prev_q_done = q_done
                prev_sgn = sS.n
                P.add("tensor", lambda e: e.transpose(
                    qs[0:2, 0:128], t_rc2[:, 6:8], t_id[:, :]),
                    waits=[(sV, q_done), (sV, prev_s1e if c > 0 else 0)], inc=sT)
                s1t = sT.n
                P.add("vector", lambda e: e.tensor_copy(t_s1r[:, :], qs[0:2, 0:128]),
                      waits=[(sT, s1t)], inc=sV)
                s1e = sV.n
                prev_s1e = s1e
                s1_st = ST(SS1, d_rw1, t_s1r[:, :], waits=[(sV, s1e), (LS1, s1_ldf)])
                LD(LS1, t_s1B[:, 0:128], d_rw1[0:1, :].broadcast_to((128, 128)),
                   waits=[(SS1, s1_st), (sV, s1B_used)])
                s1_ldf = LD(LS1, t_s1B[:, 128:256], d_rw1[1:2, :].broadcast_to((128, 128)))
                s1g = s1_ldf
                ev_v = {}
                for hp in range(2):
                    bank = pQ[hp]
                    last_t = None
                    for dd in range(4):
                        dsl = hp * 4 + dd
                        for g in range(2):
                            w = []
                            if dd == 0 and g == 0:
                                w = [(sV, q_done)]
                                if c > 0:
                                    w.append((sV, prev_ev[hp]))
                            last_t = P.add("tensor", (lambda bk, d_, g_: (lambda e: e.transpose(
                                bk[:, (2 * (d_ % 4) + g_) * 128:(2 * (d_ % 4) + g_ + 1) * 128],
                                t_qb[:, g_ * 1024 + d_ * 128: g_ * 1024 + (d_ + 1) * 128],
                                t_idb[:, :])))(bank, dsl, g), waits=w, inc=sT)
                    for dd in range(4):
                        dsl = hp * 4 + dd
                        for g in range(2):
                            P.add("vector", (lambda bk, d_, g_: (lambda e: e.tensor_copy(
                                t_xqf[:, d_ * 256 + g_ * 128: d_ * 256 + (g_ + 1) * 128],
                                bk[:, (2 * (d_ % 4) + g_) * 128:(2 * (d_ % 4) + g_ + 1) * 128])))(bank, dsl, g),
                                waits=[(sT, last_t)] if (dd == 0 and g == 0) else [], inc=sV)
                    ev_v[hp] = sV.n
                prev_ev = ev_v
                xq_ev = sV.n
                g_done = {}
                gst = {}
                for m in range(32):
                    pg = m % 2
                    w = [(sV, xq_ev), (LS1, s1g)]
                    if m >= 2:
                        w.append((sS, g_done[m - 2]))
                    else:
                        w.append((sS, prev_g[30 + m]))
                    for k in range(8):
                        P.add("tensor", (lambda pg_, k_, m_: (lambda e: e.matmul(
                            q0[pg_][:, 0:256], t_w1q[m_][:, k_ * 128:(k_ + 1) * 128],
                            t_xqf[:, k_ * 256:(k_ + 1) * 256],
                            start=(k_ == 0), stop=(k_ == 7))))(pg, k, m),
                            waits=w if k == 0 else [], inc=sT)
                    mmt = sT.n
                    vw = [(sT, mmt)]
                    vw.append((SGP[pg], gst[m - 2] if m >= 2 else prev_gst[30 + m]))
                    P.add("vector", (lambda pg_: (lambda e: e.tensor_tensor(
                        out=t_gp[pg_][:, :], in0=q0[pg_][:, 0:256], in1=t_s1B[:, :],
                        op=OP.mult)))(pg), waits=vw, inc=sV)
                    s1B_used = sV.n
                    vt = sV.n
                    P.add("scalar", (lambda pg_, m_: (lambda e: e.activation(
                        out=t_gp[pg_][:, :], in_=t_gp[pg_][:, :], func=AF.Gelu,
                        bias=t_b1[:, m_:m_ + 1], scale=1.0)))(pg, m),
                        waits=[(sV, vt)], inc=sS)
                    g_done[m] = sS.n
                    gst[m] = ST(SGP[pg], d_gel[c, m, :, :], t_gp[pg][:, :],
                                waits=[(sS, g_done[m])])
                prev_mm1 = sT.n
                prev_g = {30: g_done[30], 31: g_done[31]}
                prev_gst = {30: gst[30], 31: gst[31]}

            B1_V, B1_S, B1_T, B1_G = sV.n, sS.n, sT.n, sG.n

        barrier2 = ([(sV, B1_V), (sS, B1_S), (sT, B1_T), (sG, B1_G)]
                    + [(s, s.n) for s in ALL_DMA_SEMS if s.n > 0])

        with ExitStack() as b2ctx:
            BE2 = b2ctx.enter_context
            pm2 = [BE2(nc.psum_tensor(f"pm2{i}", [128, 512], F32)) for i in range(2)]
            pt2 = [BE2(nc.psum_tensor(f"pt2{i}", [128, 512], F32)) for i in range(2)]
            po2 = [BE2(nc.psum_tensor(f"po2{i}", [128, 1024], F32)) for i in range(2)]
            t_w2q = [BE2(nc.sbuf_tensor(f"t_w2q{i}", [128, 4096], BF16)) for i in range(8)]
            t_wp2 = [BE2(nc.sbuf_tensor(f"t_wp2{i}", [128, 4096], F32)) for i in range(2)]
            t_sgn2 = BE2(nc.sbuf_tensor("t_sgn2", [128, 4096], F32))
            t_abs2 = BE2(nc.sbuf_tensor("t_abs2", [128, 4096], F32))
            t_ab2 = BE2(nc.sbuf_tensor("t_ab2", [128, 9], F32))
            t_sc2 = BE2(nc.sbuf_tensor("t_sc2", [1, 4], F32))
            t_th2 = BE2(nc.sbuf_tensor("t_th2", [128, 2], F32))
            t_gel = BE2(nc.sbuf_tensor("t_gel", [128, 32 * 256], F32))
            t_mx = BE2(nc.sbuf_tensor("t_mx", [128, 20], F32))
            t_c2r = BE2(nc.sbuf_tensor("t_c2r", [2, 128], F32))
            t_c2B = BE2(nc.sbuf_tensor("t_c2B", [128, 256], F32))
            t_s2B = BE2(nc.sbuf_tensor("t_s2B", [128, 256], F32))
            t_sgn3 = BE2(nc.sbuf_tensor("t_sgn3", [128, 256], F32))
            t_qf2 = BE2(nc.sbuf_tensor("t_qf2", [128, 256], F32))
            t_qi2 = BE2(nc.sbuf_tensor("t_qi2", [128, 256], I32))
            t_xq2 = BE2(nc.sbuf_tensor("t_xq2", [128, 32 * 256], BF16))
            t_ff = BE2(nc.sbuf_tensor("t_ff", [128, 8 * 256], F32))
            t_b2 = BE2(nc.sbuf_tensor("t_b2", [128, 8], F32))
            t_acr = [BE2(nc.sbuf_tensor(f"t_acr{i}", [128, 1024], F32)) for i in range(2)]
            t_op = [BE2(nc.sbuf_tensor(f"t_op{i}", [128, 1024], F32)) for i in range(2)]

            LD(LMW, t_b2[:, :], d_b2, waits=barrier2)
            w2red = {}
            for i in range(8):
                w = [(sV, w2red[i - 2])] if i >= 2 else []
                wl = LD(LWI[i % 2], t_wp2[i % 2][:, :], d_w2[i, :, :], waits=w)
                P.add("vector", (lambda i_: (lambda e: e.tensor_reduce(
                    out=t_ab2[:, i_:i_ + 1], in_=t_wp2[i_ % 2][:, :], axis=AX.X,
                    op=OP.add, apply_absolute_value=True)))(i),
                    waits=[(LWI[i % 2], wl)] + (barrier2 if i == 0 else []), inc=sV)
                w2red[i] = sV.n
            P.add("vector", lambda e: e.drain(), inc=sV)
            P.add("vector", lambda e: e.tensor_reduce(
                out=t_ab2[:, 8:9], in_=t_ab2[:, 0:8], axis=AX.X, op=OP.add), inc=sV)
            vv = sV.n
            P.add("tensor", lambda e: e.matmul(
                pm2[0][0:1, 0:1], t_ones[:, 0:1], t_ab2[:, 8:9], start=True, stop=True),
                waits=[(sV, vv)] + barrier2, inc=sT)
            wsT = sT.n
            P.add("vector", lambda e: e.tensor_scalar(
                out=t_sc2[0:1, 0:1], in0=pm2[0][0:1, 0:1], scalar1=float(2.0 ** -22),
                scalar2=EPS, op0=OP.mult, op1=OP.max), waits=[(sT, wsT)], inc=sV)
            P.add("vector", lambda e: e.tensor_scalar(
                out=t_sc2[0:1, 1:2], in0=pm2[0][0:1, 0:1], scalar1=float(2.0 ** -23),
                scalar2=EPS * 0.5, op0=OP.mult, op1=OP.max), inc=sV)
            P.add("vector", lambda e: e.tensor_scalar(
                out=t_sc2[0:1, 2:3], in0=pm2[0][0:1, 0:1], scalar1=float(2.0 ** -22 / 127.0),
                scalar2=EPS / 127.0, op0=OP.mult, op1=OP.max), inc=sV)
            P.add("vector", lambda e: e.drain(), inc=sV)
            scv = sV.n
            th2_st = ST(STH, d_scd[0:1, 0:3], t_sc2[0:1, 0:3],
                        waits=[(sV, scv), (LTH, th_ld)])
            th2_ld = LD(LTH, t_th2[:, 0:2], d_scd[0:1, 1:3].broadcast_to((128, 2)),
                        waits=[(STH, th2_st)])
            thg = th2_ld
            w2q_v = {}
            sgn_f = 0
            for i in range(8):
                w = [(sV, w2q_v[i - 2])] if i >= 2 else [(sV, w2red[7])]
                rl = LD(LWI[i % 2], t_wp2[i % 2][:, :], d_w2[i, :, :], waits=w)
                P.add("scalar", (lambda i_: (lambda e: e.activation(
                    out=t_sgn2[:, :], in_=t_wp2[i_ % 2][:, :], func=AF.Sign)))(i),
                    waits=[(LWI[i % 2], rl), (sV, sgn_f)] + (barrier2 if i == 0 else []), inc=sS)
                P.add("scalar", (lambda i_: (lambda e: e.activation(
                    out=t_abs2[:, :], in_=t_wp2[i_ % 2][:, :], func=AF.Abs)))(i), inc=sS)
                ssn = sS.n
                P.add("vector", lambda e: e.tensor_scalar(
                    out=t_abs2[:, :], in0=t_abs2[:, :], scalar1=t_th2[:, 0:1],
                    scalar2=None, op0=OP.is_gt), waits=[(sS, ssn), (LTH, thg)], inc=sV)
                P.add("vector", (lambda i_: (lambda e: e.tensor_tensor(
                    out=t_w2q[i_][:, :], in0=t_abs2[:, :], in1=t_sgn2[:, :],
                    op=OP.mult)))(i), inc=sV)
                w2q_v[i] = sV.n
                sgn_f = sV.n
            w2q_done = sV.n

            prev_q2 = 0
            prev_sgn3 = 0
            c2_ldf = 0
            s2_ldf = 0
            s2B_used = 0
            prev_out_v = {0: 0, 1: 0}
            prev_out_st = {0: 0, 1: 0}
            prev_mm2 = 0
            prev_ff = {6: 0, 7: 0}
            for c in range(NBC):
                for hs in range(32):
                    w = []
                    if hs == 0:
                        w = [(sV, w2q_done if c == 0 else prev_q2)]
                    LD(LGL, t_gel[:, hs * 256:(hs + 1) * 256], d_gel[c, hs, :, :],
                       waits=w)
                gl_full = LGL.n
                mxv = {}
                for tg in range(2):
                    for grp in range(8):
                        bk = pt2[grp % 2]
                        w = [(LGL, gl_full)]
                        if (tg, grp) >= (0, 2):
                            pr = (tg, grp - 2) if grp >= 2 else (tg - 1, 6 + grp)
                            if pr in mxv:
                                w.append((sV, mxv[pr]))
                        if c > 0 and tg == 0 and grp < 2:
                            w.append((sV, prev_mxload))
                        last_t = None
                        for j in range(4):
                            hs = grp * 4 + j
                            last_t = P.add("tensor", (lambda bk_, hs_, j_, tg_: (lambda e:
                                e.transpose(bk_[:, j_ * 128:(j_ + 1) * 128],
                                            t_gel[:, hs_ * 256 + tg_ * 128: hs_ * 256 + (tg_ + 1) * 128],
                                            t_id[:, :])))(bk, hs, j, tg),
                                waits=w if j == 0 else [], inc=sT)
                        P.add("vector", (lambda bk_, tg_, grp_: (lambda e: e.tensor_reduce(
                            out=t_mx[:, tg_ * 8 + grp_: tg_ * 8 + grp_ + 1], in_=bk_[:, :],
                            axis=AX.X, op=OP.max, apply_absolute_value=True)))(bk, tg, grp),
                            waits=[(sT, last_t)], inc=sV)
                        mxv[(tg, grp)] = sV.n
                prev_mxload = sV.n
                P.add("vector", lambda e: e.drain(), inc=sV)
                for tg in range(2):
                    P.add("vector", (lambda tg_: (lambda e: e.tensor_reduce(
                        out=t_mx[:, 16 + tg_:17 + tg_], in_=t_mx[:, tg_ * 8:(tg_ + 1) * 8],
                        axis=AX.X, op=OP.max)))(tg), inc=sV)
                P.add("vector", lambda e: e.drain(), inc=sV)
                for tg in range(2):
                    P.add("vector", (lambda tg_: (lambda e: e.tensor_scalar(
                        out=t_mx[:, 16 + tg_:17 + tg_], in0=t_mx[:, 16 + tg_:17 + tg_],
                        scalar1=EPS, scalar2=None, op0=OP.max)))(tg), inc=sV)
                P.add("vector", lambda e: e.drain(), inc=sV)
                for tg in range(2):
                    P.add("vector", (lambda tg_: (lambda e: e.reciprocal(
                        out=t_mx[:, 18 + tg_:19 + tg_], in_=t_mx[:, 16 + tg_:17 + tg_])))(tg), inc=sV)
                P.add("vector", lambda e: e.drain(), inc=sV)
                P.add("vector", lambda e: e.tensor_scalar(
                    out=t_mx[:, 18:20], in0=t_mx[:, 18:20], scalar1=127.0, scalar2=None,
                    op0=OP.mult), inc=sV)
                P.add("vector", lambda e: e.tensor_scalar(
                    out=t_mx[:, 16:18], in0=t_mx[:, 16:18], scalar1=t_th2[:, 1:2],
                    scalar2=None, op0=OP.mult), inc=sV)
                P.add("vector", lambda e: e.drain(), inc=sV)
                c2v = sV.n
                P.add("tensor", lambda e: e.transpose(
                    pt2[0][0:2, 0:128], t_mx[:, 18:20], t_id[:, :]),
                    waits=[(sV, c2v)], inc=sT)
                P.add("tensor", lambda e: e.transpose(
                    pt2[0][0:2, 128:256], t_mx[:, 16:18], t_id[:, :]), inc=sT)
                c2t = sT.n
                P.add("vector", lambda e: e.tensor_copy(t_c2r[:, :], pt2[0][0:2, 0:128]),
                      waits=[(sT, c2t)], inc=sV)
                c2e = sV.n
                c2_st = ST(SC2, d_rw2, t_c2r[:, :], waits=[(sV, c2e), (LC2, c2_ldf)])
                LD(LC2, t_c2B[:, 0:128], d_rw2[0:1, :].broadcast_to((128, 128)),
                   waits=[(SC2, c2_st), (sV, prev_q2), (sT, prev_mm2)])
                c2_ldf = LD(LC2, t_c2B[:, 128:256], d_rw2[1:2, :].broadcast_to((128, 128)))
                P.add("vector", lambda e: e.tensor_copy(t_c2r[:, :], pt2[0][0:2, 128:256]),
                      waits=[(SC2, c2_st)], inc=sV)
                s2e = sV.n
                s2_st = ST(SS2, d_rw3, t_c2r[:, :], waits=[(sV, s2e), (LS2, s2_ldf)])
                LD(LS2, t_s2B[:, 0:128], d_rw3[0:1, :].broadcast_to((128, 128)),
                   waits=[(SS2, s2_st), (sV, s2B_used)])
                s2_ldf = LD(LS2, t_s2B[:, 128:256], d_rw3[1:2, :].broadcast_to((128, 128)))
                c2g = s2_ldf
                for hs in range(32):
                    w = [(LC2, c2_ldf), (LS2, s2_ldf)]
                    if hs == 0:
                        w.append((sV, prev_q2))
                        w.append((sT, prev_mm2))
                    P.add("vector", (lambda hs_: (lambda e: e.tensor_tensor(
                        out=t_qf2[:, :], in0=t_gel[:, hs_ * 256:(hs_ + 1) * 256],
                        in1=t_c2B[:, :], op=OP.mult)))(hs), waits=w, inc=sV)
                    P.add("vector", lambda e: e.tensor_copy(t_qi2[:, :], t_qf2[:, :]), inc=sV)
                    P.add("vector", lambda e: e.tensor_tensor(
                        out=t_sgn3[:, :], in0=t_qf2[:, :], in1=t_qi2[:, :],
                        op=OP.subtract), inc=sV)
                    P.add("vector", lambda e: e.scalar_tensor_tensor(
                        out=t_qf2[:, :], in0=t_sgn3[:, :], scalar=0.5, in1=t_qi2[:, :],
                        op0=OP.is_gt, op1=OP.add), inc=sV)
                    P.add("vector", lambda e: e.tensor_scalar(
                        out=t_sgn3[:, :], in0=t_sgn3[:, :], scalar1=-0.5, scalar2=None,
                        op0=OP.is_lt), inc=sV)
                    P.add("vector", lambda e: e.tensor_tensor(
                        out=t_qf2[:, :], in0=t_qf2[:, :], in1=t_sgn3[:, :],
                        op=OP.subtract), inc=sV)
                    P.add("vector", (lambda hs_: (lambda e: e.tensor_scalar(
                        out=t_xq2[:, hs_ * 256:(hs_ + 1) * 256], in0=t_qf2[:, :],
                        scalar1=127.0, scalar2=-128.0, op0=OP.min, op1=OP.max)))(hs), inc=sV)
                q2_done = sV.n
                prev_q2 = q2_done
                prev_sgn3 = sS.n
                ff_v = {}
                for m in range(8):
                    pg = m % 2
                    w = [(sV, q2_done)]
                    if m >= 2:
                        w.append((sV, ff_v[m - 2]))
                    else:
                        w.append((sV, prev_ff[6 + m]))
                    for k in range(32):
                        P.add("tensor", (lambda pg_, k_, m_: (lambda e: e.matmul(
                            pm2[pg_][:, 0:256], t_w2q[m_][:, k_ * 128:(k_ + 1) * 128],
                            t_xq2[:, k_ * 256:(k_ + 1) * 256],
                            start=(k_ == 0), stop=(k_ == 31))))(pg, k, m),
                            waits=w if k == 0 else [], inc=sT)
                    mmt = sT.n
                    P.add("vector", (lambda pg_, m_: (lambda e: e.tensor_tensor(
                        out=t_ff[:, m_ * 256:(m_ + 1) * 256], in0=pm2[pg_][:, 0:256],
                        in1=t_s2B[:, :], op=OP.mult)))(pg, m), waits=[(sT, mmt)], inc=sV)
                    s2B_used = sV.n
                    P.add("vector", (lambda m_: (lambda e: e.tensor_scalar(
                        out=t_ff[:, m_ * 256:(m_ + 1) * 256],
                        in0=t_ff[:, m_ * 256:(m_ + 1) * 256],
                        scalar1=t_b2[:, m_:m_ + 1], scalar2=None, op0=OP.add)))(m), inc=sV)
                    ff_v[m] = sV.n
                prev_mm2 = sT.n
                prev_ff = {6: ff_v[6], 7: ff_v[7]}
                ffn_done = sV.n
                for tg in range(2):
                    pg = tg % 2
                    ral = LD(LAC[pg], t_acr[pg][:, :],
                             d_acc[c * 256 + tg * 128: c * 256 + (tg + 1) * 128, :],
                             waits=[(sV, prev_out_v[tg]), (SOU[pg], prev_out_st[tg])])
                    bk = po2[pg]
                    last_t = None
                    for dsl in range(8):
                        w = []
                        if dsl == 0:
                            w = [(sV, ffn_done), (sV, prev_out_v[tg])]
                        last_t = P.add("tensor", (lambda bk_, d_, tg_: (lambda e: e.transpose(
                            bk_[:, d_ * 128:(d_ + 1) * 128],
                            t_ff[:, d_ * 256 + tg_ * 128: d_ * 256 + (tg_ + 1) * 128],
                            t_id[:, :])))(bk, dsl, tg), waits=w, inc=sT)
                    ow = [(sT, last_t), (LAC[pg], ral), (SOU[pg], prev_out_st[tg])]
                    P.add("vector", (lambda pg_, bk_: (lambda e: e.tensor_tensor(
                        out=t_op[pg_][:, :], in0=bk_[:, :], in1=t_acr[pg_][:, :],
                        op=OP.add)))(pg, bk), waits=ow, inc=sV)
                    prev_out_v[tg] = sV.n
                    prev_out_st[tg] = ST(
                        SOU[pg],
                        d_out[c * 256 + tg * 128: c * 256 + (tg + 1) * 128, :],
                        t_op[pg][:, :], waits=[(sV, prev_out_v[tg])])

        P.add("scalar", lambda e: e.nop(),
              waits=[(s, s.n) for s in ALL_DMA_SEMS if s.n > 0])

        block = OE(nc.Block())
        P.emit(block)

    return nc


def _host_layout(inputs):
    f = np.ascontiguousarray
    ip = inputs
    win = np.stack([ip["L0_attn_in_proj_w"][l].reshape(32, 128, 8, 128)
                    .transpose(0, 3, 2, 1).reshape(32, 128, 1024) for l in range(NLAY)])
    wout = np.stack([ip["L0_attn_out_w"][l].reshape(8, 128, 16, 128)
                     .transpose(2, 3, 0, 1).reshape(16, 128, 1024) for l in range(NLAY)])
    wdt = np.stack([ip["L0_attn_dt_w"][l].reshape(16, 128, 64)
                    .transpose(2, 0, 1).reshape(64, 2048) for l in range(NLAY)])
    wxp = np.stack([ip["L0_attn_x_proj_w"][l].T.reshape(16, 128, 96) for l in range(NLAY)])
    alog = np.stack([ip["L0_attn_A_log"][l].reshape(16, 128, 16)
                     .transpose(1, 0, 2).reshape(128, 256) for l in range(NLAY)])
    cw = np.stack([ip["L0_attn_conv_w"][l].reshape(16, 128, 4)
                   .transpose(1, 0, 2).reshape(128, 64) for l in range(NLAY)])
    cb = np.stack([ip["L0_attn_conv_b"][l].reshape(16, 128).T for l in range(NLAY)])
    dtb = np.stack([ip["L0_attn_dt_b"][l].reshape(16, 128).T for l in range(NLAY)])
    dp = np.stack([ip["L0_attn_Dp"][l].reshape(16, 128).T for l in range(NLAY)])
    nw = np.stack([ip["L0_attn_norm_w"][l].reshape(8, 128).T for l in range(NLAY)])
    w1 = ip["L0_ffn_W1"].reshape(32, 128, 8, 128).transpose(0, 3, 2, 1).reshape(32, 128, 1024)
    b1 = ip["L0_ffn_b1"].reshape(32, 128).T
    w2 = ip["L0_ffn_W2"].reshape(8, 128, 32, 128).transpose(0, 3, 2, 1).reshape(8, 128, 4096)
    b2 = ip["L0_ffn_b2"].reshape(8, 128).T
    com = {
        "w_in": f(win), "w_out": f(wout), "w_dt": f(wdt), "w_xp": f(wxp),
        "alog": f(alog), "cw": f(cw), "cb": f(cb), "dtb": f(dtb), "dp": f(dp),
        "nw": f(nw), "w1": f(w1), "b1": f(b1), "w2": f(w2), "b2": f(b2),
        "cst": np.tile(np.array([[1e-5, 1.0, 0.5, 0.0]], dtype=np.float32), (128, 1)),
        "ident": np.eye(128, dtype=np.float32),
        "identb": np.eye(128, dtype=ml_dtypes.bfloat16),
    }
    maps = []
    for b in range(N_CORES):
        m = dict(com)
        m["x"] = f(np.asarray(ip["L0_attn_x"][b]))
        maps.append(m)
    return maps


def kernel(**inputs):
    in_maps = _host_layout(inputs)
    nc = build_nc()
    res = run_bass_kernel_spmd(nc, in_maps, core_ids=list(range(N_CORES)))
    return np.stack([np.asarray(r["out"], dtype=np.float32) for r in res.results])


# version 18 (final)
# speedup vs baseline: 1.0732x; 1.0302x over previous
import numpy as np
import ml_dtypes
import concourse.bass as bass
import concourse.mybir as mybir
from concourse.alu_op_type import AluOpType as OP
from concourse.bass_utils import run_bass_kernel_spmd
from contextlib import ExitStack

F32 = mybir.dt.float32
BF16 = mybir.dt.bfloat16
I32 = mybir.dt.int32
AF = mybir.ActivationFunctionType
AX = mybir.AxisListType

B, L, DM, EI, NS, RK = 8, 4096, 1024, 2048, 16, 64
NLAY = 2
TC = 512
NCH = L // TC
TB = 256
NBC = L // TB
NSL = 16
EPS = 1e-5
BETA = 0.6
N_CORES = 8
SLAB_H = 8


class Sem:
    def __init__(self, h):
        self.h = h
        self.n = 0


class Prog:
    ENGS = ("sync", "vector", "scalar", "gpsimd", "tensor")

    def __init__(self):
        self.ops = {e: [] for e in self.ENGS}

    def add(self, eng, fn, waits=(), inc=None, amt=1):
        ws = [(s.h, int(v)) for (s, v) in waits if v and v > 0]
        post = None
        if inc is not None:
            inc.n += amt
            post = inc.n
        self.ops[eng].append((fn, ws, inc.h if inc is not None else None, amt))
        return post

    def emit(self, block):
        for eng in self.ENGS:
            ops = self.ops[eng]
            if not ops:
                continue

            def make(ops_):
                def run(e):
                    for fn, ws, inch, amt in ops_:
                        for (h, v) in ws:
                            e.wait_ge(h, v)
                        inst = fn(e) if fn is not None else None
                        if inch is not None:
                            if inst is None:
                                inst = e.nop()
                            inst.then_inc(inch, amt)
                return run
            getattr(block, eng)(make(ops))


def build_nc():
    nc = bass.Bass("TRN2", target_bir_lowering=False,
                   detect_race_conditions=False)

    d_x = nc.dram_tensor("x", [L, DM], F32, kind="ExternalInput").ap()
    d_win = nc.dram_tensor("w_in", [NLAY, 32, 128, 1024], F32, kind="ExternalInput").ap()
    d_wout = nc.dram_tensor("w_out", [NLAY, 16, 128, 1024], F32, kind="ExternalInput").ap()
    d_wdt = nc.dram_tensor("w_dt", [NLAY, 64, 2048], F32, kind="ExternalInput").ap()
    d_wxp = nc.dram_tensor("w_xp", [NLAY, 16, 128, 96], F32, kind="ExternalInput").ap()
    d_alog = nc.dram_tensor("alog", [NLAY, 128, 256], F32, kind="ExternalInput").ap()
    d_cw = nc.dram_tensor("cw", [NLAY, 128, 64], F32, kind="ExternalInput").ap()
    d_cb = nc.dram_tensor("cb", [NLAY, 128, 16], F32, kind="ExternalInput").ap()
    d_dtb = nc.dram_tensor("dtb", [NLAY, 128, 16], F32, kind="ExternalInput").ap()
    d_dp = nc.dram_tensor("dp", [NLAY, 128, 16], F32, kind="ExternalInput").ap()
    d_nw = nc.dram_tensor("nw", [NLAY, 128, 8], F32, kind="ExternalInput").ap()
    d_w1 = nc.dram_tensor("w1", [32, 128, 1024], F32, kind="ExternalInput").ap()
    d_b1 = nc.dram_tensor("b1", [128, 32], F32, kind="ExternalInput").ap()
    d_w2 = nc.dram_tensor("w2", [8, 128, 4096], F32, kind="ExternalInput").ap()
    d_b2 = nc.dram_tensor("b2", [128, 8], F32, kind="ExternalInput").ap()
    d_cst = nc.dram_tensor("cst", [128, 4], F32, kind="ExternalInput").ap()
    d_id = nc.dram_tensor("ident", [128, 128], F32, kind="ExternalInput").ap()
    d_idb = nc.dram_tensor("identb", [128, 128], BF16, kind="ExternalInput").ap()

    d_out = nc.dram_tensor("out", [L, DM], F32, kind="ExternalOutput").ap()

    d_szsp = nc.dram_tensor("sz_sp", [NLAY, NCH, NSL, 128, TC], F32, kind="Internal").ap()
    d_xbsp = nc.dram_tensor("xb_sp", [NLAY, NCH, NSL, 128, TC], F32, kind="Internal").ap()
    d_acc = nc.dram_tensor("acc_sp", [L, DM], F32, kind="Internal").ap()
    d_gel = nc.dram_tensor("gel_sp", [NBC, 32, 128, TB], F32, kind="Internal").ap()
    d_bc = nc.dram_tensor("bc_sp", [32, 512], F32, kind="Internal").ap()
    d_r2d = nc.dram_tensor("r2_sp", [1, 512], F32, kind="Internal").ap()
    d_rw1 = nc.dram_tensor("rw1_sp", [2, 128], F32, kind="Internal").ap()
    d_rw2 = nc.dram_tensor("rw2_sp", [2, 128], F32, kind="Internal").ap()
    d_rw3 = nc.dram_tensor("rw3_sp", [2, 128], F32, kind="Internal").ap()
    d_scd = nc.dram_tensor("sc_sp", [1, 4], F32, kind="Internal").ap()

    P = Prog()

    with ExitStack() as octx:
        OE = octx.enter_context
        sT = Sem(OE(nc.semaphore("sT")))
        sV = Sem(OE(nc.semaphore("sV")))
        sS = Sem(OE(nc.semaphore("sS")))
        sG = Sem(OE(nc.semaphore("sG")))
        LX = Sem(OE(nc.semaphore("LX")))
        LXP = [Sem(OE(nc.semaphore(f"LXP{i}"))) for i in range(2)]
        LWI = [Sem(OE(nc.semaphore(f"LWI{i}"))) for i in range(2)]
        LWO = [Sem(OE(nc.semaphore(f"LWO{i}"))) for i in range(2)]
        LBB = Sem(OE(nc.semaphore("LBB")))
        LCB = Sem(OE(nc.semaphore("LCB")))
        LR2 = Sem(OE(nc.semaphore("LR2")))
        LMW = Sem(OE(nc.semaphore("LMW")))
        LGL = Sem(OE(nc.semaphore("LGL")))
        LAC = [Sem(OE(nc.semaphore(f"LAC{i}"))) for i in range(2)]
        LTH = Sem(OE(nc.semaphore("LTH")))
        LS1 = Sem(OE(nc.semaphore("LS1")))
        LC2 = Sem(OE(nc.semaphore("LC2")))
        LS2 = Sem(OE(nc.semaphore("LS2")))
        SZ = [Sem(OE(nc.semaphore(f"SZ{i}"))) for i in range(2)]
        SXB = Sem(OE(nc.semaphore("SXB")))
        SBC = Sem(OE(nc.semaphore("SBC")))
        SR2 = Sem(OE(nc.semaphore("SR2")))
        SAC = Sem(OE(nc.semaphore("SAC")))
        SGP = [Sem(OE(nc.semaphore(f"SGP{i}"))) for i in range(2)]
        STH = Sem(OE(nc.semaphore("STH")))
        SS1 = Sem(OE(nc.semaphore("SS1")))
        SC2 = Sem(OE(nc.semaphore("SC2")))
        SS2 = Sem(OE(nc.semaphore("SS2")))
        SOU = [Sem(OE(nc.semaphore(f"SOU{i}"))) for i in range(2)]
        ALL_DMA_SEMS = ([LX, LBB, LCB, LR2, LMW, LGL, LTH, LS1, LC2, LS2,
                         SXB, SBC, SR2, SAC, STH, SS1, SC2, SS2]
                        + LXP + LWI + LWO + LAC + SZ + SGP + SOU)

        t_id = OE(nc.sbuf_tensor("t_id", [128, 128], F32))
        t_idb = OE(nc.sbuf_tensor("t_idb", [128, 128], BF16))
        t_ones = OE(nc.sbuf_tensor("t_ones", [128, 1], F32))
        t_cst = OE(nc.sbuf_tensor("t_cst", [128, 4], F32))

        def LD(sem, dst, src, waits=(), eng="sync"):
            return P.add(eng, (lambda d_, s_: (lambda e: e.dma_start(out=d_, in_=s_)))(dst, src),
                         waits=waits, inc=sem, amt=16)

        def ST(sem, dst, src, waits=()):
            return P.add("scalar", (lambda d_, s_: (lambda e: e.dma_start(out=d_, in_=s_)))(dst, src),
                         waits=waits, inc=sem, amt=16)

        LD(LMW, t_cst[:, :], d_cst)
        LD(LMW, t_id[:, :], d_id)
        LD(LMW, t_idb[:, :], d_idb)
        P.add("vector", lambda e: e.memset(t_ones[:, :], 1.0), inc=sV)

        with ExitStack() as actx:
            AE = actx.enter_context
            p = [AE(nc.psum_tensor(f"p{i}", [128, 512], F32)) for i in range(8)]
            t_dtw = AE(nc.sbuf_tensor("t_dtw", [64, 2048], F32))
            t_xpw = AE(nc.sbuf_tensor("t_xpw", [128, 16 * 96], F32))
            t_negA = [AE(nc.sbuf_tensor(f"t_negA{l}", [128, 256], F32)) for l in range(NLAY)]
            t_cw = [AE(nc.sbuf_tensor(f"t_cw{l}", [128, 64], F32)) for l in range(NLAY)]
            t_cb = [AE(nc.sbuf_tensor(f"t_cb{l}", [128, 16], F32)) for l in range(NLAY)]
            t_dtb = [AE(nc.sbuf_tensor(f"t_dtb{l}", [128, 16], F32)) for l in range(NLAY)]
            t_dp = [AE(nc.sbuf_tensor(f"t_dp{l}", [128, 16], F32)) for l in range(NLAY)]
            t_nw = [AE(nc.sbuf_tensor(f"t_nw{l}", [128, 8], F32)) for l in range(NLAY)]
            t_vst = [AE(nc.sbuf_tensor(f"t_vst{l}", [128, 256], F32)) for l in range(NLAY)]
            t_hst = [AE(nc.sbuf_tensor(f"t_hst{l}", [128, 256], F32)) for l in range(NLAY)]
            t_tail = [AE(nc.sbuf_tensor(f"t_tail{l}", [128, 48], F32)) for l in range(NLAY)]
            t_xt = AE(nc.sbuf_tensor("t_xt", [128, 4096], F32))
            t_base = AE(nc.sbuf_tensor("t_base", [128, 8 * 512], F32))
            t_xb = AE(nc.sbuf_tensor("t_xb", [128, 16 * 512], F32))
            t_dl = AE(nc.sbuf_tensor("t_dl", [128, 16 * 512], F32))
            t_vh = AE(nc.sbuf_tensor("t_vh", [128, 8 * 512], F32))
            t_y = AE(nc.sbuf_tensor("t_y", [128, 8 * 512], F32))
            t_tb0 = AE(nc.sbuf_tensor("t_tb0", [128, 512], F32))
            t_bpw = AE(nc.sbuf_tensor("t_bpw", [128, 512], F32))
            t_rhs = AE(nc.sbuf_tensor("t_rhs", [128, 2560], F32))
            RHS_OFF = [0, 512, 896, 1152]
            t_BT = AE(nc.sbuf_tensor("t_BT", [128, 64], F32))
            t_pr = AE(nc.sbuf_tensor("t_pr", [128, 1], F32))
            t_Cb = AE(nc.sbuf_tensor("t_Cb", [128, 1024], F32))
            t_dbc = AE(nc.sbuf_tensor("t_dbc", [128, 512], F32))
            t_mv = AE(nc.sbuf_tensor("t_mv", [128, 8], F32))
            t_bnst = AE(nc.sbuf_tensor("t_bnst", [128, 48], F32))
            t_r = AE(nc.sbuf_tensor("t_r", [128, 4], F32))
            t_r2 = AE(nc.sbuf_tensor("t_r2", [1, 512], F32))
            t_r2B = AE(nc.sbuf_tensor("t_r2B", [128, 512], F32))
            t_wi = [AE(nc.sbuf_tensor(f"t_wi{i}", [128, 1024], F32)) for i in range(2)]
            t_wo = [AE(nc.sbuf_tensor(f"t_wo{i}", [128, 1024], F32)) for i in range(2)]
            t_xc = [AE(nc.sbuf_tensor(f"t_xc{i}", [128, 512], F32)) for i in range(2)]
            t_sz = [AE(nc.sbuf_tensor(f"t_sz{i}", [128, 512], F32)) for i in range(2)]
            t_e = t_xc
            t_dA = [AE(nc.sbuf_tensor(f"t_dA{i}", [128, 512], F32)) for i in range(3)]
            t_vw = t_xt
            t_xbr = [t_wi[0][:, 0:512], t_wi[1][:, 0:512]]
            t_szr = [t_wi[0][:, 512:1024], t_wi[1][:, 512:1024]]
            t_y2 = t_sz

            for l in range(NLAY):
                LD(LMW, t_negA[l][:, :], d_alog[l, :, :])
                LD(LMW, t_cw[l][:, :], d_cw[l, :, :])
                LD(LMW, t_cb[l][:, :], d_cb[l, :, :])
                LD(LMW, t_dtb[l][:, :], d_dtb[l, :, :])
                LD(LMW, t_dp[l][:, :], d_dp[l, :, :])
                LD(LMW, t_nw[l][:, :], d_nw[l, :, :])
            ld_init = LMW.n
            for l in range(NLAY):
                P.add("scalar", (lambda l_: (lambda e: e.activation(
                    out=t_negA[l_][:, :], in_=t_negA[l_][:, :], func=AF.Exp)))(l),
                    waits=[(LMW, ld_init)], inc=sS)
                sA = sS.n
                P.add("vector", (lambda l_: (lambda e: e.tensor_scalar(
                    out=t_negA[l_][:, :], in0=t_negA[l_][:, :], scalar1=-1.0,
                    scalar2=None, op0=OP.mult)))(l), waits=[(sS, sA)], inc=sV)
                P.add("vector", (lambda l_: (lambda e: e.memset(t_vst[l_][:, :], 0.0)))(l), inc=sV)
                P.add("vector", (lambda l_: (lambda e: e.memset(t_hst[l_][:, :], 0.0)))(l), inc=sV)
                P.add("vector", (lambda l_: (lambda e: e.memset(t_tail[l_][:, :], 0.0)))(l), inc=sV)
            LNB = float(np.log(np.float64(BETA)))
            P.add("vector", lambda e: e.memset(t_bpw[:, :], 1.0), inc=sV)
            P.add("vector", lambda e: e.drain(), inc=sV)
            P.add("vector", lambda e: e.tensor_tensor_scan(
                out=t_tb0[:, :], data0=t_bpw[:, :], data1=t_bpw[:, :],
                initial=t_cst[:, 3:4], op0=OP.mult, op1=OP.add),
                waits=[(LMW, ld_init)], inc=sV)
            io_v = sV.n
            P.add("scalar", lambda e: e.activation(
                out=t_bpw[:, :], in_=t_tb0[:, :], func=AF.Exp, scale=LNB),
                waits=[(sV, io_v), (LMW, ld_init)], inc=sS)
            P.add("tensor", lambda e: e.transpose(
                p[7][:, 0:1], t_tb0[0:1, 0:128], t_id[0:1, 0:1]),
                waits=[(sV, io_v), (LMW, ld_init)], inc=sT)
            pr_t = sT.n
            P.add("scalar", lambda e: e.activation(
                out=t_pr[:, :], in_=p[7][:, 0:1], func=AF.Copy),
                waits=[(sT, pr_t)], inc=sS)
            pr_s = sS.n
            P.add("scalar", lambda e: e.drain(), inc=sS)
            P.add("vector", lambda e: e.tensor_scalar(
                out=t_tb0[:, :], in0=t_tb0[:, :], scalar1=t_pr[:, 0:1],
                scalar2=None, op0=OP.subtract), waits=[(sS, pr_s)], inc=sV)
            P.add("vector", lambda e: e.drain(), inc=sV)
            P.add("vector", lambda e: e.tensor_scalar(
                out=t_rhs[:, 0:512], in0=t_tb0[:, :], scalar1=0.0,
                scalar2=None, op0=OP.is_ge), inc=sV)
            msk_v = sV.n
            P.add("scalar", lambda e: e.activation(
                out=t_tb0[:, :], in_=t_tb0[:, :], func=AF.Exp, scale=LNB),
                waits=[(sV, msk_v)], inc=sS)
            tb_exp = sS.n
            P.add("vector", lambda e: e.tensor_tensor(
                out=t_tb0[:, :], in0=t_tb0[:, :], in1=t_rhs[:, 0:512],
                op=OP.mult), waits=[(sS, tb_exp)], inc=sV)
            P.add("vector", lambda e: e.drain(), inc=sV)
            P.add("vector", lambda e: e.memset(t_rhs[:, :], 0.0), inc=sV)

            acc_fence = sV.n
            s0tp_fence = 0
            wi_fence = [0, 0]
            wo_fence = [0, 0]
            xh_fence = 0
            xbst_full_prev = [0] * NLAY
            stz_prev = [[0, 0] for _ in range(NLAY)]
            prev_land = {s: 0 for s in range(SLAB_H)}
            prev_xv = {s: 0 for s in range(SLAB_H)}
            prev_hsc = {s: 0 for s in range(SLAB_H)}
            prev_vmm_f = 0
            prev_ep = {0: 0, 1: 0}
            dtw_fence = 0
            xpw_fence = 0
            r2_ld = 0
            r2B_used = 0
            bc_ld = 0
            bb_ld = 0
            prev_opT = 0
            accst_fence = 0
            base_done = sV.n

            for c in range(NCH):
                for g in range(4):
                    LD(LX, t_xt[:, g * 1024:(g + 1) * 1024],
                       d_x[c * 512 + g * 128: c * 512 + (g + 1) * 128, :],
                       waits=[(sT, s0tp_fence), (sV, acc_fence),
                              (sT, prev_opT)] if g == 0 else [])
                xld_full = LX.n
                ln_v = []
                for g in range(4):
                    P.add("vector", (lambda a, g_: (lambda e: e.bn_stats(
                        t_bnst[:, 12 * g_:12 * g_ + 6], a)))(
                        t_xt[:, g * 1024: g * 1024 + 512], g),
                        waits=[(LX, xld_full)] if g == 0 else [], inc=sV)
                    P.add("vector", (lambda a, g_: (lambda e: e.bn_stats(
                        t_bnst[:, 12 * g_ + 6:12 * g_ + 12], a)))(
                        t_xt[:, g * 1024 + 512: (g + 1) * 1024], g), inc=sV)
                vstats = sV.n
                for g in range(4):
                    P.add("vector", (lambda g_: (lambda e: e.bn_aggr(
                        t_mv[:, 2 * g_:2 * g_ + 2], t_bnst[:, 12 * g_:12 * g_ + 12])))(g),
                        waits=[(sV, vstats)], inc=sV)
                vg = sV.n
                for g in range(4):
                    P.add("scalar", (lambda g_: (lambda e: e.activation(
                        out=t_r[:, g_:g_ + 1], in_=t_mv[:, 2 * g_ + 1:2 * g_ + 2],
                        func=AF.Ln, bias=t_cst[:, 0:1], scale=1.0)))(g),
                        waits=[(sV, vg)] if g == 0 else [], inc=sS)
                P.add("scalar", lambda e: e.drain(), inc=sS)
                for g in range(4):
                    P.add("scalar", (lambda g_: (lambda e: e.activation(
                        out=t_r[:, g_:g_ + 1], in_=t_r[:, g_:g_ + 1],
                        func=AF.Exp, scale=-0.5)))(g), inc=sS)
                P.add("scalar", lambda e: e.drain(), inc=sS)
                sg = sS.n
                for g in range(4):
                    P.add("vector", (lambda a, g_: (lambda e: e.tensor_scalar(
                        out=a, in0=a, scalar1=t_mv[:, 2 * g_:2 * g_ + 1],
                        scalar2=t_r[:, g_:g_ + 1], op0=OP.subtract, op1=OP.mult)))(
                        t_xt[:, g * 1024:(g + 1) * 1024], g),
                        waits=[(sS, sg)] if g == 0 else [], inc=sV)
                    ln_v.append(sV.n)
                ln_done = sV.n
                ev_f = {}
                for dsl in range(8):
                    bank = p[dsl % 2]
                    w0 = [(sV, ln_done), (sV, acc_fence)] if dsl == 0 else []
                    if dsl >= 2:
                        w0 = [(sV, ev_f[dsl - 2])]
                    last_t = None
                    for g in range(4):
                        last_t = P.add("tensor", (lambda bk, g_, d_: (lambda e: e.transpose(
                            bk[:, g_ * 128:(g_ + 1) * 128],
                            t_xt[:, g_ * 1024 + d_ * 128: g_ * 1024 + (d_ + 1) * 128],
                            t_id[:, :])))(bank, g, dsl),
                            waits=w0 if g == 0 else [], inc=sT)
                    P.add("vector", (lambda bk, d_: (lambda e: e.tensor_copy(
                        t_base[:, d_ * 512:(d_ + 1) * 512], bk[:, :])))(bank, dsl),
                        waits=[(sT, last_t), (SAC, accst_fence)], inc=sV)
                    ev_f[dsl] = sV.n
                s0tp_fence = sT.n
                base_done = sV.n

                for l in range(NLAY):
                    LD(LMW, t_dtw[:, :], d_wdt[l, :, :],
                       waits=[(sT, dtw_fence), (sT, xpw_fence)])
                    for k in range(16):
                        LD(LMW, t_xpw[:, k * 96:(k + 1) * 96], d_wxp[l, k, :, :])
                    mw_full = LMW.n
                    for dsl in range(8):
                        P.add("scalar", (lambda d_, i_: (lambda e: e.activation(
                            out=t_e[i_][:, :], in_=t_base[:, d_ * 512:(d_ + 1) * 512],
                            func=AF.Square)))(dsl, dsl % 2),
                            waits=[(sV, base_done)] if dsl < 2 else [(sT, sT.n)], inc=sS)
                        sq = sS.n
                        P.add("tensor", (lambda d_, i_: (lambda e: e.matmul(
                            p[2][0:1, :], t_ones[:, 0:1], t_e[i_][:, :],
                            start=(d_ == 0), stop=(d_ == 7))))(dsl, dsl % 2),
                            waits=[(sS, sq)], inc=sT)
                    rms_mm = sT.n
                    P.add("scalar", lambda e: e.activation(
                        out=t_r2[0:1, :], in_=p[2][0:1, :], func=AF.Ln,
                        bias=t_cst[0:1, 0:1], scale=1.0 / DM), waits=[(sT, rms_mm)], inc=sS)
                    P.add("scalar", lambda e: e.drain(), inc=sS)
                    P.add("scalar", lambda e: e.activation(
                        out=t_r2[0:1, :], in_=t_r2[0:1, :], func=AF.Exp, scale=-0.5), inc=sS)
                    rms_s = sS.n
                    r2st = ST(SR2, d_r2d, t_r2[0:1, :], waits=[(sS, rms_s), (LR2, r2_ld)])
                    r2_ld = LD(LR2, t_r2B[:, :], d_r2d[0:1, :].broadcast_to((128, 512)),
                               waits=[(SR2, r2st), (sV, r2B_used)])
                    for dsl in range(8):
                        P.add("vector", (lambda d_: (lambda e: e.tensor_tensor(
                            out=t_xt[:, d_ * 512:(d_ + 1) * 512],
                            in0=t_base[:, d_ * 512:(d_ + 1) * 512],
                            in1=t_r2B[:, :], op=OP.mult)))(dsl),
                            waits=[(LR2, r2_ld)] if dsl == 0 else [], inc=sV)
                        P.add("vector", (lambda d_, l_: (lambda e: e.tensor_scalar(
                            out=t_xt[:, d_ * 512:(d_ + 1) * 512],
                            in0=t_xt[:, d_ * 512:(d_ + 1) * 512],
                            scalar1=t_nw[l_][:, d_:d_ + 1], scalar2=None,
                            op0=OP.mult)))(dsl, l), inc=sV)
                    xnorm_done = sV.n
                    r2B_used = xnorm_done

                    consV = {}
                    consS = {}
                    wiT = {}
                    st_z = {}
                    st_xb = {}
                    for m in range(32):
                        pg = m % 2
                        lw = [(sT, wiT[m - 2])] if m >= 2 else \
                            [(sT, wi_fence[pg]), (sV, prev_ep[pg])]
                        wl = LD(LWI[pg], t_wi[pg][:, :], d_win[l, m, :, :], waits=lw)
                        bw = [(LWI[pg], wl)]
                        if m == 0:
                            bw.append((sV, xnorm_done))
                        if (m - 2) in consV:
                            bw.append((sV, consV[m - 2]))
                        if (m - 2) in consS:
                            bw.append((sS, consS[m - 2]))
                        for k in range(8):
                            P.add("tensor", (lambda pg_, k_: (lambda e: e.matmul(
                                p[pg_][:, :], t_wi[pg_][:, k_ * 128:(k_ + 1) * 128],
                                t_xt[:, k_ * 512:(k_ + 1) * 512],
                                start=(k_ == 0), stop=(k_ == 7))))(pg, k),
                                waits=bw if k == 0 else [], inc=sT)
                        wiT[m] = sT.n
                        wi_fence[pg] = sT.n
                        mm_done = sT.n
                        if m < 16:
                            slab = m
                            P.add("vector", (lambda pg_, s_, l_: (lambda e: e.tensor_scalar(
                                out=t_xc[pg_][:, :], in0=p[pg_][:, :],
                                scalar1=t_cw[l_][:, s_ * 4 + 3:s_ * 4 + 4], scalar2=None,
                                op0=OP.mult)))(pg, slab, l),
                                waits=[(sT, mm_done)], inc=sV)
                            for kk in (2, 1, 0):
                                sh = 3 - kk
                                P.add("vector", (lambda pg_, s_, l_, kk_, sh_: (lambda e:
                                    e.scalar_tensor_tensor(
                                        out=t_xc[pg_][:, sh_:512],
                                        in0=p[pg_][:, 0:512 - sh_],
                                        scalar=t_cw[l_][:, s_ * 4 + kk_:s_ * 4 + kk_ + 1],
                                        in1=t_xc[pg_][:, sh_:512],
                                        op0=OP.mult, op1=OP.add)))(pg, slab, l, kk, sh),
                                    inc=sV)
                                P.add("vector", (lambda pg_, s_, l_, kk_, sh_: (lambda e:
                                    e.scalar_tensor_tensor(
                                        out=t_xc[pg_][:, 0:sh_],
                                        in0=t_tail[l_][:, s_ * 3 + kk_:s_ * 3 + 3],
                                        scalar=t_cw[l_][:, s_ * 4 + kk_:s_ * 4 + kk_ + 1],
                                        in1=t_xc[pg_][:, 0:sh_],
                                        op0=OP.mult, op1=OP.add)))(pg, slab, l, kk, sh),
                                    inc=sV)
                            P.add("vector", (lambda pg_, s_, l_: (lambda e: e.tensor_copy(
                                t_tail[l_][:, s_ * 3:s_ * 3 + 3], p[pg_][:, 509:512])))(pg, slab, l),
                                inc=sV)
                            consV[m] = sV.n
                            conv_v = sV.n
                            P.add("scalar", (lambda pg_, s_, l_: (lambda e: e.activation(
                                out=t_xb[:, s_ * 512:(s_ + 1) * 512], in_=t_xc[pg_][:, :],
                                func=AF.Silu, bias=t_cb[l_][:, s_:s_ + 1], scale=1.0)))(pg, slab, l),
                                waits=[(sV, conv_v), (SXB, xbst_full_prev[l])], inc=sS)
                            consS[m] = sS.n
                            st_xb[slab] = ST(SXB, d_xbsp[l, c, slab, :, :],
                                             t_xb[:, slab * 512:(slab + 1) * 512],
                                             waits=[(sS, sS.n)])
                        else:
                            slab = m - 16
                            zw = [(sT, mm_done)]
                            zw.append((SZ[pg], st_z[slab - 2] if slab >= 2
                                       else stz_prev[l][slab % 2]))
                            P.add("scalar", (lambda pg_: (lambda e: e.activation(
                                out=t_sz[pg_][:, :], in_=p[pg_][:, :], func=AF.Silu)))(pg),
                                waits=zw, inc=sS)
                            consS[m] = sS.n
                            st_z[slab] = ST(SZ[pg], d_szsp[l, c, slab, :, :],
                                            t_sz[pg][:, :], waits=[(sS, sS.n)])
                            stz_prev[l][slab % 2] = st_z[slab]

                    xb_ok = consS[15]
                    xb_full = SXB.n
                    xbst_full_prev[l] = xb_full
                    for k in range(16):
                        P.add("tensor", (lambda k_, l_: (lambda e: e.matmul(
                            p[2][0:96, :], t_xpw[:, k_ * 96:(k_ + 1) * 96],
                            t_xb[:, k_ * 512:(k_ + 1) * 512],
                            start=(k_ == 0), stop=(k_ == 15))))(k, l),
                            waits=[(sS, xb_ok), (LMW, mw_full)] if k == 0 else [], inc=sT)
                    dbc_mm = sT.n
                    xpw_fence = dbc_mm
                    P.add("scalar", lambda e: e.activation(
                        out=t_dbc[0:96, :], in_=p[2][0:96, :], func=AF.Copy),
                        waits=[(sT, dbc_mm), (sG, sG.n)], inc=sS)
                    dbc_ev = sS.n
                    bc_st = ST(SBC, d_bc[16:32, :], t_dbc[80:96, :],
                               waits=[(sS, dbc_ev), (LCB, bc_ld)])
                    for uu in range(4):
                        P.add("tensor", (lambda uu_: (lambda e: e.transpose(
                            p[7][:, uu_ * 16:(uu_ + 1) * 16],
                            t_dbc[64:80, uu_ * 128:(uu_ + 1) * 128],
                            t_id[64:80, 64:80])))(uu),
                            waits=[(sS, dbc_ev)] if uu == 0 else [], inc=sT)
                    btp = sT.n
                    P.add("scalar", lambda e: e.activation(
                        out=t_BT[:, :], in_=p[7][:, 0:64], func=AF.Copy),
                        waits=[(sT, btp)], inc=sS)
                    btl = sS.n
                    dt_s = {}
                    for m in range(16):
                        pg = m % 2
                        w = [(sS, dbc_ev)]
                        if m >= 2:
                            w.append((sS, dt_s[m - 2]))
                        P.add("tensor", (lambda pg_, m_, l_: (lambda e: e.matmul(
                            p[pg_][:, :], t_dtw[:, m_ * 128:(m_ + 1) * 128],
                            t_dbc[0:64, :], start=True, stop=True)))(pg, m, l),
                            waits=w + ([(LMW, mw_full)] if m == 0 else []), inc=sT)
                        mm_t = sT.n
                        dtw_fence = mm_t
                        P.add("scalar", (lambda pg_, m_, l_: (lambda e: e.activation(
                            out=t_e[pg_][:, :], in_=p[pg_][:, :], func=AF.Exp,
                            bias=t_dtb[l_][:, m_:m_ + 1], scale=1.0)))(pg, m, l),
                            waits=[(sT, mm_t)], inc=sS)
                        P.add("scalar", lambda e: e.drain(), inc=sS)
                        P.add("scalar", (lambda pg_, m_: (lambda e: e.activation(
                            out=t_dl[:, m_ * 512:(m_ + 1) * 512], in_=t_e[pg_][:, :],
                            func=AF.Ln, bias=t_cst[:, 1:2], scale=1.0)))(pg, m), inc=sS)
                        dt_s[m] = sS.n
                        P.add("vector", (lambda m_: (lambda e: e.tensor_tensor(
                            out=t_xb[:, m_ * 512:(m_ + 1) * 512],
                            in0=t_dl[:, m_ * 512:(m_ + 1) * 512],
                            in1=t_xb[:, m_ * 512:(m_ + 1) * 512], op=OP.mult)))(m),
                            waits=[(sS, dt_s[m]), (SXB, xb_full)], inc=sV)
                    w_done = sV.n

                    wtl = {}
                    rdl = {}
                    for r in range(4):
                        b0 = (r % 2) * 4
                        tps = {}
                        for i in range(4):
                            s = 4 * r + i
                            tw = []
                            if r == 0 and i == 0:
                                tw.append((sV, w_done))
                            if r == 1 and i == 3:
                                tw.append((sS, btl))
                            if r >= 2:
                                tw.append((sS, rdl[r - 2]))
                            for uu in range(4):
                                P.add("tensor", (lambda s_, uu_, b_: (lambda e: e.transpose(
                                    p[b_][:, uu_ * 128:(uu_ + 1) * 128],
                                    t_xb[:, s_ * 512 + uu_ * 128: s_ * 512 + (uu_ + 1) * 128],
                                    t_id[:, :])))(s, uu, b0 + i),
                                    waits=tw if uu == 0 else [], inc=sT)
                            tps[i] = sT.n
                        for i in range(4):
                            s = 4 * r + i
                            P.add("scalar", (lambda s_, b_: (lambda e: e.activation(
                                out=t_xb[:, s_ * 512:(s_ + 1) * 512], in_=p[b_][:, :],
                                func=AF.Copy)))(s, b0 + i),
                                waits=[(sT, tps[i])], inc=sS)
                            wtl[s] = sS.n
                        rdl[r] = sS.n

                    vmm_last = [prev_vmm_f]
                    vml = [prev_vmm_f, prev_vmm_f]
                    epd_v = {}
                    for hf in range(2):
                        s0 = hf * SLAB_H
                        hsc = {}
                        dA_c = {}
                        xv_c = {}
                        hc_c = {}
                        xh_n = {}
                        cb_n = {}
                        vmm = {}
                        land_c = {}
                        rhs_c = {}

                        def load_cb(n, hf=hf, cb_n=cb_n, hc_c=hc_c):
                            par = n % 2
                            gw = [(SBC, bc_st)]
                            if n >= 2:
                                gw.append((sV, hc_c[n - 2]))
                            elif hf == 1:
                                gw.append((sV, prev_half_hc))
                            cb_n[n] = LD(LCB, t_Cb[:, par * 512:(par + 1) * 512],
                                         d_bc[16 + n:17 + n, :].broadcast_to((128, 512)),
                                         waits=gw)

                        def emit_rhs_uu(n, uu, rhs_c=rhs_c):
                            w0 = [(sT, vml[n % 2])] if uu == 0 else []
                            pb = (n % 2) * 1280
                            P.add("scalar", (lambda n_, uu_, pb_: (lambda e: e.activation(
                                out=t_rhs[:, pb_ + RHS_OFF[uu_]:
                                          pb_ + RHS_OFF[uu_] + 512 - 128 * uu_],
                                in_=t_tb0[:, 0:512 - 128 * uu_], func=AF.Copy,
                                scale=t_BT[:, uu_ * 16 + n_:uu_ * 16 + n_ + 1])))(n, uu, pb),
                                waits=w0, inc=sS)
                            rhs_c[n] = sS.n

                        def emit_rhs(n):
                            for uu in range(4):
                                emit_rhs_uu(n, uu)

                        def emit_vmm(n, hf=hf, s0=s0, vmm=vmm, land_c=land_c,
                                     rhs_c=rhs_c):
                            pb = (n % 2) * 1280
                            for s in range(SLAB_H):
                                slab = s0 + s
                                for uu in range(4):
                                    mw = []
                                    if uu == 0:
                                        mw.append((sS, rhs_c[n]))
                                        mw.append((sV, land_c[(n - 1, s)] if n > 0
                                                   else prev_land[s]))
                                    P.add("tensor", (lambda s_, slab_, uu_, pb_: (lambda e: e.matmul(
                                        p[s_][:, 128 * uu_:512],
                                        t_xb[:, slab_ * 512 + uu_ * 128: slab_ * 512 + (uu_ + 1) * 128],
                                        t_rhs[:, pb_ + RHS_OFF[uu_]:
                                              pb_ + RHS_OFF[uu_] + 512 - 128 * uu_],
                                        start=(uu_ == 0), stop=(uu_ == 3))))(s, slab, uu, pb),
                                        waits=mw, inc=sT)
                                vmm[(n, s)] = sT.n
                            vml[n % 2] = sT.n
                            vmm_last[0] = sT.n

                        def emit_land(n, hf=hf, s0=s0, l=l, vmm=vmm,
                                      land_c=land_c, xv_c=xv_c):
                            for s in range(SLAB_H):
                                slab = s0 + s
                                lw = [(sT, vmm[(n, s)])]
                                if n > 0:
                                    lw.append((sS, xv_c[(n - 1, s)]))
                                else:
                                    lw.append((sS, prev_xv[s]))
                                P.add("vector", (lambda s_, slab_, l_, n_: (lambda e:
                                    e.scalar_tensor_tensor(
                                        out=t_vw[:, s_ * 512:(s_ + 1) * 512],
                                        in0=t_bpw[:, :],
                                        scalar=t_vst[l_][:, slab_ * 16 + n_:slab_ * 16 + n_ + 1],
                                        in1=p[s_][:, :], op0=OP.mult,
                                        op1=OP.add)))(s, slab, l, n),
                                    waits=lw, inc=sV)
                                land_c[(n, s)] = sV.n
                                P.add("scalar", (lambda s_, slab_, l_, n_: (lambda e: e.activation(
                                    out=t_vst[l_][:, slab_ * 16 + n_:slab_ * 16 + n_ + 1],
                                    in_=t_vw[:, s_ * 512 + 511:s_ * 512 + 512],
                                    func=AF.Copy)))(s, slab, l, n),
                                    waits=[(sV, land_c[(n, s)])], inc=sS)
                                xv_c[(n, s)] = sS.n

                        load_cb(0)
                        load_cb(1)
                        emit_rhs(0)
                        emit_vmm(0)
                        emit_land(0)
                        for n in range(16):
                            par = n % 2
                            if n + 1 < 16:
                                emit_rhs(n + 1)
                                emit_vmm(n + 1)
                            def emit_dA(n_, s_, hf=hf, s0=s0, l=l,
                                        hsc=hsc, dA_c=dA_c):
                                slab_ = s0 + s_
                                sw = []
                                if s_ >= 3:
                                    sw.append((sV, hsc[(n_, s_ - 3)]))
                                elif n_ > 0:
                                    sw.append((sV, hsc[(n_ - 1, SLAB_H - 3 + s_)]))
                                else:
                                    sw.append((sV, prev_hsc[SLAB_H - 3 + s_]))
                                P.add("scalar", (lambda sx, sl, lx, nx: (lambda e: e.activation(
                                    out=t_dA[sx % 3][:, :],
                                    in_=t_dl[:, sl * 512:(sl + 1) * 512],
                                    func=AF.Exp,
                                    scale=t_negA[lx][:, sl * 16 + nx:sl * 16 + nx + 1])))(s_, slab_, l, n_),
                                    waits=sw, inc=sS)
                                dA_c[(n_, s_)] = sS.n

                            for s in range(SLAB_H):
                                slab = s0 + s
                                if n == 0 or s >= 3:
                                    emit_dA(n, s)
                                P.add("vector", (lambda s_, slab_, l_, n_: (lambda e: e.tensor_tensor_scan(
                                    out=t_vh[:, s_ * 512:(s_ + 1) * 512],
                                    data0=t_dA[s_ % 3][:, :],
                                    data1=t_vw[:, s_ * 512:(s_ + 1) * 512],
                                    initial=t_hst[l_][:, slab_ * 16 + n_:slab_ * 16 + n_ + 1],
                                    op0=OP.mult, op1=OP.add)))(s, slab, l, n),
                                    waits=[(sS, dA_c[(n, s)])], inc=sV)
                                hsc[(n, s)] = sV.n
                            if n + 1 < 16:
                                emit_dA(n + 1, 0)
                            P.add("scalar", (lambda l_, n_, s0_: (lambda e: e.activation(
                                out=t_hst[l_][:, s0_ * 16 + n_: s0_ * 16 + n_ + 113: 16],
                                in_=t_vh[:, 511::512], func=AF.Copy)))(l, n, s0),
                                waits=[(sV, hsc[(n, SLAB_H - 1)])], inc=sS)
                            xh_n[n] = sS.n
                            if n + 1 < 16:
                                emit_dA(n + 1, 1)
                                emit_dA(n + 1, 2)
                            vw = [(LCB, cb_n[n]), (sS, xh_n[n])]
                            if n == 0:
                                P.add("vector", (lambda par_: (lambda e: e.tensor_tensor(
                                    out=t_y[:, :].rearrange("p (r f) -> p r f", r=SLAB_H),
                                    in0=t_vh[:, :].rearrange("p (r f) -> p r f", r=SLAB_H),
                                    in1=t_Cb[:, par_ * 512:(par_ + 1) * 512].unsqueeze(1)
                                    .broadcast_to((128, SLAB_H, 512)),
                                    op=OP.mult)))(par),
                                    waits=vw, inc=sV)
                                hc_c[n] = sV.n
                                yac_w = sV.n
                            else:
                                P.add("vector", (lambda par_: (lambda e: e.tensor_tensor(
                                    out=t_vh[:, :].rearrange("p (r f) -> p r f", r=SLAB_H),
                                    in0=t_vh[:, :].rearrange("p (r f) -> p r f", r=SLAB_H),
                                    in1=t_Cb[:, par_ * 512:(par_ + 1) * 512].unsqueeze(1)
                                    .broadcast_to((128, SLAB_H, 512)),
                                    op=OP.mult)))(par),
                                    waits=vw, inc=sV)
                                hc_c[n] = sV.n
                                P.add("vector", lambda e: e.tensor_tensor(
                                    out=t_y[:, :], in0=t_y[:, :],
                                    in1=t_vh[:, :], op=OP.add), inc=sV)
                                yac_w = sV.n
                            if n + 1 < 16:
                                emit_land(n + 1)
                            if n + 2 < 16:
                                load_cb(n + 2)
                        prev_half_hc = hc_c[15]
                        prev_land = {s: land_c[(15, s)] for s in range(SLAB_H)}
                        prev_xv = {s: xv_c[(15, s)] for s in range(SLAB_H)}
                        prev_hsc = {s: hsc[(15, s)] for s in range(SLAB_H)}
                        xh_fence = sS.n

                        ep_v = {}
                        for s in range(SLAB_H):
                            slab = s0 + s
                            pg = s % 2
                            w = [(sV, ep_v[s - 2] if s >= 2 else prev_ep[s % 2]),
                                 (sT, wi_fence[pg]), (SXB, xb_full),
                                 (SZ[pg], st_z[14 + pg])]
                            rx = LD(LWI[pg], t_xbr[pg], d_xbsp[l, c, slab, :, :],
                                    waits=w)
                            rz = LD(LWI[pg], t_szr[pg], d_szsp[l, c, slab, :, :])
                            y2w = [(LWI[pg], rz)]
                            if s == 0:
                                y2w.append((sV, yac_w))
                                y2w.append((sS, xh_n[15]))
                            P.add("vector", (lambda pg_, s_, l_, slab_: (lambda e:
                                e.scalar_tensor_tensor(
                                    out=t_y2[pg_][:, :], in0=t_xbr[pg_],
                                    scalar=t_dp[l_][:, slab_:slab_ + 1],
                                    in1=t_y[:, s_ * 512:(s_ + 1) * 512],
                                    op0=OP.mult, op1=OP.add)))(pg, s, l, slab),
                                waits=y2w, inc=sV)
                            P.add("vector", (lambda pg_, slab_: (lambda e: e.tensor_tensor(
                                out=t_dl[:, slab_ * 512:(slab_ + 1) * 512],
                                in0=t_y2[pg_][:, :], in1=t_szr[pg_],
                                op=OP.mult)))(pg, slab), inc=sV)
                            ep_v[s] = sV.n
                            epd_v[slab] = sV.n
                        prev_ep = {0: ep_v[6], 1: ep_v[7]}
                    prev_vmm_f = vmm_last[0]

                    ow_T_l = {}
                    for slab in range(16):
                        pg = slab % 2
                        oww = [(sT, ow_T_l[slab - 2])] if slab >= 2 else \
                            [(sT, wo_fence[pg])]
                        owl = LD(LWO[pg], t_wo[pg][:, :], d_wout[l, slab, :, :],
                                 waits=oww)
                        mw = [(LWO[pg], owl), (sV, epd_v[slab])]
                        if slab == 0:
                            mw.append((sV, prev_land[SLAB_H - 1]))
                        for m in range(8):
                            P.add("tensor", (lambda pg_, m_, slab_: (lambda e: e.matmul(
                                p[m_][:, :], t_wo[pg_][:, m_ * 128:(m_ + 1) * 128],
                                t_dl[:, slab_ * 512:(slab_ + 1) * 512],
                                start=(slab_ == 0), stop=(slab_ == 15))))(pg, m, slab),
                                waits=mw if m == 0 else [], inc=sT)
                        ow_T_l[slab] = sT.n
                        wo_fence[pg] = sT.n
                    op_T = sT.n
                    for m in range(8):
                        P.add("vector", (lambda m_: (lambda e: e.tensor_tensor(
                            out=t_base[:, m_ * 512:(m_ + 1) * 512],
                            in0=t_base[:, m_ * 512:(m_ + 1) * 512],
                            in1=p[m_][:, :], op=OP.add)))(m),
                            waits=[(sT, op_T)] if m == 0 else [], inc=sV)
                    base_done = sV.n

                last_t = None
                for tg in range(4):
                    for dsl in range(8):
                        bk = p[2 * tg + dsl // 4]
                        last_t = P.add("tensor", (lambda bk_, d_, tg_: (lambda e: e.transpose(
                            bk_[:, (d_ % 4) * 128:((d_ % 4) + 1) * 128],
                            t_base[:, d_ * 512 + tg_ * 128: d_ * 512 + (tg_ + 1) * 128],
                            t_id[:, :])))(bk, dsl, tg),
                            waits=[(sV, base_done)] if (tg == 0 and dsl == 0) else [],
                            inc=sT)
                acc_v = {}
                acc_st = {}
                for tg in range(4):
                    pg = tg % 2
                    xw = [(sT, op_T if tg < 2 else last_t)]
                    if tg >= 2:
                        xw.append((sV, acc_v[tg - 2]))
                    rx2 = LD(LXP[pg], t_xt[:, pg * 1024:(pg + 1) * 1024],
                             d_x[c * 512 + tg * 128: c * 512 + (tg + 1) * 128, :],
                             waits=xw)
                    aw = [(sT, last_t), (LXP[pg], rx2), (SAC, accst_fence)]
                    P.add("vector", (lambda pg_, tg_: (lambda e: e.tensor_tensor(
                        out=t_base[:, tg_ * 1024: tg_ * 1024 + 512],
                        in0=p[2 * tg_][:, :],
                        in1=t_xt[:, pg_ * 1024: pg_ * 1024 + 512], op=OP.add)))(pg, tg),
                        waits=aw, inc=sV)
                    P.add("vector", (lambda pg_, tg_: (lambda e: e.tensor_tensor(
                        out=t_base[:, tg_ * 1024 + 512: (tg_ + 1) * 1024],
                        in0=p[2 * tg_ + 1][:, :],
                        in1=t_xt[:, pg_ * 1024 + 512: (pg_ + 1) * 1024], op=OP.add)))(pg, tg),
                        inc=sV)
                    acc_v[tg] = sV.n
                    acc_st[tg] = ST(SAC, d_acc[c * 512 + tg * 128: c * 512 + (tg + 1) * 128, :],
                                    t_base[:, tg * 1024:(tg + 1) * 1024],
                                    waits=[(sV, acc_v[tg])])
                acc_fence = acc_v[3]
                accst_fence = acc_st[3]
                prev_opT = op_T

            A_V, A_S, A_T, A_G = sV.n, sS.n, sT.n, sG.n

        barrier = ([(sV, A_V), (sS, A_S), (sT, A_T), (sG, A_G)]
                   + [(s, s.n) for s in ALL_DMA_SEMS if s.n > 0])

        with ExitStack() as bctx:
            BE = bctx.enter_context
            q0 = [BE(nc.psum_tensor(f"q0{i}", [128, 512], F32)) for i in range(2)]
            qs = BE(nc.psum_tensor("qs", [128, 512], F32))
            pQ = [BE(nc.psum_tensor(f"pQ{i}", [128, 1024], BF16)) for i in range(2)]
            t_w1q = [BE(nc.sbuf_tensor(f"t_w1q{i}", [128, 1024], BF16)) for i in range(32)]
            t_wp = [BE(nc.sbuf_tensor(f"t_wp{i}", [128, 1024], F32)) for i in range(2)]
            t_sgn = BE(nc.sbuf_tensor("t_sgn", [128, 1024], F32))
            t_absv = BE(nc.sbuf_tensor("t_absv", [128, 1024], F32))
            t_absb = BE(nc.sbuf_tensor("t_absb", [128, 33], F32))
            t_sc = BE(nc.sbuf_tensor("t_sc", [1, 8], F32))
            t_thB = BE(nc.sbuf_tensor("t_thB", [128, 2], F32))
            t_xt2 = BE(nc.sbuf_tensor("t_xt2", [128, 2048], F32))
            t_bn2 = BE(nc.sbuf_tensor("t_bn2", [128, 24], F32))
            t_mv2 = BE(nc.sbuf_tensor("t_mv2", [128, 4], F32))
            t_rc2 = BE(nc.sbuf_tensor("t_rc2", [128, 8], F32))
            t_qf = BE(nc.sbuf_tensor("t_qf", [128, 1024], F32))
            t_qi = BE(nc.sbuf_tensor("t_qi", [128, 1024], I32))
            t_qb = BE(nc.sbuf_tensor("t_qb", [128, 2048], BF16))
            t_xqf = BE(nc.sbuf_tensor("t_xqf", [128, 8 * 256], BF16))
            t_s1r = BE(nc.sbuf_tensor("t_s1r", [2, 128], F32))
            t_s1B = BE(nc.sbuf_tensor("t_s1B", [128, 256], F32))
            t_gp = [BE(nc.sbuf_tensor(f"t_gp{i}", [128, 256], F32)) for i in range(2)]
            t_b1 = BE(nc.sbuf_tensor("t_b1", [128, 32], F32))

            LD(LMW, t_b1[:, :], d_b1, waits=barrier)

            w1red = {}
            for i in range(32):
                w = [(sV, w1red[i - 2])] if i >= 2 else []
                wl = LD(LWI[i % 2], t_wp[i % 2][:, :], d_w1[i, :, :], waits=w)
                P.add("vector", (lambda i_: (lambda e: e.tensor_reduce(
                    out=t_absb[:, i_:i_ + 1], in_=t_wp[i_ % 2][:, :], axis=AX.X,
                    op=OP.add, apply_absolute_value=True)))(i),
                    waits=[(LWI[i % 2], wl)] + (barrier if i == 0 else []), inc=sV)
                w1red[i] = sV.n
            P.add("vector", lambda e: e.drain(), inc=sV)
            P.add("vector", lambda e: e.tensor_reduce(
                out=t_absb[:, 32:33], in_=t_absb[:, 0:32], axis=AX.X, op=OP.add), inc=sV)
            vv = sV.n
            P.add("tensor", lambda e: e.matmul(
                qs[0:1, 0:1], t_ones[:, 0:1], t_absb[:, 32:33], start=True, stop=True),
                waits=[(sV, vv)] + barrier, inc=sT)
            wsT = sT.n
            P.add("vector", lambda e: e.tensor_scalar(
                out=t_sc[0:1, 0:1], in0=qs[0:1, 0:1], scalar1=float(2.0 ** -22),
                scalar2=EPS, op0=OP.mult, op1=OP.max), waits=[(sT, wsT)], inc=sV)
            P.add("vector", lambda e: e.tensor_scalar(
                out=t_sc[0:1, 1:2], in0=qs[0:1, 0:1], scalar1=float(2.0 ** -23),
                scalar2=EPS * 0.5, op0=OP.mult, op1=OP.max), inc=sV)
            P.add("vector", lambda e: e.tensor_scalar(
                out=t_sc[0:1, 2:3], in0=qs[0:1, 0:1], scalar1=float(2.0 ** -22 / 127.0),
                scalar2=EPS / 127.0, op0=OP.mult, op1=OP.max), inc=sV)
            P.add("vector", lambda e: e.drain(), inc=sV)
            scv = sV.n
            th_st = ST(STH, d_scd[0:1, 0:3], t_sc[0:1, 0:3], waits=[(sV, scv)])
            th_ld = LD(LTH, t_thB[:, 0:2], d_scd[0:1, 1:3].broadcast_to((128, 2)),
                       waits=[(STH, th_st)])
            thg = th_ld
            w1q_v = {}
            sgn_f = 0
            for i in range(32):
                w = [(sV, w1q_v[i - 2])] if i >= 2 else [(sV, w1red[31])]
                rl = LD(LWI[i % 2], t_wp[i % 2][:, :], d_w1[i, :, :], waits=w)
                P.add("scalar", (lambda i_: (lambda e: e.activation(
                    out=t_sgn[:, :], in_=t_wp[i_ % 2][:, :], func=AF.Sign)))(i),
                    waits=[(LWI[i % 2], rl), (sV, sgn_f)] + (barrier if i == 0 else []), inc=sS)
                P.add("scalar", (lambda i_: (lambda e: e.activation(
                    out=t_absv[:, :], in_=t_wp[i_ % 2][:, :], func=AF.Abs)))(i), inc=sS)
                ssn = sS.n
                P.add("vector", lambda e: e.tensor_scalar(
                    out=t_absv[:, :], in0=t_absv[:, :], scalar1=t_thB[:, 0:1],
                    scalar2=None, op0=OP.is_gt), waits=[(sS, ssn), (LTH, thg)], inc=sV)
                P.add("vector", (lambda i_: (lambda e: e.tensor_tensor(
                    out=t_w1q[i_][:, :], in0=t_absv[:, :], in1=t_sgn[:, :],
                    op=OP.mult)))(i), inc=sV)
                w1q_v[i] = sV.n
                sgn_f = sV.n
            w1q_done = sV.n

            prev_q_done = 0
            prev_sgn = 0
            s1_ldf = 0
            s1B_used = 0
            prev_g = {30: 0, 31: 0}
            prev_tq = 0
            prev_gst = {30: 0, 31: 0}
            for c in range(NBC):
                for g in range(2):
                    LD(LX, t_xt2[:, g * 1024:(g + 1) * 1024],
                       d_x[c * 256 + g * 128: c * 256 + (g + 1) * 128, :],
                       waits=([(sV, prev_q_done)] if g == 0 else []))
                x2_full = LX.n
                qv = {}
                for g in range(2):
                    w = [(LX, x2_full)] if g == 0 else []
                    if c == 0 and g == 0:
                        w.append((sV, w1q_done))
                    P.add("vector", (lambda a, g_: (lambda e: e.bn_stats(
                        t_bn2[:, 12 * g_:12 * g_ + 6], a)))(
                        t_xt2[:, g * 1024:g * 1024 + 512], g), waits=w, inc=sV)
                    P.add("vector", (lambda a, g_: (lambda e: e.bn_stats(
                        t_bn2[:, 12 * g_ + 6:12 * g_ + 12], a)))(
                        t_xt2[:, g * 1024 + 512:(g + 1) * 1024], g), inc=sV)
                for g in range(2):
                    P.add("vector", (lambda g_: (lambda e: e.bn_aggr(
                        t_mv2[:, 2 * g_:2 * g_ + 2], t_bn2[:, 12 * g_:12 * g_ + 12])))(g),
                        inc=sV)
                vg = sV.n
                for g in range(2):
                    P.add("scalar", (lambda g_: (lambda e: e.activation(
                        out=t_rc2[:, g_:g_ + 1], in_=t_mv2[:, 2 * g_ + 1:2 * g_ + 2],
                        func=AF.Ln, bias=t_cst[:, 0:1], scale=1.0)))(g),
                        waits=[(sV, vg)] if g == 0 else [], inc=sS)
                P.add("scalar", lambda e: e.drain(), inc=sS)
                for g in range(2):
                    P.add("scalar", (lambda g_: (lambda e: e.activation(
                        out=t_rc2[:, g_:g_ + 1], in_=t_rc2[:, g_:g_ + 1],
                        func=AF.Exp, scale=-0.5)))(g), inc=sS)
                sg = sS.n
                for g in range(2):
                    P.add("vector", (lambda a, g_: (lambda e: e.tensor_scalar(
                        out=a, in0=a, scalar1=t_mv2[:, 2 * g_:2 * g_ + 1],
                        scalar2=t_rc2[:, g_:g_ + 1], op0=OP.subtract, op1=OP.mult)))(
                        t_xt2[:, g * 1024:(g + 1) * 1024], g),
                        waits=[(sS, sg)] if g == 0 else [], inc=sV)
                for g in range(2):
                    P.add("vector", (lambda a, g_: (lambda e: e.tensor_reduce(
                        out=t_rc2[:, 2 + g_:3 + g_], in_=a, axis=AX.X, op=OP.max,
                        apply_absolute_value=True)))(
                        t_xt2[:, g * 1024:(g + 1) * 1024], g), inc=sV)
                P.add("vector", lambda e: e.drain(), inc=sV)
                for g in range(2):
                    P.add("vector", (lambda g_: (lambda e: e.tensor_scalar(
                        out=t_rc2[:, 2 + g_:3 + g_], in0=t_rc2[:, 2 + g_:3 + g_],
                        scalar1=EPS, scalar2=None, op0=OP.max)))(g), inc=sV)
                P.add("vector", lambda e: e.drain(), inc=sV)
                for g in range(2):
                    P.add("vector", (lambda g_: (lambda e: e.reciprocal(
                        out=t_rc2[:, 4 + g_:5 + g_], in_=t_rc2[:, 2 + g_:3 + g_])))(g), inc=sV)
                P.add("vector", lambda e: e.drain(), inc=sV)
                for g in range(2):
                    P.add("vector", (lambda g_: (lambda e: e.tensor_scalar(
                        out=t_rc2[:, 4 + g_:5 + g_], in0=t_rc2[:, 4 + g_:5 + g_],
                        scalar1=127.0, scalar2=None, op0=OP.mult)))(g), inc=sV)
                for g in range(2):
                    P.add("vector", (lambda g_: (lambda e: e.tensor_scalar(
                        out=t_rc2[:, 6 + g_:7 + g_], in0=t_rc2[:, 2 + g_:3 + g_],
                        scalar1=t_thB[:, 1:2], scalar2=None, op0=OP.mult)))(g), inc=sV)
                P.add("vector", lambda e: e.drain(), inc=sV)
                vq0 = sV.n
                for g in range(2):
                    gsl = t_xt2[:, g * 1024:(g + 1) * 1024]
                    P.add("vector", (lambda a, g_: (lambda e: e.tensor_scalar(
                        out=t_qf[:, :], in0=a, scalar1=t_rc2[:, 4 + g_:5 + g_],
                        scalar2=None, op0=OP.mult)))(gsl, g),
                        waits=[(sV, vq0)] if g == 0 else [], inc=sV)
                    P.add("vector", lambda e: e.tensor_copy(t_qi[:, :], t_qf[:, :]), inc=sV)
                    P.add("vector", lambda e: e.tensor_tensor(
                        out=t_sgn[:, :], in0=t_qf[:, :], in1=t_qi[:, :],
                        op=OP.subtract), inc=sV)
                    P.add("vector", lambda e: e.scalar_tensor_tensor(
                        out=t_qf[:, :], in0=t_sgn[:, :], scalar=0.5, in1=t_qi[:, :],
                        op0=OP.is_gt, op1=OP.add), inc=sV)
                    P.add("vector", lambda e: e.tensor_scalar(
                        out=t_sgn[:, :], in0=t_sgn[:, :], scalar1=-0.5, scalar2=None,
                        op0=OP.is_lt), inc=sV)
                    P.add("vector", lambda e: e.tensor_tensor(
                        out=t_qf[:, :], in0=t_qf[:, :], in1=t_sgn[:, :],
                        op=OP.subtract), inc=sV)
                    P.add("vector", (lambda g_: (lambda e: e.tensor_scalar(
                        out=t_qb[:, g_ * 1024:(g_ + 1) * 1024], in0=t_qf[:, :],
                        scalar1=127.0, scalar2=-128.0, op0=OP.min, op1=OP.max)))(g), inc=sV)
                    qv[g] = sV.n
                q_done = sV.n
                prev_q_done = q_done
                prev_sgn = sS.n
                P.add("tensor", lambda e: e.transpose(
                    qs[0:2, 0:128], t_rc2[:, 6:8], t_id[:, :]),
                    waits=[(sV, q_done), (sV, prev_s1e if c > 0 else 0)], inc=sT)
                s1t = sT.n
                P.add("vector", lambda e: e.tensor_copy(t_s1r[:, :], qs[0:2, 0:128]),
                      waits=[(sT, s1t)], inc=sV)
                s1e = sV.n
                prev_s1e = s1e
                s1_st = ST(SS1, d_rw1, t_s1r[:, :], waits=[(sV, s1e), (LS1, s1_ldf)])
                LD(LS1, t_s1B[:, 0:128], d_rw1[0:1, :].broadcast_to((128, 128)),
                   waits=[(SS1, s1_st), (sV, s1B_used)])
                s1_ldf = LD(LS1, t_s1B[:, 128:256], d_rw1[1:2, :].broadcast_to((128, 128)))
                s1g = s1_ldf
                ev_v = {}
                for hp in range(2):
                    bank = pQ[hp]
                    last_t = None
                    for dd in range(4):
                        dsl = hp * 4 + dd
                        for g in range(2):
                            w = []
                            if dd == 0 and g == 0:
                                w = [(sV, q_done)]
                                if c > 0:
                                    w.append((sV, prev_ev[hp]))
                            last_t = P.add("tensor", (lambda bk, d_, g_: (lambda e: e.transpose(
                                bk[:, (2 * (d_ % 4) + g_) * 128:(2 * (d_ % 4) + g_ + 1) * 128],
                                t_qb[:, g_ * 1024 + d_ * 128: g_ * 1024 + (d_ + 1) * 128],
                                t_idb[:, :])))(bank, dsl, g), waits=w, inc=sT)
                    for dd in range(4):
                        dsl = hp * 4 + dd
                        for g in range(2):
                            P.add("vector", (lambda bk, d_, g_: (lambda e: e.tensor_copy(
                                t_xqf[:, d_ * 256 + g_ * 128: d_ * 256 + (g_ + 1) * 128],
                                bk[:, (2 * (d_ % 4) + g_) * 128:(2 * (d_ % 4) + g_ + 1) * 128])))(bank, dsl, g),
                                waits=[(sT, last_t)] if (dd == 0 and g == 0) else [], inc=sV)
                    ev_v[hp] = sV.n
                prev_ev = ev_v
                xq_ev = sV.n
                g_done = {}
                gst = {}
                for m in range(32):
                    pg = m % 2
                    w = [(sV, xq_ev), (LS1, s1g)]
                    if m >= 2:
                        w.append((sS, g_done[m - 2]))
                    else:
                        w.append((sS, prev_g[30 + m]))
                    for k in range(8):
                        P.add("tensor", (lambda pg_, k_, m_: (lambda e: e.matmul(
                            q0[pg_][:, 0:256], t_w1q[m_][:, k_ * 128:(k_ + 1) * 128],
                            t_xqf[:, k_ * 256:(k_ + 1) * 256],
                            start=(k_ == 0), stop=(k_ == 7))))(pg, k, m),
                            waits=w if k == 0 else [], inc=sT)
                    mmt = sT.n
                    vw = [(sT, mmt)]
                    vw.append((SGP[pg], gst[m - 2] if m >= 2 else prev_gst[30 + m]))
                    P.add("vector", (lambda pg_: (lambda e: e.tensor_tensor(
                        out=t_gp[pg_][:, :], in0=q0[pg_][:, 0:256], in1=t_s1B[:, :],
                        op=OP.mult)))(pg), waits=vw, inc=sV)
                    s1B_used = sV.n
                    vt = sV.n
                    P.add("scalar", (lambda pg_, m_: (lambda e: e.activation(
                        out=t_gp[pg_][:, :], in_=t_gp[pg_][:, :], func=AF.Gelu,
                        bias=t_b1[:, m_:m_ + 1], scale=1.0)))(pg, m),
                        waits=[(sV, vt)], inc=sS)
                    g_done[m] = sS.n
                    gst[m] = ST(SGP[pg], d_gel[c, m, :, :], t_gp[pg][:, :],
                                waits=[(sS, g_done[m])])
                prev_mm1 = sT.n
                prev_g = {30: g_done[30], 31: g_done[31]}
                prev_gst = {30: gst[30], 31: gst[31]}

            B1_V, B1_S, B1_T, B1_G = sV.n, sS.n, sT.n, sG.n

        barrier2 = ([(sV, B1_V), (sS, B1_S), (sT, B1_T), (sG, B1_G)]
                    + [(s, s.n) for s in ALL_DMA_SEMS if s.n > 0])

        with ExitStack() as b2ctx:
            BE2 = b2ctx.enter_context
            pm2 = [BE2(nc.psum_tensor(f"pm2{i}", [128, 512], F32)) for i in range(2)]
            pt2 = [BE2(nc.psum_tensor(f"pt2{i}", [128, 512], F32)) for i in range(2)]
            po2 = [BE2(nc.psum_tensor(f"po2{i}", [128, 1024], F32)) for i in range(2)]
            t_w2q = [BE2(nc.sbuf_tensor(f"t_w2q{i}", [128, 4096], BF16)) for i in range(8)]
            t_wp2 = [BE2(nc.sbuf_tensor(f"t_wp2{i}", [128, 4096], F32)) for i in range(2)]
            t_sgn2 = BE2(nc.sbuf_tensor("t_sgn2", [128, 4096], F32))
            t_abs2 = BE2(nc.sbuf_tensor("t_abs2", [128, 4096], F32))
            t_ab2 = BE2(nc.sbuf_tensor("t_ab2", [128, 9], F32))
            t_sc2 = BE2(nc.sbuf_tensor("t_sc2", [1, 4], F32))
            t_th2 = BE2(nc.sbuf_tensor("t_th2", [128, 2], F32))
            t_gel = BE2(nc.sbuf_tensor("t_gel", [128, 32 * 256], F32))
            t_mx = BE2(nc.sbuf_tensor("t_mx", [128, 20], F32))
            t_c2r = BE2(nc.sbuf_tensor("t_c2r", [2, 128], F32))
            t_c2B = BE2(nc.sbuf_tensor("t_c2B", [128, 256], F32))
            t_s2B = BE2(nc.sbuf_tensor("t_s2B", [128, 256], F32))
            t_sgn3 = BE2(nc.sbuf_tensor("t_sgn3", [128, 256], F32))
            t_qf2 = BE2(nc.sbuf_tensor("t_qf2", [128, 256], F32))
            t_qi2 = BE2(nc.sbuf_tensor("t_qi2", [128, 256], I32))
            t_xq2 = BE2(nc.sbuf_tensor("t_xq2", [128, 32 * 256], BF16))
            t_ff = BE2(nc.sbuf_tensor("t_ff", [128, 8 * 256], F32))
            t_b2 = BE2(nc.sbuf_tensor("t_b2", [128, 8], F32))
            t_acr = [BE2(nc.sbuf_tensor(f"t_acr{i}", [128, 1024], F32)) for i in range(2)]
            t_op = [BE2(nc.sbuf_tensor(f"t_op{i}", [128, 1024], F32)) for i in range(2)]

            LD(LMW, t_b2[:, :], d_b2, waits=barrier2)
            w2red = {}
            for i in range(8):
                w = [(sV, w2red[i - 2])] if i >= 2 else []
                wl = LD(LWI[i % 2], t_wp2[i % 2][:, :], d_w2[i, :, :], waits=w)
                P.add("vector", (lambda i_: (lambda e: e.tensor_reduce(
                    out=t_ab2[:, i_:i_ + 1], in_=t_wp2[i_ % 2][:, :], axis=AX.X,
                    op=OP.add, apply_absolute_value=True)))(i),
                    waits=[(LWI[i % 2], wl)] + (barrier2 if i == 0 else []), inc=sV)
                w2red[i] = sV.n
            P.add("vector", lambda e: e.drain(), inc=sV)
            P.add("vector", lambda e: e.tensor_reduce(
                out=t_ab2[:, 8:9], in_=t_ab2[:, 0:8], axis=AX.X, op=OP.add), inc=sV)
            vv = sV.n
            P.add("tensor", lambda e: e.matmul(
                pm2[0][0:1, 0:1], t_ones[:, 0:1], t_ab2[:, 8:9], start=True, stop=True),
                waits=[(sV, vv)] + barrier2, inc=sT)
            wsT = sT.n
            P.add("vector", lambda e: e.tensor_scalar(
                out=t_sc2[0:1, 0:1], in0=pm2[0][0:1, 0:1], scalar1=float(2.0 ** -22),
                scalar2=EPS, op0=OP.mult, op1=OP.max), waits=[(sT, wsT)], inc=sV)
            P.add("vector", lambda e: e.tensor_scalar(
                out=t_sc2[0:1, 1:2], in0=pm2[0][0:1, 0:1], scalar1=float(2.0 ** -23),
                scalar2=EPS * 0.5, op0=OP.mult, op1=OP.max), inc=sV)
            P.add("vector", lambda e: e.tensor_scalar(
                out=t_sc2[0:1, 2:3], in0=pm2[0][0:1, 0:1], scalar1=float(2.0 ** -22 / 127.0),
                scalar2=EPS / 127.0, op0=OP.mult, op1=OP.max), inc=sV)
            P.add("vector", lambda e: e.drain(), inc=sV)
            scv = sV.n
            th2_st = ST(STH, d_scd[0:1, 0:3], t_sc2[0:1, 0:3],
                        waits=[(sV, scv), (LTH, th_ld)])
            th2_ld = LD(LTH, t_th2[:, 0:2], d_scd[0:1, 1:3].broadcast_to((128, 2)),
                        waits=[(STH, th2_st)])
            thg = th2_ld
            w2q_v = {}
            sgn_f = 0
            for i in range(8):
                w = [(sV, w2q_v[i - 2])] if i >= 2 else [(sV, w2red[7])]
                rl = LD(LWI[i % 2], t_wp2[i % 2][:, :], d_w2[i, :, :], waits=w)
                P.add("scalar", (lambda i_: (lambda e: e.activation(
                    out=t_sgn2[:, :], in_=t_wp2[i_ % 2][:, :], func=AF.Sign)))(i),
                    waits=[(LWI[i % 2], rl), (sV, sgn_f)] + (barrier2 if i == 0 else []), inc=sS)
                P.add("scalar", (lambda i_: (lambda e: e.activation(
                    out=t_abs2[:, :], in_=t_wp2[i_ % 2][:, :], func=AF.Abs)))(i), inc=sS)
                ssn = sS.n
                P.add("vector", lambda e: e.tensor_scalar(
                    out=t_abs2[:, :], in0=t_abs2[:, :], scalar1=t_th2[:, 0:1],
                    scalar2=None, op0=OP.is_gt), waits=[(sS, ssn), (LTH, thg)], inc=sV)
                P.add("vector", (lambda i_: (lambda e: e.tensor_tensor(
                    out=t_w2q[i_][:, :], in0=t_abs2[:, :], in1=t_sgn2[:, :],
                    op=OP.mult)))(i), inc=sV)
                w2q_v[i] = sV.n
                sgn_f = sV.n
            w2q_done = sV.n

            prev_q2 = 0
            prev_sgn3 = 0
            c2_ldf = 0
            s2_ldf = 0
            s2B_used = 0
            prev_out_v = {0: 0, 1: 0}
            prev_out_st = {0: 0, 1: 0}
            prev_mm2 = 0
            prev_ff = {6: 0, 7: 0}
            for c in range(NBC):
                for hs in range(32):
                    w = []
                    if hs == 0:
                        w = [(sV, w2q_done if c == 0 else prev_q2)]
                    LD(LGL, t_gel[:, hs * 256:(hs + 1) * 256], d_gel[c, hs, :, :],
                       waits=w)
                gl_full = LGL.n
                mxv = {}
                for tg in range(2):
                    for grp in range(8):
                        bk = pt2[grp % 2]
                        w = [(LGL, gl_full)]
                        if (tg, grp) >= (0, 2):
                            pr = (tg, grp - 2) if grp >= 2 else (tg - 1, 6 + grp)
                            if pr in mxv:
                                w.append((sV, mxv[pr]))
                        if c > 0 and tg == 0 and grp < 2:
                            w.append((sV, prev_mxload))
                        last_t = None
                        for j in range(4):
                            hs = grp * 4 + j
                            last_t = P.add("tensor", (lambda bk_, hs_, j_, tg_: (lambda e:
                                e.transpose(bk_[:, j_ * 128:(j_ + 1) * 128],
                                            t_gel[:, hs_ * 256 + tg_ * 128: hs_ * 256 + (tg_ + 1) * 128],
                                            t_id[:, :])))(bk, hs, j, tg),
                                waits=w if j == 0 else [], inc=sT)
                        P.add("vector", (lambda bk_, tg_, grp_: (lambda e: e.tensor_reduce(
                            out=t_mx[:, tg_ * 8 + grp_: tg_ * 8 + grp_ + 1], in_=bk_[:, :],
                            axis=AX.X, op=OP.max, apply_absolute_value=True)))(bk, tg, grp),
                            waits=[(sT, last_t)], inc=sV)
                        mxv[(tg, grp)] = sV.n
                prev_mxload = sV.n
                P.add("vector", lambda e: e.drain(), inc=sV)
                for tg in range(2):
                    P.add("vector", (lambda tg_: (lambda e: e.tensor_reduce(
                        out=t_mx[:, 16 + tg_:17 + tg_], in_=t_mx[:, tg_ * 8:(tg_ + 1) * 8],
                        axis=AX.X, op=OP.max)))(tg), inc=sV)
                P.add("vector", lambda e: e.drain(), inc=sV)
                for tg in range(2):
                    P.add("vector", (lambda tg_: (lambda e: e.tensor_scalar(
                        out=t_mx[:, 16 + tg_:17 + tg_], in0=t_mx[:, 16 + tg_:17 + tg_],
                        scalar1=EPS, scalar2=None, op0=OP.max)))(tg), inc=sV)
                P.add("vector", lambda e: e.drain(), inc=sV)
                for tg in range(2):
                    P.add("vector", (lambda tg_: (lambda e: e.reciprocal(
                        out=t_mx[:, 18 + tg_:19 + tg_], in_=t_mx[:, 16 + tg_:17 + tg_])))(tg), inc=sV)
                P.add("vector", lambda e: e.drain(), inc=sV)
                P.add("vector", lambda e: e.tensor_scalar(
                    out=t_mx[:, 18:20], in0=t_mx[:, 18:20], scalar1=127.0, scalar2=None,
                    op0=OP.mult), inc=sV)
                P.add("vector", lambda e: e.tensor_scalar(
                    out=t_mx[:, 16:18], in0=t_mx[:, 16:18], scalar1=t_th2[:, 1:2],
                    scalar2=None, op0=OP.mult), inc=sV)
                P.add("vector", lambda e: e.drain(), inc=sV)
                c2v = sV.n
                P.add("tensor", lambda e: e.transpose(
                    pt2[0][0:2, 0:128], t_mx[:, 18:20], t_id[:, :]),
                    waits=[(sV, c2v)], inc=sT)
                P.add("tensor", lambda e: e.transpose(
                    pt2[0][0:2, 128:256], t_mx[:, 16:18], t_id[:, :]), inc=sT)
                c2t = sT.n
                P.add("vector", lambda e: e.tensor_copy(t_c2r[:, :], pt2[0][0:2, 0:128]),
                      waits=[(sT, c2t)], inc=sV)
                c2e = sV.n
                c2_st = ST(SC2, d_rw2, t_c2r[:, :], waits=[(sV, c2e), (LC2, c2_ldf)])
                LD(LC2, t_c2B[:, 0:128], d_rw2[0:1, :].broadcast_to((128, 128)),
                   waits=[(SC2, c2_st), (sV, prev_q2), (sT, prev_mm2)])
                c2_ldf = LD(LC2, t_c2B[:, 128:256], d_rw2[1:2, :].broadcast_to((128, 128)))
                P.add("vector", lambda e: e.tensor_copy(t_c2r[:, :], pt2[0][0:2, 128:256]),
                      waits=[(SC2, c2_st)], inc=sV)
                s2e = sV.n
                s2_st = ST(SS2, d_rw3, t_c2r[:, :], waits=[(sV, s2e), (LS2, s2_ldf)])
                LD(LS2, t_s2B[:, 0:128], d_rw3[0:1, :].broadcast_to((128, 128)),
                   waits=[(SS2, s2_st), (sV, s2B_used)])
                s2_ldf = LD(LS2, t_s2B[:, 128:256], d_rw3[1:2, :].broadcast_to((128, 128)))
                c2g = s2_ldf
                for hs in range(32):
                    w = [(LC2, c2_ldf), (LS2, s2_ldf)]
                    if hs == 0:
                        w.append((sV, prev_q2))
                        w.append((sT, prev_mm2))
                    P.add("vector", (lambda hs_: (lambda e: e.tensor_tensor(
                        out=t_qf2[:, :], in0=t_gel[:, hs_ * 256:(hs_ + 1) * 256],
                        in1=t_c2B[:, :], op=OP.mult)))(hs), waits=w, inc=sV)
                    P.add("vector", lambda e: e.tensor_copy(t_qi2[:, :], t_qf2[:, :]), inc=sV)
                    P.add("vector", lambda e: e.tensor_tensor(
                        out=t_sgn3[:, :], in0=t_qf2[:, :], in1=t_qi2[:, :],
                        op=OP.subtract), inc=sV)
                    P.add("vector", lambda e: e.scalar_tensor_tensor(
                        out=t_qf2[:, :], in0=t_sgn3[:, :], scalar=0.5, in1=t_qi2[:, :],
                        op0=OP.is_gt, op1=OP.add), inc=sV)
                    P.add("vector", lambda e: e.tensor_scalar(
                        out=t_sgn3[:, :], in0=t_sgn3[:, :], scalar1=-0.5, scalar2=None,
                        op0=OP.is_lt), inc=sV)
                    P.add("vector", lambda e: e.tensor_tensor(
                        out=t_qf2[:, :], in0=t_qf2[:, :], in1=t_sgn3[:, :],
                        op=OP.subtract), inc=sV)
                    P.add("vector", (lambda hs_: (lambda e: e.tensor_scalar(
                        out=t_xq2[:, hs_ * 256:(hs_ + 1) * 256], in0=t_qf2[:, :],
                        scalar1=127.0, scalar2=-128.0, op0=OP.min, op1=OP.max)))(hs), inc=sV)
                q2_done = sV.n
                prev_q2 = q2_done
                prev_sgn3 = sS.n
                ff_v = {}
                for m in range(8):
                    pg = m % 2
                    w = [(sV, q2_done)]
                    if m >= 2:
                        w.append((sV, ff_v[m - 2]))
                    else:
                        w.append((sV, prev_ff[6 + m]))
                    for k in range(32):
                        P.add("tensor", (lambda pg_, k_, m_: (lambda e: e.matmul(
                            pm2[pg_][:, 0:256], t_w2q[m_][:, k_ * 128:(k_ + 1) * 128],
                            t_xq2[:, k_ * 256:(k_ + 1) * 256],
                            start=(k_ == 0), stop=(k_ == 31))))(pg, k, m),
                            waits=w if k == 0 else [], inc=sT)
                    mmt = sT.n
                    P.add("vector", (lambda pg_, m_: (lambda e: e.tensor_tensor(
                        out=t_ff[:, m_ * 256:(m_ + 1) * 256], in0=pm2[pg_][:, 0:256],
                        in1=t_s2B[:, :], op=OP.mult)))(pg, m), waits=[(sT, mmt)], inc=sV)
                    s2B_used = sV.n
                    P.add("vector", (lambda m_: (lambda e: e.tensor_scalar(
                        out=t_ff[:, m_ * 256:(m_ + 1) * 256],
                        in0=t_ff[:, m_ * 256:(m_ + 1) * 256],
                        scalar1=t_b2[:, m_:m_ + 1], scalar2=None, op0=OP.add)))(m), inc=sV)
                    ff_v[m] = sV.n
                prev_mm2 = sT.n
                prev_ff = {6: ff_v[6], 7: ff_v[7]}
                ffn_done = sV.n
                for tg in range(2):
                    pg = tg % 2
                    ral = LD(LAC[pg], t_acr[pg][:, :],
                             d_acc[c * 256 + tg * 128: c * 256 + (tg + 1) * 128, :],
                             waits=[(sV, prev_out_v[tg]), (SOU[pg], prev_out_st[tg])])
                    bk = po2[pg]
                    last_t = None
                    for dsl in range(8):
                        w = []
                        if dsl == 0:
                            w = [(sV, ffn_done), (sV, prev_out_v[tg])]
                        last_t = P.add("tensor", (lambda bk_, d_, tg_: (lambda e: e.transpose(
                            bk_[:, d_ * 128:(d_ + 1) * 128],
                            t_ff[:, d_ * 256 + tg_ * 128: d_ * 256 + (tg_ + 1) * 128],
                            t_id[:, :])))(bk, dsl, tg), waits=w, inc=sT)
                    ow = [(sT, last_t), (LAC[pg], ral), (SOU[pg], prev_out_st[tg])]
                    P.add("vector", (lambda pg_, bk_: (lambda e: e.tensor_tensor(
                        out=t_op[pg_][:, :], in0=bk_[:, :], in1=t_acr[pg_][:, :],
                        op=OP.add)))(pg, bk), waits=ow, inc=sV)
                    prev_out_v[tg] = sV.n
                    prev_out_st[tg] = ST(
                        SOU[pg],
                        d_out[c * 256 + tg * 128: c * 256 + (tg + 1) * 128, :],
                        t_op[pg][:, :], waits=[(sV, prev_out_v[tg])])

        P.add("scalar", lambda e: e.nop(),
              waits=[(s, s.n) for s in ALL_DMA_SEMS if s.n > 0])

        block = OE(nc.Block())
        P.emit(block)

    return nc


def _host_layout(inputs):
    f = np.ascontiguousarray
    ip = inputs
    win = np.stack([ip["L0_attn_in_proj_w"][l].reshape(32, 128, 8, 128)
                    .transpose(0, 3, 2, 1).reshape(32, 128, 1024) for l in range(NLAY)])
    wout = np.stack([ip["L0_attn_out_w"][l].reshape(8, 128, 16, 128)
                     .transpose(2, 3, 0, 1).reshape(16, 128, 1024) for l in range(NLAY)])
    wdt = np.stack([ip["L0_attn_dt_w"][l].reshape(16, 128, 64)
                    .transpose(2, 0, 1).reshape(64, 2048) for l in range(NLAY)])
    wxp = np.stack([ip["L0_attn_x_proj_w"][l].T.reshape(16, 128, 96) for l in range(NLAY)])
    alog = np.stack([ip["L0_attn_A_log"][l].reshape(16, 128, 16)
                     .transpose(1, 0, 2).reshape(128, 256) for l in range(NLAY)])
    cw = np.stack([ip["L0_attn_conv_w"][l].reshape(16, 128, 4)
                   .transpose(1, 0, 2).reshape(128, 64) for l in range(NLAY)])
    cb = np.stack([ip["L0_attn_conv_b"][l].reshape(16, 128).T for l in range(NLAY)])
    dtb = np.stack([ip["L0_attn_dt_b"][l].reshape(16, 128).T for l in range(NLAY)])
    dp = np.stack([ip["L0_attn_Dp"][l].reshape(16, 128).T for l in range(NLAY)])
    nw = np.stack([ip["L0_attn_norm_w"][l].reshape(8, 128).T for l in range(NLAY)])
    w1 = ip["L0_ffn_W1"].reshape(32, 128, 8, 128).transpose(0, 3, 2, 1).reshape(32, 128, 1024)
    b1 = ip["L0_ffn_b1"].reshape(32, 128).T
    w2 = ip["L0_ffn_W2"].reshape(8, 128, 32, 128).transpose(0, 3, 2, 1).reshape(8, 128, 4096)
    b2 = ip["L0_ffn_b2"].reshape(8, 128).T
    com = {
        "w_in": f(win), "w_out": f(wout), "w_dt": f(wdt), "w_xp": f(wxp),
        "alog": f(alog), "cw": f(cw), "cb": f(cb), "dtb": f(dtb), "dp": f(dp),
        "nw": f(nw), "w1": f(w1), "b1": f(b1), "w2": f(w2), "b2": f(b2),
        "cst": np.tile(np.array([[1e-5, 1.0, 0.5, 0.0]], dtype=np.float32), (128, 1)),
        "ident": np.eye(128, dtype=np.float32),
        "identb": np.eye(128, dtype=ml_dtypes.bfloat16),
    }
    maps = []
    for b in range(N_CORES):
        m = dict(com)
        m["x"] = f(np.asarray(ip["L0_attn_x"][b]))
        maps.append(m)
    return maps


def kernel(**inputs):
    in_maps = _host_layout(inputs)
    nc = build_nc()
    res = run_bass_kernel_spmd(nc, in_maps, core_ids=list(range(N_CORES)))
    return np.stack([np.asarray(r["out"], dtype=np.float32) for r in res.results])
